# version 06
# baseline (speedup 1.0000x reference)
import math
import numpy as np
import concourse.bass as bass
import concourse.mybir as mybir
from concourse.bass_utils import run_bass_kernel_spmd
from concourse.bass import IndirectOffsetOnAxis
from contextlib import ExitStack

F32 = mybir.dt.float32
BF16 = mybir.dt.bfloat16
I32 = mybir.dt.int32
AF = mybir.ActivationFunctionType
ALU = mybir.AluOpType
AX = mybir.AxisListType

D = 1024
S = 16
E = 16
FEXP = 2048
FSH = 2048
LN20 = math.log(20.0)
SQ20 = math.sqrt(20.0)
CAVG = 0.01 / 1024.0
LNEPS = 1e-5
DUMP = 600.0
FCONST_VALS = [1e-8, 1e-20, LN20, LNEPS, 1e-30, 1.0, 0.0, 2.0]


def host_consts():
    sp = np.arange(128) // 8
    c8p = np.arange(128) % 8
    c = {}
    m = np.zeros((128, 16), np.float32)
    m[np.arange(128), sp] = 1.0
    c["smask16"] = m
    c["repmask16"] = np.ascontiguousarray(m.T)
    m8 = np.zeros((8, 128), np.float32)
    m8[c8p, np.arange(128)] = 1.0
    c["mask8"] = m8
    mc = np.zeros((128, 8), np.float32)
    mc[np.arange(128), c8p] = 1.0
    c["c8maskT"] = mc
    for j, (nm, sg) in enumerate([("bre", 1.0), ("bim", 1.0), ("cre", 1.0), ("cim", -1.0)]):
        mm = np.zeros((64, 128), np.float32)
        mm[j * 16 + sp, np.arange(128)] = sg
        c["bcm_" + nm] = mm
    ph = np.linspace(0.0, 2 * math.pi * (1 - 1.0 / S), S)
    c["d0"] = (0.01 * np.stack([np.cos(ph), np.sin(ph)], -1)).astype(np.float32)
    ep = np.zeros((16, 2), np.float32)
    ep[:, 0] = 1e-10
    c["epsC"] = ep
    c["ident"] = np.eye(128, dtype=np.float32)
    c["iota512"] = np.ascontiguousarray(
        np.broadcast_to(np.arange(512, dtype=np.float32), (128, 512)))
    c["iotacol"] = np.arange(128, dtype=np.float32).reshape(128, 1)
    c["tri128"] = np.ascontiguousarray(np.triu(np.ones((128, 128), np.float32)))
    c["onesrow"] = np.ones((1, 128), np.float32)
    ee = np.arange(128) // 8
    tc = np.arange(128) % 8
    trig = ((ee[:, None] == ee[None, :]) & (tc[:, None] < tc[None, :])).astype(np.float32)
    c["trig8"] = np.ascontiguousarray(trig)
    fv = np.zeros((128, 8), np.float32)
    for i, v in enumerate(FCONST_VALS):
        fv[:, i] = v
    c["fvals"] = fv
    c["clim"] = np.ascontiguousarray(
        np.broadcast_to(np.array([SQ20, 20.0], np.float32), (16, 2)))
    return c


CONST_SHAPES = {"smask16": [128, 16], "repmask16": [16, 128], "mask8": [8, 128],
                "c8maskT": [128, 8], "bcm_bre": [64, 128], "bcm_bim": [64, 128],
                "bcm_cre": [64, 128], "bcm_cim": [64, 128], "d0": [16, 2],
                "epsC": [16, 2], "ident": [128, 128], "iota512": [128, 512],
                "iotacol": [128, 1], "tri128": [128, 128], "onesrow": [1, 128],
                "clim": [16, 2], "trig8": [128, 128], "fvals": [128, 8]}

WEIGHT_SHAPES = {"w_in_t": [D, 2 * D], "w_xp_t": [D, 128], "w_dt_fm": [65, D],
                 "w_res_t": [D, 64], "w_out_aug": [D + 1, D], "gate_t": [D, E],
                 "w_sg_t": [D, FSH], "w_su_t": [D, FSH], "w_sd_t": [FSH, D],
                 "wg_t": [E, D, FEXP], "wu_t": [E, D, FEXP], "wd_t": [E, FEXP, D],
                 "a_log": [1, 16]}


def stage_weights(inp):
    w = {}
    w["w_in_t"] = np.ascontiguousarray(np.asarray(inp["L0_attn_W_in"]).T)
    Wx = np.asarray(inp["L0_attn_W_xproj"])
    perm = list(range(64)) + [64 + 2 * s for s in range(16)] + \
        [64 + 2 * s + 1 for s in range(16)] + [96 + 2 * s for s in range(16)] + \
        [96 + 2 * s + 1 for s in range(16)]
    w["w_xp_t"] = np.ascontiguousarray(Wx[perm].T)
    Wdt = np.asarray(inp["L0_attn_W_dt"])
    bdt = np.asarray(inp["L0_attn_b_dt"])
    w["w_dt_fm"] = np.ascontiguousarray(np.vstack([Wdt.T, bdt[None, :]]))
    Wres = np.asarray(inp["L0_attn_W_res"])
    wr = np.zeros((1024, 64), np.float32)
    wr[:, 0:16] = Wres[0::2].T
    wr[:, 32:48] = Wres[1::2].T
    w["w_res_t"] = wr
    w["w_out_aug"] = np.ascontiguousarray(
        np.vstack([np.asarray(inp["L0_attn_W_out"]).T,
                   np.asarray(inp["L0_attn_b_out"])[None, :]]))
    w["gate_t"] = np.ascontiguousarray(np.asarray(inp["L0_ffn_gate_w"]).T)
    w["w_sg_t"] = np.ascontiguousarray(np.asarray(inp["L0_ffn_Ws_g"]).T)
    w["w_su_t"] = np.ascontiguousarray(np.asarray(inp["L0_ffn_Ws_u"]).T)
    w["w_sd_t"] = np.ascontiguousarray(np.asarray(inp["L0_ffn_Ws_d"]).T)
    w["wg_t"] = np.ascontiguousarray(np.asarray(inp["L0_ffn_Wg"]).transpose(0, 2, 1))
    w["wu_t"] = np.ascontiguousarray(np.asarray(inp["L0_ffn_Wu"]).transpose(0, 2, 1))
    w["wd_t"] = np.ascontiguousarray(np.asarray(inp["L0_ffn_Wd"]).transpose(0, 2, 1))
    w["a_log"] = np.ascontiguousarray(np.asarray(inp["L0_attn_A_log"]).reshape(1, 16))
    return w


def build(T, dbg=False):
    TILE = 512
    NT = T // TILE
    TG = TILE // 128
    SUB = 32
    NSUB = T // SUB
    YC = 64
    CAP = T // 8
    NBLK = max(CAP // 128, 1)

    nc = bass.Bass("TRN2", target_bir_lowering=False, debug=False,
                   detect_race_conditions=False)
    es = ExitStack()

    x_tm = nc.dram_tensor("x_tm", [T, D], F32, kind="ExternalInput").ap()
    wdr = {k: nc.dram_tensor(k, v, F32, kind="ExternalInput").ap()
           for k, v in WEIGHT_SHAPES.items()}
    cdr = {k: nc.dram_tensor("c_" + k, v, F32, kind="ExternalInput").ap()
           for k, v in CONST_SHAPES.items()}
    out = nc.dram_tensor("out", [T, D], F32, kind="ExternalOutput").ap()

    dt_dram = nc.dram_tensor("dt_dram", [D, T], F32, kind="Internal").ap()
    dtx_dram = nc.dram_tensor("dtx_dram", [D, T], F32, kind="Internal").ap()
    zsil_dram = nc.dram_tensor("zsil_dram", [D, T], F32, kind="Internal").ap()
    bc_dram = nc.dram_tensor("bc_dram", [512, T], F32, kind="Internal").ap()
    xvn_dram = nc.dram_tensor("xvn_dram", [16, 2 * T], F32, kind="Internal").ap()
    y_dram = nc.dram_tensor("y_dram", [128, 8 * T], F32, kind="Internal").ap()
    x1_dram = nc.dram_tensor("x1_dram", [T, D], F32, kind="Internal").ap()
    x2n_dram = nc.dram_tensor("x2n_dram", [T, D], BF16, kind="Internal").ap()
    scem_dram = nc.dram_tensor("scem_dram", [E, T], F32, kind="Internal").ap()
    rkm_dram = nc.dram_tensor("rkm_dram", [E * 8, T // 8], F32, kind="Internal").ap()
    a_dram = nc.dram_tensor("a_dram", [16, 1], F32, kind="Internal").ap()
    dbg_mirror = {}
    if dbg:
        for nm, src in [("dt_dram", dt_dram), ("dtx_dram", dtx_dram),
                        ("zsil_dram", zsil_dram), ("bc_dram", bc_dram),
                        ("xvn_dram", xvn_dram), ("y_dram", y_dram),
                        ("x1_dram", x1_dram), ("scem_dram", scem_dram),
                        ("a_dram", a_dram)]:
            dbg_mirror[nm] = (src, nc.dram_tensor(
                "dbg_" + nm, list(src.tensor.shape), F32, kind="ExternalOutput").ap())
        dbg_mirror["x2n_dram"] = (x2n_dram, nc.dram_tensor(
            "dbg_x2n_dram", [T, D], BF16, kind="ExternalOutput").ap())
    if dbg:
        xn_dbg = nc.dram_tensor("dbg_xn", [T, D], BF16, kind="ExternalOutput").ap()
        xnfm_dbg = nc.dram_tensor("dbg_xnfm", [128, (T // 512) * 4096], BF16,
                                  kind="ExternalOutput").ap()
        stat_dbg = nc.dram_tensor("dbg_stat", [128, 12], F32, kind="ExternalOutput").ap()

    sems = {}
    cnt = {}
    for nm in ["s_ld", "s_gld", "s_v", "s_a", "s_g", "s_p", "s_pb", "s_al",
               "s_ye", "s_out", "s_sc"]:
        sems[nm] = es.enter_context(nc.semaphore(nm))
        cnt[nm] = 0

    nc._dbg_sems = {k: v for k, v in sems.items()}

    def inc(inst, nm, n=1):
        inst.then_inc(sems[nm], n)
        cnt[nm] += n
        return cnt[nm]

    ev = {}

    def mark(name, nm):
        ev[name] = (nm, cnt[nm])

    def wt(eng, name):
        nm, c = ev[name]
        if c > 0:
            eng.wait_ge(sems[nm], c)

    def sb(name, shape, dt=F32):
        return es.enter_context(nc.sbuf_tensor(name, shape, dt))

    cs = {k: sb("cs_" + k, v) for k, v in CONST_SHAPES.items()}
    for i, v in enumerate(FCONST_VALS):
        nc.const_aps.register(F32, v, cs["fvals"][:, i:i + 1])
    ident_bf = sb("identbf", [128, 128], BF16)
    acol = sb("acol", [128, 1])
    arow = sb("arow", [1, 16])
    w_xp_sb = sb("w_xp_sb", [128, 8 * 128], BF16)
    w_dt_sb = sb("w_dt_sb", [65, D], BF16)
    w_res_sb = sb("w_res_sb", [128, 8 * 64], BF16)
    gate_sb = sb("gate_sb", [128, 8 * E], BF16)
    w_out_sb = sb("w_out_sb", [128, 8 * 1024], BF16)
    b_out_sb = sb("b_out_sb", [1, 1024], BF16)
    sco_tm = sb("sco_tm", [128, (T // 128) * E])
    slots = sb("slots", [128, 128])
    idint = sb("idint", [128, 64], I32)
    ones_bf = sb("ones_bf", [1, 128], BF16)

    blk = es.enter_context(nc.Block())

    @blk.sync
    def _(sync):
        for k in CONST_SHAPES:
            inc(sync.dma_start(out=cs[k][:], in_=cdr[k]), "s_ld", 16)
        inc(sync.dma_start(out=arow[:], in_=wdr["a_log"]), "s_ld", 16)
        mark("consts", "s_ld")

    @blk.gpsimd
    def _(gpsimd):
        wt(gpsimd, "consts")
        inc(gpsimd.dma_start(out=ident_bf[:], in_=cdr["ident"]), "s_gld", 16)
        inc(gpsimd.dma_start(out=w_dt_sb[:], in_=wdr["w_dt_fm"]), "s_gld", 16)
        inc(gpsimd.dma_start(out=b_out_sb[:], in_=wdr["w_out_aug"][D:D + 1, :]), "s_gld", 16)
        for k in range(8):
            kk = slice(k * 128, (k + 1) * 128)
            inc(gpsimd.dma_start(out=w_xp_sb[:, k * 128:(k + 1) * 128],
                                 in_=wdr["w_xp_t"][kk, :]), "s_gld", 16)
            inc(gpsimd.dma_start(out=w_res_sb[:, k * 64:(k + 1) * 64],
                                 in_=wdr["w_res_t"][kk, :]), "s_gld", 16)
            inc(gpsimd.dma_start(out=gate_sb[:, k * E:(k + 1) * E],
                                 in_=wdr["gate_t"][kk, :]), "s_gld", 16)
            inc(gpsimd.dma_start(out=w_out_sb[:, k * 1024:(k + 1) * 1024],
                                 in_=wdr["w_out_aug"][kk, :]), "s_gld", 16)
        mark("resw", "s_gld")

    @blk.scalar
    def _(scalar):
        wt(scalar, "consts")
        scalar.activation(arow[:], arow[:], AF.Exp)
        inc(scalar.mul(arow[:], arow[:], -1.0), "s_a")
        mark("arow", "s_a")

    @blk.sync
    def _(sync):
        wt(sync, "arow")
        inc(sync.dma_start(out=a_dram.rearrange("s one -> one s"), in_=arow[:]), "s_ld", 16)
        sync.wait_ge(sems["s_ld"], cnt["s_ld"])
        for s in range(16):
            inc(sync.dma_start(out=acol[8 * s:8 * (s + 1), 0:1],
                               in_=a_dram[s:s + 1, 0:1].to_broadcast([8, 1])), "s_ld", 16)
        mark("acol", "s_ld")

    pa = ExitStack()

    def asb(name, shape, dt=F32):
        return pa.enter_context(nc.sbuf_tensor(name, shape, dt))

    def aps(name, shape, dt=F32):
        return pa.enter_context(nc.psum_tensor(name, shape, dt))

    w_in_sb = asb("w_in_sb", [128, 8 * 2048], BF16)
    xt = asb("xt", [128, TG * D])
    stat = asb("stat", [128, 3 * TG])
    sqs = asb("sqs", [128, D])
    xn_tm = asb("xn_tm", [128, TG * D], BF16)
    xn_fm = asb("xn_fm", [128, 8 * TILE], BF16)
    xin_fm32 = asb("xin_fm32", [128, 8 * TILE])
    xin_fm = asb("xin_fm", [128, 8 * TILE], BF16)
    zs_fm = asb("zs_fm", [128, 8 * TILE])
    dtr_aug = asb("dtr_aug", [128, TILE], BF16)
    bcsb = asb("bcsb", [64, TILE])
    bccol_s = asb("bccol_s", [128, 4 * TILE])
    dt_f = asb("dt_f", [128, 8 * TILE])
    spa = asb("spa", [128, 2 * TILE])
    spb = asb("spb", [128, 2 * TILE])
    dtx_f = asb("dtx_f", [128, 8 * TILE])
    xvt = asb("xvt", [16, 2 * TILE])
    xvsc = asb("xvsc", [16, 2 * TILE])
    ps_a = aps("ps_a", [128, 512])
    ps_b = aps("ps_b", [128, 512])
    ps_c = aps("ps_c", [128, 512])
    ps_xc = aps("ps_xc", [64, 512])

    @blk.gpsimd
    def _(gpsimd):
        wt(gpsimd, "resw")
        for k in range(8):
            kk = slice(k * 128, (k + 1) * 128)
            inc(gpsimd.dma_start(out=w_in_sb[:, k * 2048:(k + 1) * 2048],
                                 in_=wdr["w_in_t"][kk, :]), "s_gld", 16)
        mark("winld", "s_gld")

    def phase_a(tile):
        t0 = tile * TILE
        tg = "_%d" % tile

        @blk.sync
        def _(sync):
            if tile == 0:
                wt(sync, "acol")
            else:
                wt(sync, "pa" + "_%d" % (tile - 1))
            for g in range(TG):
                inc(sync.dma_start(out=xt[:, g * D:(g + 1) * D],
                                   in_=x_tm[t0 + g * 128:t0 + (g + 1) * 128, :]), "s_ld", 16)
            mark("xld" + tg, "s_ld")

        @blk.vector
        def _(vector):
            wt(vector, "xld" + tg)
            for g in range(TG):
                vector.tensor_scalar(sqs[:, 0:D], xt[:, g * D:(g + 1) * D], 1.0, 0.0,
                                     ALU.mult, ALU.add, accum_out=stat[:, 3 * g:3 * g + 1])
            inc(vector.tensor_copy(sqs[:, 0:1], stat[:, 0:1]), "s_v")
            mark("asx" + tg, "s_v")

        @blk.scalar
        def _(scalar):
            wt(scalar, "xld" + tg)
            wt(scalar, "asx" + tg)
            for g in range(TG):
                scalar.activation(sqs[:, 0:D], xt[:, g * D:(g + 1) * D], AF.Square,
                                  accum_out=stat[:, 3 * g + 1:3 * g + 2])
            inc(scalar.copy(sqs[:, 0:1], stat[:, 0:1]), "s_a")
            mark("asxx" + tg, "s_a")

        @blk.vector
        def _(vector):
            wt(vector, "asxx" + tg)
            for g in range(TG):
                mcol = stat[:, 3 * g:3 * g + 1]
                vcol = stat[:, 3 * g + 1:3 * g + 2]
                m2col = stat[:, 3 * g + 2:3 * g + 3]
                inc(vector.tensor_scalar(mcol, mcol, 1.0 / D, None, ALU.mult), "s_v")
                vector.wait_ge(sems["s_v"], cnt["s_v"])
                inc(vector.tensor_tensor(m2col, mcol, mcol, ALU.mult), "s_v")
                vector.wait_ge(sems["s_v"], cnt["s_v"])
                vector.scalar_tensor_tensor(vcol, vcol, 1.0 / D, m2col, ALU.mult,
                                            ALU.subtract)
            inc(vector.tensor_copy(sqs[:, 0:1], stat[:, 0:1]), "s_v")
            mark("avar" + tg, "s_v")

        @blk.scalar
        def _(scalar):
            wt(scalar, "avar" + tg)
            for g in range(TG):
                vcol = stat[:, 3 * g + 1:3 * g + 2]
                scalar.activation(vcol, vcol, AF.Ln, bias=LNEPS)
                scalar.activation(vcol, vcol, AF.Exp, scale=-0.5)
            inc(scalar.copy(sqs[:, 0:1], stat[:, 0:1]), "s_a")
            mark("arstd" + tg, "s_a")

        @blk.vector
        def _(vector):
            wt(vector, "arstd" + tg)
            for g in range(TG):
                vector.tensor_scalar(xn_tm[:, g * D:(g + 1) * D], xt[:, g * D:(g + 1) * D],
                                     stat[:, 3 * g:3 * g + 1], stat[:, 3 * g + 1:3 * g + 2],
                                     ALU.subtract, ALU.mult)
            inc(vector.tensor_copy(sqs[:, 0:1], stat[:, 0:1]), "s_v")
            mark("axn" + tg, "s_v")

        if dbg:
            @blk.sync
            def _(sync):
                wt(sync, "axn" + tg)
                inc(sync.dma_start(
                    out=xn_dbg[t0:t0 + TILE, :].rearrange("(g p) d -> p g d", p=128),
                    in_=xn_tm[:].rearrange("p (g d) -> p g d", d=D)), "s_ld", 16)
                if tile == 0:
                    inc(sync.dma_start(out=stat_dbg, in_=stat[:]), "s_ld", 16)

        vbase = cnt["s_v"]

        @blk.tensor
        def _(tensor):
            wt(tensor, "axn" + tg)
            if tile == 0:
                wt(tensor, "resw")
            for i in range(32):
                pst = ps_a if i % 2 == 0 else ps_b
                if i >= 2:
                    tensor.wait_ge(sems["s_v"], vbase + i - 1)
                g, k = divmod(i, 8)
                inc(tensor.transpose(pst[:].bitcast(BF16)[:, 0:128],
                                     xn_tm[:, g * D + k * 128:g * D + (k + 1) * 128],
                                     ident_bf[:]), "s_p")
            mark("atr" + tg, "s_p")

        pbase = ev["atr" + tg][1] - 32

        @blk.vector
        def _(vector):
            for i in range(32):
                pst = ps_a if i % 2 == 0 else ps_b
                vector.wait_ge(sems["s_p"], pbase + i + 1)
                g, k = divmod(i, 8)
                inc(vector.tensor_copy(xn_fm[:, k * TILE + g * 128:k * TILE + (g + 1) * 128],
                                       pst[:].bitcast(BF16)[:, 0:128]), "s_v")
            mark("axnfm" + tg, "s_v")

        if dbg:
            @blk.sync
            def _(sync):
                wt(sync, "axnfm" + tg)
                inc(sync.dma_start(out=xnfm_dbg[:, tile * 4096:(tile + 1) * 4096],
                                   in_=xn_fm[:]), "s_ld", 16)

        abase = cnt["s_a"]

        @blk.tensor
        def _(tensor):
            wt(tensor, "axnfm" + tg)
            if tile == 0:
                wt(tensor, "winld")
            for o in range(16):
                pst = ps_a if o % 2 == 0 else ps_b
                if o >= 2:
                    tensor.wait_ge(sems["s_a"], abase + o - 1)
                for k in range(8):
                    inc(tensor.matmul(pst[:],
                                      w_in_sb[:, k * 2048 + o * 128:k * 2048 + (o + 1) * 128],
                                      xn_fm[:, k * TILE:(k + 1) * TILE],
                                      start=(k == 0), stop=(k == 7)), "s_p")
            mark("axz" + tg, "s_p")

        pbase2 = ev["axz" + tg][1] - 128

        @blk.scalar
        def _(scalar):
            for o in range(16):
                pst = ps_a if o % 2 == 0 else ps_b
                scalar.wait_ge(sems["s_p"], pbase2 + (o + 1) * 8)
                if o < 8:
                    inc(scalar.activation(xin_fm32[:, o * TILE:(o + 1) * TILE], pst[:],
                                          AF.Silu), "s_a")
                else:
                    inc(scalar.activation(zs_fm[:, (o - 8) * TILE:(o - 7) * TILE], pst[:],
                                          AF.Silu), "s_a")
            mark("asilu" + tg, "s_a")

        @blk.vector
        def _(vector):
            wt(vector, "asilu" + tg)
            for o in range(8):
                inc(vector.tensor_copy(xin_fm[:, o * TILE:(o + 1) * TILE],
                                       xin_fm32[:, o * TILE:(o + 1) * TILE]), "s_v")
            inc(vector.memset(dtr_aug[64:65, :], 1.0), "s_v")
            mark("axinbf" + tg, "s_v")

        @blk.sync
        def _(sync):
            wt(sync, "asilu" + tg)
            for o in range(8):
                inc(sync.dma_start(out=zsil_dram[o * 128:(o + 1) * 128, t0:t0 + TILE],
                                   in_=zs_fm[:, o * TILE:(o + 1) * TILE]), "s_ld", 16)
            mark("azsil" + tg, "s_ld")

        @blk.tensor
        def _(tensor):
            wt(tensor, "axinbf" + tg)
            for k in range(8):
                inc(tensor.matmul(ps_c[:], w_xp_sb[:, k * 128:(k + 1) * 128],
                                  xin_fm[:, k * TILE:(k + 1) * TILE],
                                  start=(k == 0), stop=(k == 7)), "s_p")
            mark("axp" + tg, "s_p")

        @blk.scalar
        def _(scalar):
            wt(scalar, "axp" + tg)
            inc(scalar.activation(dtr_aug[0:64, :].bitcast(BF16), ps_c[0:64, :],
                                  AF.Identity), "s_a")
            inc(scalar.activation(bcsb[:], ps_c[64:128, :], AF.Identity), "s_a")
            mark("adtr" + tg, "s_a")

        ABCV0 = cnt["s_v"]

        @blk.tensor
        def _(tensor):
            wt(tensor, "adtr" + tg)
            for j, nm in enumerate(["bre", "bim", "cre", "cim"]):
                if j >= 1:
                    tensor.wait_ge(sems["s_v"], ABCV0 + j)
                inc(tensor.matmul(ps_c[:], cs["bcm_" + nm][:], bcsb[:],
                                  start=True, stop=True), "s_p")
            mark("abcmm" + tg, "s_p")

        pbase3 = ev["abcmm" + tg][1] - 4

        @blk.vector
        def _(vector):
            for j in range(4):
                vector.wait_ge(sems["s_p"], pbase3 + j + 1)
                inc(vector.tensor_copy(bccol_s[:, j * TILE:(j + 1) * TILE], ps_c[:]), "s_v")
            mark("abcv" + tg, "s_v")

        @blk.sync
        def _(sync):
            wt(sync, "abcv" + tg)
            for j in range(4):
                inc(sync.dma_start(out=bc_dram[j * 128:(j + 1) * 128, t0:t0 + TILE],
                                   in_=bccol_s[:, j * TILE:(j + 1) * TILE]), "s_ld", 16)
            mark("abcst" + tg, "s_ld")

        VDT0 = cnt["s_v"]
        abase2 = cnt["s_a"]

        @blk.tensor
        def _(tensor):
            wt(tensor, "adtr" + tg)
            wt(tensor, "abcv" + tg)
            for c in range(8):
                pst = ps_a if c % 2 == 0 else ps_b
                if c >= 2:
                    tensor.wait_ge(sems["s_v"], VDT0 + 4 * (c - 2) + 2)
                inc(tensor.matmul(pst[:], w_dt_sb[:, c * 128:(c + 1) * 128],
                                  dtr_aug[0:65, :], start=True, stop=True), "s_p")
            mark("adtmm" + tg, "s_p")

        pbase4 = ev["adtmm" + tg][1] - 8

        @blk.scalar
        def _(scalar):
            for c in range(8):
                par = c % 2
                scalar.wait_ge(sems["s_v"], VDT0 + 4 * c + 2)
                if c >= 2:
                    scalar.wait_ge(sems["s_v"], VDT0 + 4 * (c - 2) + 3)
                scalar.activation(spb[:, par * TILE:(par + 1) * TILE],
                                  spa[:, par * TILE:(par + 1) * TILE], AF.Exp)
                inc(scalar.activation(spb[:, par * TILE:(par + 1) * TILE],
                                      spb[:, par * TILE:(par + 1) * TILE],
                                      AF.Ln, bias=1.0), "s_a")
            mark("adtsp" + tg, "s_a")

        @blk.vector
        def _(vector):
            for c in range(8):
                par = c % 2
                pst = ps_a if c % 2 == 0 else ps_b
                vector.wait_ge(sems["s_p"], pbase4 + c + 1)
                inc(vector.tensor_scalar(dt_f[:, c * TILE:(c + 1) * TILE], pst[:],
                                         0.0, None, ALU.max), "s_v")
                vector.wait_ge(sems["s_v"], cnt["s_v"])
                inc(vector.scalar_tensor_tensor(spa[:, par * TILE:(par + 1) * TILE],
                                                dt_f[:, c * TILE:(c + 1) * TILE],
                                                -2.0, pst[:],
                                                ALU.mult, ALU.add), "s_v")
                vector.wait_ge(sems["s_a"], abase2 + c + 1)
                dtc = dt_f[:, c * TILE:(c + 1) * TILE]
                inc(vector.tensor_tensor(dtc, dtc, spb[:, par * TILE:(par + 1) * TILE],
                                         ALU.add), "s_v")
                inc(vector.tensor_tensor(dtx_f[:, c * TILE:(c + 1) * TILE], dtc,
                                         xin_fm32[:, c * TILE:(c + 1) * TILE], ALU.mult), "s_v")
            mark("adtx" + tg, "s_v")

        @blk.sync
        def _(sync):
            wt(sync, "adtx" + tg)
            for c in range(8):
                inc(sync.dma_start(out=dt_dram[c * 128:(c + 1) * 128, t0:t0 + TILE],
                                   in_=dt_f[:, c * TILE:(c + 1) * TILE]), "s_ld", 16)
                inc(sync.dma_start(out=dtx_dram[c * 128:(c + 1) * 128, t0:t0 + TILE],
                                   in_=dtx_f[:, c * TILE:(c + 1) * TILE]), "s_ld", 16)
            mark("adtst" + tg, "s_ld")

        @blk.tensor
        def _(tensor):
            wt(tensor, "axinbf" + tg)
            wt(tensor, "adtmm" + tg)
            for k in range(8):
                inc(tensor.matmul(ps_xc[:], w_res_sb[:, k * 64:(k + 1) * 64],
                                  xin_fm[:, k * TILE:(k + 1) * TILE],
                                  start=(k == 0), stop=(k == 7)), "s_p")
            mark("axc" + tg, "s_p")

        @blk.vector
        def _(vector):
            wt(vector, "axc" + tg)
            xv_re = xvt[:].rearrange("p (t r) -> p t r", r=2)[:, :, 0:1]
            xv_im = xvt[:].rearrange("p (t r) -> p t r", r=2)[:, :, 1:2]
            inc(vector.tensor_scalar(xv_re, ps_xc[0:16, :].unsqueeze(2),
                                     1e-10, None, ALU.add), "s_v")
            inc(vector.tensor_copy(xv_im, ps_xc[32:48, :].unsqueeze(2)), "s_v")
            mark("axv1" + tg, "s_v")

        @blk.scalar
        def _(scalar):
            wt(scalar, "axv1" + tg)
            inc(scalar.activation(xvsc[:], xvt[:], AF.Square), "s_a")
            mark("axv2" + tg, "s_a")

        @blk.vector
        def _(vector):
            wt(vector, "axv2" + tg)
            inc(vector.tensor_reduce(xvsc[:, 0:TILE],
                                     xvsc[:].rearrange("p (t r) -> p t r", r=2),
                                     AX.X, ALU.add), "s_v")
            mark("axv3" + tg, "s_v")

        @blk.scalar
        def _(scalar):
            wt(scalar, "axv3" + tg)
            scalar.activation(xvsc[:, 0:TILE], xvsc[:, 0:TILE], AF.Ln, bias=1e-30)
            inc(scalar.activation(xvsc[:, 0:TILE], xvsc[:, 0:TILE], AF.Exp, scale=-0.5), "s_a")
            mark("axv4" + tg, "s_a")

        @blk.vector
        def _(vector):
            wt(vector, "axv4" + tg)
            xv_re = xvt[:].rearrange("p (t r) -> p t r", r=2)[:, :, 0:1]
            xv_im = xvt[:].rearrange("p (t r) -> p t r", r=2)[:, :, 1:2]
            rx = xvsc[:, 0:TILE].unsqueeze(2)
            inc(vector.tensor_tensor(xv_re, xv_re, rx, ALU.mult), "s_v")
            inc(vector.tensor_tensor(xv_im, xv_im, rx, ALU.mult), "s_v")
            mark("axv5" + tg, "s_v")

        @blk.sync
        def _(sync):
            wt(sync, "axv5" + tg)
            inc(sync.dma_start(out=xvn_dram[:, 2 * t0:2 * (t0 + TILE)], in_=xvt[:]), "s_ld", 16)
            mark("pa" + tg, "s_ld")

    for tile in range(NT):
        phase_a(tile)
    mark("phaseA", "s_ld")
    pa.close()

    pb_scope = ExitStack()

    def bsb(name, shape, dt=F32):
        return pb_scope.enter_context(nc.sbuf_tensor(name, shape, dt))

    def bps(name, shape, dt=F32):
        return pb_scope.enter_context(nc.psum_tensor(name, shape, dt))

    h2 = bsb("h2", [128, 512])
    alph = bsb("alph", [128, 2 * 512])
    relt = bsb("relt", [128, 256])
    sqt = bsb("sqt", [128, 256])
    Mt = bsb("Mt", [128, 128])
    ddt = bsb("ddt", [128, 128])
    lnt = bsb("lnt", [128, 128])
    ldt = bsb("ldt", [128, 128])
    t3t = bsb("t3t", [128, 128])
    rt = bsb("rt", [128, 128])
    compt = bsb("compt", [128, 256])
    clipt = bsb("clipt", [128, 256])
    davg = bsb("davg", [16, 2])
    dtiny = bsb("dtiny", [16, 32])
    surf = bsb("surf", [16, 2])
    yrhs = bsb("yrhs", [128, 16])
    bcc = bsb("bcc", [128, 2 * 4 * SUB])
    xvns = bsb("xvns", [16, 2 * 2 * SUB])
    slab_t = bsb("slab_t", [8, 2 * SUB * 128])
    slab_x = bsb("slab_x", [8, 2 * SUB * 128])
    ysb = bsb("ysb", [128, 2 * 512])

    ps_bc2 = [bps("ps_bc2_%d" % i, [128, 1024]) for i in range(2)]
    ps_y = bps("ps_y", [128, 512])
    ps_red = bps("ps_red", [16, 256])
    ps_surf = bps("ps_surf", [128, 2])

    NB = T // 4
    V0 = cnt["s_v"]
    A0 = cnt["s_a"]
    G0 = cnt["s_g"]
    P0 = cnt["s_p"]
    PB0 = cnt["s_pb"]
    AL0 = cnt["s_al"]
    GLD0 = cnt["s_sc"]
    YE0 = cnt["s_ye"]

    OV = {"yrhs": 2, "hdec": 4, "h": 8, "dd": 9, "t3": 10, "clip": 12, "red": 13,
          "ssq": 16, "dd2": 22, "u": 25, "surf": 32}
    V_PER = 32
    OA = {"sq": 1, "lm": 2, "ld": 3, "r": 4, "lnn": 5, "e1": 6, "lnd": 7, "exps": 8}
    A_PER = 8
    OG = {"rel": 1, "M": 2, "cim": 3}
    G_PER = 3
    OP = {"surfb": 1, "y": 3, "smask": 4}
    P_PER = 4
    GLD_PER = 64

    def vv(t, name):
        return V0 + V_PER * t + OV[name]

    def aa(t, name):
        return A0 + A_PER * t + OA[name]

    def gg(t, name):
        return G0 + G_PER * t + OG[name]

    def pp(t, name):
        return P0 + P_PER * t + OP[name]

    def pbn(batch):
        return PB0 + 2 * (batch + 1)

    def aln(batch):
        return AL0 + batch + 1

    def gldn(sub):
        return GLD0 + GLD_PER * (sub + 1)

    OUTY0 = cnt["s_out"]

    def ystn(rnd):
        return OUTY0 + 16 * (rnd + 1)

    def yen(rnd):
        return YE0 + rnd + 1

    def dtb_view(t):
        par = (t // 4) % 2
        return ps_bc2[par][:, (t % 4) * 128:(t % 4) * 128 + 128]

    def dtxb_view(t):
        par = (t // 4) % 2
        return ps_bc2[par][:, 512 + (t % 4) * 128:512 + (t % 4) * 128 + 128]

    def bcc_col(t, j):
        par = (t // SUB) % 2
        return bcc[:, par * 4 * SUB + j * SUB + (t % SUB):par * 4 * SUB + j * SUB + (t % SUB) + 1]

    def xvn_view(t):
        par = (t // SUB) % 2
        o = par * 2 * SUB + 2 * (t % SUB)
        return xvns[:, o:o + 2]

    def h_re(t):
        return h2[:, (t % 2) * 256:(t % 2) * 256 + 128]

    def h_im(t):
        return h2[:, (t % 2) * 256 + 128:(t % 2) * 256 + 256]

    def h_full(t):
        return h2[:, (t % 2) * 256:(t % 2) * 256 + 256]

    @blk.sync
    def _(sync):
        wt(sync, "phaseA")
        inc(sync.dma_start(out=dtiny[:, 0:2], in_=cdr["d0"]), "s_ld", 16)
        mark("d0ld", "s_ld")

    @blk.sync
    def _(sync):
        wt(sync, "phaseA")
        for sub in range(NSUB):
            par = sub % 2
            if sub >= 2:
                sync.wait_ge(sems["s_v"], V0 + V_PER * ((sub - 1) * SUB))
                sync.wait_ge(sems["s_pb"], PB0 + 2 * ((sub - 1) * SUB // 4))
            t0 = sub * SUB
            sv = dt_dram[:, t0:t0 + SUB].rearrange("(k dp) t -> k dp t", k=8)
            ov = slab_t[:, par * SUB * 128:(par + 1) * SUB * 128].rearrange(
                "k (dp t) -> k dp t", t=SUB)
            inc(sync.dma_start(out=ov, in_=sv), "s_sc", 16)
            sv2 = dtx_dram[:, t0:t0 + SUB].rearrange("(k dp) t -> k dp t", k=8)
            ov2 = slab_x[:, par * SUB * 128:(par + 1) * SUB * 128].rearrange(
                "k (dp t) -> k dp t", t=SUB)
            inc(sync.dma_start(out=ov2, in_=sv2), "s_sc", 16)
            bv = bc_dram[:, t0:t0 + SUB].rearrange("(j p) t -> p j t", j=4)
            obv = bcc[:, par * 4 * SUB:(par + 1) * 4 * SUB].rearrange(
                "p (j t) -> p j t", t=SUB)
            inc(sync.dma_start(out=obv, in_=bv), "s_sc", 16)
            inc(sync.dma_start(out=xvns[:, par * 2 * SUB:(par + 1) * 2 * SUB],
                               in_=xvn_dram[:, 2 * t0:2 * (t0 + SUB)]), "s_sc", 16)
        mark("scanloads", "s_sc")

    PRO_V = [0]

    @blk.vector
    def _(vector):
        wt(vector, "d0ld")
        vector.wait_ge(sems["s_sc"], gldn(0))
        vector.memset(h2[:, 256:512], 0.0)
        vector.memset(ones_bf[:], 1.0)
        d0c = dtiny[:, 0:2]
        inc(vector.tensor_tensor(dtiny[:, 4:6], d0c, d0c, ALU.mult), "s_v")
        vector.wait_ge(sems["s_v"], cnt["s_v"])
        inc(vector.tensor_reduce(dtiny[:, 6:7], dtiny[:, 4:6], AX.X, ALU.add), "s_v")
        mark("pro_ssq", "s_v")

    @blk.scalar
    def _(scalar):
        wt(scalar, "pro_ssq")
        inc(scalar.activation(dtiny[:, 7:8], dtiny[:, 6:7], AF.Ln, bias=1e-8), "s_a")
        mark("pro_lnn", "s_a")

    @blk.vector
    def _(vector):
        wt(vector, "pro_lnn")
        inc(vector.tensor_tensor(dtiny[:, 10:12], dtiny[:, 0:2], cs["epsC"][:],
                                 ALU.add), "s_v")
        vector.wait_ge(sems["s_v"], cnt["s_v"])
        inc(vector.tensor_tensor(dtiny[:, 12:14], dtiny[:, 10:12], dtiny[:, 10:12],
                                 ALU.mult), "s_v")
        inc(vector.tensor_tensor(dtiny[:, 14:16], xvns[:, 0:2], dtiny[:, 10:12],
                                 ALU.mult), "s_v")
        vector.wait_ge(sems["s_v"], cnt["s_v"])
        inc(vector.tensor_reduce(dtiny[:, 16:18], dtiny[:, 12:16].rearrange(
            "p (a b) -> p a b", a=2), AX.X, ALU.add), "s_v")
        inc(vector.tensor_copy(dtiny[:, 2:3], dtiny[:, 0:1]), "s_v")
        mark("pro_dd2", "s_v")

    @blk.scalar
    def _(scalar):
        wt(scalar, "pro_dd2")
        inc(scalar.activation(dtiny[:, 18:19], dtiny[:, 16:17], AF.Ln, bias=1e-20), "s_a")
        mark("pro_lnd", "s_a")

    @blk.vector
    def _(vector):
        wt(vector, "pro_lnd")
        vector.tensor_scalar(dtiny[:, 19:20], dtiny[:, 18:19], -0.5, None, ALU.mult)
        vector.tensor_scalar(dtiny[:, 20:21], dtiny[:, 7:8], 0.25, None, ALU.mult)
        inc(vector.tensor_scalar(dtiny[:, 21:22], dtiny[:, 7:8], 0.5, None, ALU.mult), "s_v")
        mark("pro_u", "s_v")

    @blk.scalar
    def _(scalar):
        wt(scalar, "pro_u")
        inc(scalar.activation(dtiny[:, 22:25], dtiny[:, 19:22], AF.Exp), "s_a")
        mark("pro_exps", "s_a")

    @blk.vector
    def _(vector):
        wt(vector, "pro_exps")
        inc(vector.tensor_tensor(dtiny[:, 25:27], dtiny[:, 23:25], cs["clim"][:],
                                 ALU.min), "s_v")
        vector.wait_ge(sems["s_v"], cnt["s_v"])
        inc(vector.tensor_scalar(dtiny[:, 27:28], dtiny[:, 26:27], 1e-8, None,
                                 ALU.add), "s_v")
        vector.wait_ge(sems["s_v"], cnt["s_v"])
        inc(vector.reciprocal(dtiny[:, 28:29], dtiny[:, 27:28]), "s_v")
        inc(vector.scalar_tensor_tensor(dtiny[:, 29:30], dtiny[:, 17:18], 0.5,
                                        dtiny[:, 22:23], ALU.mult, ALU.mult), "s_v")
        vector.wait_ge(sems["s_v"], cnt["s_v"])
        inc(vector.scalar_tensor_tensor(dtiny[:, 30:31], dtiny[:, 29:30], 0.5,
                                        dtiny[:, 25:26], ALU.add, ALU.mult), "s_v")
        vector.wait_ge(sems["s_v"], cnt["s_v"])
        inc(vector.tensor_tensor(dtiny[:, 31:32], dtiny[:, 30:31],
                                 dtiny[:, 28:29], ALU.mult), "s_v")
        vector.wait_ge(sems["s_v"], cnt["s_v"])
        inc(vector.tensor_scalar(surf[:], dtiny[:, 0:2], dtiny[:, 31:32], None,
                                 ALU.mult), "s_v")
        mark("pro_surf", "s_v")

    PROSURF = ev["pro_surf"][1]

    V0 = cnt["s_v"]
    A0 = cnt["s_a"]
    G0 = cnt["s_g"]
    P0 = cnt["s_p"]

    @blk.tensor
    def _(tensor):
        tensor.wait_ge(sems["s_v"], PROSURF)
        for t in range(T):
            if t % 4 == 0:
                batch = t // 4
                par = batch % 2
                tensor.wait_ge(sems["s_sc"], gldn(t // SUB))
                if batch >= 2:
                    tensor.wait_ge(sems["s_al"], aln(batch - 2))
                    tensor.wait_ge(sems["s_v"], vv(4 * (batch - 1) - 1, "h"))
                spar = (t // SUB) % 2
                svt = slab_t[:, spar * SUB * 128:(spar + 1) * SUB * 128].rearrange(
                    "k (dp tt) -> k dp tt", tt=SUB)[:, :, (t % SUB):(t % SUB) + 4]
                svx = slab_x[:, spar * SUB * 128:(spar + 1) * SUB * 128].rearrange(
                    "k (dp tt) -> k dp tt", tt=SUB)[:, :, (t % SUB):(t % SUB) + 4]
                svt = svt.transpose([0, 2, 1])
                svx = svx.transpose([0, 2, 1])
                inc(tensor.matmul(ps_bc2[par][:, 0:512], cs["mask8"][:], svt,
                                  start=True, stop=True), "s_pb")
                inc(tensor.matmul(ps_bc2[par][:, 512:1024], cs["mask8"][:], svx,
                                  start=True, stop=True), "s_pb")
            if t > 0:
                tensor.wait_ge(sems["s_v"], vv(t - 1, "surf"))
            inc(tensor.matmul(ps_surf[:], cs["repmask16"][:], surf[:],
                              start=True, stop=True), "s_p")
            if t % YC == 0 and t // YC >= 1:
                tensor.wait_ge(sems["s_ye"], yen(t // YC - 1))
            tensor.wait_ge(sems["s_v"], vv(t, "h"))
            yslot = ps_y[:, (t % YC) * 8:(t % YC) * 8 + 8]
            inc(tensor.matmul(yslot, h_re(t), yrhs[:, 0:8], start=True, stop=False), "s_p")
            inc(tensor.matmul(yslot, h_im(t), yrhs[:, 8:16], start=False, stop=True), "s_p")
            tensor.wait_ge(sems["s_v"], vv(t, "clip"))
            inc(tensor.matmul(ps_red[:], cs["smask16"][:], clipt[:],
                              start=True, stop=True), "s_p")
        mark("pe_scan", "s_p")

    @blk.scalar
    def _(scalar):
        scalar.wait_ge(sems["s_v"], PROSURF)
        for t in range(T):
            if t % 4 == 0:
                batch = t // 4
                par = batch % 2
                scalar.wait_ge(sems["s_pb"], pbn(batch))
                if batch >= 2:
                    scalar.wait_ge(sems["s_v"], vv(4 * (batch - 1) - 1, "hdec"))
                inc(scalar.activation(alph[:, par * 512:(par + 1) * 512],
                                      ps_bc2[par][:, 0:512], AF.Exp,
                                      scale=acol[:, 0:1]), "s_al")
            scalar.wait_ge(sems["s_g"], gg(t, "rel"))
            if t > 0:
                scalar.wait_ge(sems["s_g"], gg(t - 1, "M"))
            inc(scalar.activation(sqt[:], relt[:], AF.Square), "s_a")
            scalar.wait_ge(sems["s_g"], gg(t, "M"))
            if t > 0:
                scalar.wait_ge(sems["s_v"], vv(t - 1, "t3"))
            inc(scalar.activation(lnt[:], Mt[:], AF.Ln, bias=1e-8), "s_a")
            scalar.wait_ge(sems["s_v"], vv(t, "dd"))
            inc(scalar.activation(ldt[:], ddt[:], AF.Ln, bias=1e-20), "s_a")
            scalar.wait_ge(sems["s_v"], vv(t, "t3"))
            if t > 0:
                scalar.wait_ge(sems["s_v"], vv(t - 1, "clip"))
                scalar.wait_ge(sems["s_g"], gg(t - 1, "cim"))
            inc(scalar.activation(rt[:], t3t[:], AF.Exp, scale=1.0 / 3.0), "s_a")
            scalar.wait_ge(sems["s_v"], vv(t, "ssq"))
            inc(scalar.activation(dtiny[:, 7:8], dtiny[:, 6:7], AF.Ln, bias=1e-8), "s_a")
            inc(scalar.activation(dtiny[:, 8:9], dtiny[:, 7:8], AF.Exp,
                                  scale=-0.5, bias=LN20), "s_a")
            scalar.wait_ge(sems["s_v"], vv(t, "dd2"))
            inc(scalar.activation(dtiny[:, 18:19], dtiny[:, 16:17], AF.Ln,
                                  bias=1e-20), "s_a")
            scalar.wait_ge(sems["s_v"], vv(t, "u"))
            inc(scalar.activation(dtiny[:, 22:25], dtiny[:, 19:22], AF.Exp), "s_a")
            if t % YC == YC - 1:
                scalar.wait_ge(sems["s_p"], pp(t, "y"))
                rnd = t // YC
                if rnd >= 2:
                    scalar.wait_ge(sems["s_out"], ystn(rnd - 2))
                inc(scalar.activation(ysb[:, (rnd % 2) * 512:(rnd % 2) * 512 + 512],
                                      ps_y[:], AF.Identity), "s_ye")
        mark("act_scan", "s_a")

    @blk.gpsimd
    def _(gpsimd):
        gpsimd.wait_ge(sems["s_v"], PROSURF)
        for t in range(T):
            gpsimd.wait_ge(sems["s_v"], vv(t, "hdec"))
            if t > 0:
                gpsimd.wait_ge(sems["s_v"], vv(t - 1, "clip"))
                gpsimd.wait_ge(sems["s_a"], aa(t - 1, "sq"))
            inc(gpsimd.tensor_tensor(relt[:], h_full(t + 1), compt[:], ALU.subtract),
                "s_g")
            gpsimd.wait_ge(sems["s_a"], aa(t, "sq"))
            if t > 0:
                gpsimd.wait_ge(sems["s_a"], aa(t - 1, "lm"))
                gpsimd.wait_ge(sems["s_v"], vv(t - 1, "dd"))
            inc(gpsimd.tensor_tensor(Mt[:], sqt[:, 0:128], sqt[:, 128:256], ALU.add),
                "s_g")
            gpsimd.wait_ge(sems["s_a"], aa(t, "r"))
            inc(gpsimd.tensor_tensor(compt[:, 128:256], relt[:, 128:256], rt[:],
                                     ALU.mult), "s_g")
            if t % YC == YC - 1:
                rnd = t // YC
                gpsimd.wait_ge(sems["s_ye"], yen(rnd))
                inc(gpsimd.dma_start(out=y_dram[:, rnd * YC * 8:(rnd + 1) * YC * 8],
                                     in_=ysb[:, (rnd % 2) * 512:(rnd % 2) * 512 + 512]),
                    "s_out", 16)
        mark("pool_scan", "s_g")
        mark("ylast", "s_out")


    @blk.vector
    def _(vector):
        for t in range(T):
            vector.wait_ge(sems["s_sc"], gldn(t // SUB))
            if t > 0:
                vector.wait_ge(sems["s_p"], pp(t - 1, "y"))
            inc(vector.tensor_scalar(yrhs[:, 0:8], cs["c8maskT"][:], bcc_col(t, 2),
                                     None, ALU.mult), "s_v")
            inc(vector.tensor_scalar(yrhs[:, 8:16], cs["c8maskT"][:], bcc_col(t, 3),
                                     None, ALU.mult), "s_v")
            vector.wait_ge(sems["s_al"], aln(t // 4))
            if t > 0:
                vector.wait_ge(sems["s_g"], gg(t - 1, "rel"))
                vector.wait_ge(sems["s_v"], vv(t - 1, "clip"))
            apar = (t // 4) % 2
            av = alph[:, apar * 512 + (t % 4) * 128:apar * 512 + (t % 4) * 128 + 128]
            inc(vector.tensor_tensor(compt[:, 0:128], h_re(t + 1), av, ALU.mult), "s_v")
            inc(vector.tensor_tensor(compt[:, 128:256], h_im(t + 1), av, ALU.mult), "s_v")
            vector.wait_ge(sems["s_p"], pp(t, "surfb"))
            vector.wait_ge(sems["s_pb"], pbn(t // 4))
            vector.wait_ge(sems["s_v"], cnt["s_v"])
            inc(vector.scalar_tensor_tensor(h_re(t), dtb_view(t), ps_surf[:, 0:1],
                                            compt[:, 0:128], ALU.mult, ALU.add), "s_v")
            inc(vector.scalar_tensor_tensor(h_im(t), dtb_view(t), ps_surf[:, 1:2],
                                            compt[:, 128:256], ALU.mult, ALU.add), "s_v")
            vector.wait_ge(sems["s_v"], cnt["s_v"])
            inc(vector.scalar_tensor_tensor(h_re(t), dtxb_view(t), bcc_col(t, 0),
                                            h_re(t), ALU.mult, ALU.add), "s_v")
            inc(vector.scalar_tensor_tensor(h_im(t), dtxb_view(t), bcc_col(t, 1),
                                            h_im(t), ALU.mult, ALU.add), "s_v")
            vector.wait_ge(sems["s_g"], gg(t, "M"))
            if t > 0:
                vector.wait_ge(sems["s_a"], aa(t - 1, "ld"))
            inc(vector.scalar_tensor_tensor(ddt[:], relt[:, 0:128], 2e-10, Mt[:],
                                            ALU.mult, ALU.add), "s_v")
            vector.wait_ge(sems["s_a"], aa(t, "ld"))
            if t > 0:
                vector.wait_ge(sems["s_a"], aa(t - 1, "r"))
            inc(vector.scalar_tensor_tensor(t3t[:], ldt[:], -1.5, lnt[:],
                                            ALU.mult, ALU.add), "s_v")
            vector.wait_ge(sems["s_a"], aa(t, "r"))
            inc(vector.scalar_tensor_tensor(compt[:, 0:128], relt[:, 0:128], 1e-10,
                                            rt[:], ALU.add, ALU.mult), "s_v")
            vector.wait_ge(sems["s_g"], gg(t, "cim"))
            if t > 0:
                vector.wait_ge(sems["s_p"], pp(t - 1, "smask"))
            vector.wait_ge(sems["s_v"], cnt["s_v"])
            inc(vector.tensor_scalar(clipt[:], compt[:], 10.0, -10.0,
                                     ALU.min, ALU.max), "s_v")
            vector.wait_ge(sems["s_p"], pp(t, "smask"))
            inc(vector.tensor_reduce(davg[:], ps_red[:].rearrange(
                "p (r dp) -> p r dp", r=2), AX.X, ALU.add), "s_v")
            if t > 0:
                vector.wait_ge(sems["s_a"], aa(t - 1, "lnn"))
            vector.wait_ge(sems["s_v"], cnt["s_v"])
            inc(vector.scalar_tensor_tensor(dtiny[:, 2:4], davg[:], CAVG,
                                            dtiny[:, 0:2], ALU.mult, ALU.add), "s_v")
            vector.wait_ge(sems["s_v"], cnt["s_v"])
            inc(vector.tensor_tensor(dtiny[:, 4:6], dtiny[:, 2:4], dtiny[:, 2:4],
                                     ALU.mult), "s_v")
            vector.wait_ge(sems["s_v"], cnt["s_v"])
            inc(vector.tensor_reduce(dtiny[:, 6:7], dtiny[:, 4:6], AX.X, ALU.add),
                "s_v")
            vector.wait_ge(sems["s_a"], aa(t, "e1"))
            vector.wait_ge(sems["s_sc"], gldn(min((t + 1) // SUB, NSUB - 1)))
            inc(vector.tensor_scalar(dtiny[:, 9:10], dtiny[:, 8:9], 1.0, None,
                                     ALU.min), "s_v")
            vector.wait_ge(sems["s_v"], cnt["s_v"])
            inc(vector.tensor_scalar(dtiny[:, 0:2], dtiny[:, 2:4], dtiny[:, 9:10],
                                     None, ALU.mult), "s_v")
            vector.wait_ge(sems["s_v"], cnt["s_v"])
            inc(vector.tensor_tensor(dtiny[:, 10:12], dtiny[:, 0:2], cs["epsC"][:],
                                     ALU.add), "s_v")
            vector.wait_ge(sems["s_v"], cnt["s_v"])
            inc(vector.tensor_tensor(dtiny[:, 12:14], dtiny[:, 10:12],
                                     dtiny[:, 10:12], ALU.mult), "s_v")
            inc(vector.tensor_tensor(dtiny[:, 14:16], xvn_view(t + 1),
                                     dtiny[:, 10:12], ALU.mult), "s_v")
            vector.wait_ge(sems["s_v"], cnt["s_v"])
            inc(vector.tensor_reduce(dtiny[:, 16:18], dtiny[:, 12:16].rearrange(
                "p (a b) -> p a b", a=2), AX.X, ALU.add), "s_v")
            vector.wait_ge(sems["s_a"], aa(t, "lnd"))
            inc(vector.tensor_scalar(dtiny[:, 19:20], dtiny[:, 18:19], -0.5, None,
                                     ALU.mult), "s_v")
            inc(vector.tensor_scalar(dtiny[:, 20:21], dtiny[:, 7:8], 0.25, None,
                                     ALU.mult), "s_v")
            inc(vector.tensor_scalar(dtiny[:, 21:22], dtiny[:, 7:8], 0.5, None,
                                     ALU.mult), "s_v")
            vector.wait_ge(sems["s_a"], aa(t, "exps"))
            inc(vector.tensor_tensor(dtiny[:, 25:27], dtiny[:, 23:25], cs["clim"][:],
                                     ALU.min), "s_v")
            vector.wait_ge(sems["s_v"], cnt["s_v"])
            inc(vector.tensor_scalar(dtiny[:, 27:28], dtiny[:, 26:27], 1e-8, None,
                                     ALU.add), "s_v")
            vector.wait_ge(sems["s_v"], cnt["s_v"])
            inc(vector.reciprocal(dtiny[:, 28:29], dtiny[:, 27:28]), "s_v")
            inc(vector.scalar_tensor_tensor(dtiny[:, 29:30], dtiny[:, 17:18], 0.5,
                                            dtiny[:, 22:23], ALU.mult, ALU.mult), "s_v")
            vector.wait_ge(sems["s_v"], cnt["s_v"])
            inc(vector.scalar_tensor_tensor(dtiny[:, 30:31], dtiny[:, 29:30], 0.5,
                                            dtiny[:, 25:26], ALU.add, ALU.mult), "s_v")
            vector.wait_ge(sems["s_v"], cnt["s_v"])
            inc(vector.tensor_tensor(dtiny[:, 31:32], dtiny[:, 30:31],
                                     dtiny[:, 28:29], ALU.mult), "s_v")
            vector.wait_ge(sems["s_v"], cnt["s_v"])
            inc(vector.tensor_scalar(surf[:], dtiny[:, 0:2], dtiny[:, 31:32], None,
                                     ALU.mult), "s_v")
        mark("dve_scan", "s_v")

    mark("scan_done_v", "s_v")
    mark("scan_done_a", "s_a")
    mark("scan_done_g", "s_g")
    mark("scan_done_p", "s_p")
    mark("scan_done_out", "s_out")

    @blk.vector
    def _(vector):
        for evn in ["scan_done_a", "scan_done_g", "scan_done_p", "scan_done_out"]:
            wt(vector, evn)
        inc(vector.tensor_copy(dtiny[:, 0:1], dtiny[:, 1:2]), "s_v")
        mark("barrier_v", "s_v")

    @blk.scalar
    def _(scalar):
        for evn in ["barrier_v"]:
            wt(scalar, evn)
        inc(scalar.copy(dtiny[:, 0:1], dtiny[:, 1:2]), "s_a")
        mark("barrier_a", "s_a")

    @blk.gpsimd
    def _(gpsimd):
        wt(gpsimd, "barrier_v")
        inc(gpsimd.tensor_copy(dtiny[:, 0:1], dtiny[:, 1:2]), "s_g")
        mark("barrier_g", "s_g")

    @blk.tensor
    def _(tensor):
        wt(tensor, "barrier_v")
        inc(tensor.matmul(ps_surf[0:16, 0:1], cs["repmask16"][:, 0:16], dtiny[0:16, 0:1],
                          start=True, stop=True), "s_p")
        mark("barrier_p", "s_p")

    @blk.sync
    def _(sync):
        wt(sync, "barrier_v")
        sync.wait_ge(sems["s_ld"], cnt["s_ld"])
        inc(sync.dma_start(out=dtiny[:, 0:2], in_=cdr["d0"]), "s_ld", 16)
        mark("barrier_s", "s_ld")

    pb_scope.close()

    pc = ExitStack()

    def csb(name, shape, dt=F32):
        return pc.enter_context(nc.sbuf_tensor(name, shape, dt))

    def cps(name, shape, dt=F32):
        return pc.enter_context(nc.psum_tensor(name, shape, dt))

    ytile = csb("ytile", [128, 8 * TILE])
    zstile = csb("zstile", [128, 2 * TILE])
    yz = csb("yz", [128, 8 * TILE], BF16)
    xres = csb("xres", [128, 2 * D])
    x1sb = csb("x1sb", [128, TG * D])
    cstat = csb("cstat", [128, 3 * TG])
    csqs = csb("csqs", [128, D])
    x2n_tm = csb("x2n_tm", [128, TG * D], BF16)
    ps_c1 = cps("ps_c1", [128, 512])
    ps_c2 = cps("ps_c2", [128, 512])
    ps_tm = cps("ps_tm", [128, 1024])
    ps_g16 = cps("ps_g16", [128, 16])
    ps_str = cps("ps_str", [16, 128])

    def phase_c(tile):
        t0 = tile * TILE
        tg = "_c%d" % tile

        CYZV0 = cnt["s_v"]

        @blk.sync
        def _(sync):
            if tile == 0:
                wt(sync, "barrier_s")
            else:
                wt(sync, "c1done_%d" % (tile - 1))
            inc(sync.dma_start(out=ytile[:], in_=y_dram[:, t0 * 8:(t0 + TILE) * 8]),
                "s_ld", 16)
            for c8 in range(8):
                if c8 >= 2:
                    sync.wait_ge(sems["s_v"], CYZV0 + c8 - 1)
                inc(sync.dma_start(out=zstile[:, (c8 % 2) * TILE:(c8 % 2 + 1) * TILE],
                                   in_=zsil_dram[c8 * 128:(c8 + 1) * 128, t0:t0 + TILE]),
                    "s_ld", 16)
                mark("czs%d" % c8 + tg, "s_ld")
            mark("cyld" + tg, "s_ld")

        @blk.vector
        def _(vector):
            if tile == 0:
                wt(vector, "barrier_a")
                wt(vector, "barrier_g")
                wt(vector, "barrier_p")
            for c8 in range(8):
                wt(vector, "czs%d" % c8 + tg)
                yv = ytile[:].rearrange("p (t c) -> p t c", c=8)[:, :, c8:c8 + 1]
                inc(vector.tensor_tensor(
                    yz[:, c8 * TILE:(c8 + 1) * TILE].unsqueeze(2),
                    yv, zstile[:, (c8 % 2) * TILE:(c8 % 2 + 1) * TILE].unsqueeze(2),
                    ALU.mult), "s_v")
            mark("cyz" + tg, "s_v")

        CX1V0 = cnt["s_v"]

        @blk.tensor
        def _(tensor):
            wt(tensor, "cyz" + tg)
            if tile == 0:
                wt(tensor, "barrier_a")
                wt(tensor, "barrier_g")
            for g in range(TG):
                if g >= 1:
                    tensor.wait_ge(sems["s_v"], CX1V0 + g)
                for h2 in range(2):
                    for c8 in range(8):
                        inc(tensor.matmul(ps_tm[:, h2 * 512:(h2 + 1) * 512],
                                          yz[:, c8 * TILE + g * 128:c8 * TILE + (g + 1) * 128],
                                          w_out_sb[:, c8 * 1024 + h2 * 512:c8 * 1024 + (h2 + 1) * 512],
                                          start=(c8 == 0), stop=False), "s_p")
                    inc(tensor.matmul(ps_tm[:, h2 * 512:(h2 + 1) * 512], ones_bf[0:1, :],
                                      b_out_sb[:, h2 * 512:(h2 + 1) * 512],
                                      start=False, stop=True), "s_p")
            mark("cwout" + tg, "s_p")

        pbase = ev["cwout" + tg][1] - TG * 18

        @blk.sync
        def _(sync):
            for g in range(TG):
                if g >= 2:
                    sync.wait_ge(sems["s_v"], CX1V0 + g - 1)
                xv = x_tm[t0 + g * 128:t0 + (g + 1) * 128, :]
                inc(sync.dma_start(out=xres[:, (g % 2) * D:(g % 2 + 1) * D], in_=xv),
                    "s_ld", 16)
                mark("cxr%d" % g + tg, "s_ld")

        @blk.vector
        def _(vector):
            for g in range(TG):
                vector.wait_ge(sems["s_p"], pbase + (g + 1) * 18)
                wt(vector, "cxr%d" % g + tg)
                inc(vector.tensor_tensor(x1sb[:, g * D:(g + 1) * D],
                                         xres[:, (g % 2) * D:(g % 2 + 1) * D],
                                         ps_tm[:], ALU.add), "s_v")
            mark("cx1" + tg, "s_v")

        @blk.sync
        def _(sync):
            wt(sync, "cx1" + tg)
            xo = x1_dram[t0:t0 + TILE, :].rearrange("(g p) d -> p g d", p=128)
            inc(sync.dma_start(out=xo, in_=x1sb[:].rearrange("p (g d) -> p g d", d=D)),
                "s_ld", 16)
            mark("cx1st" + tg, "s_ld")

        @blk.vector
        def _(vector):
            wt(vector, "cx1" + tg)
            for g in range(TG):
                vector.tensor_scalar(csqs[:, 0:D], x1sb[:, g * D:(g + 1) * D], 1.0,
                                     0.0, ALU.mult, ALU.add, accum_out=cstat[:, 3 * g:3 * g + 1])
            inc(vector.tensor_copy(csqs[:, 0:1], cstat[:, 0:1]), "s_v")
            mark("csx" + tg, "s_v")

        @blk.scalar
        def _(scalar):
            wt(scalar, "csx" + tg)
            if tile == 0:
                wt(scalar, "barrier_g")
                wt(scalar, "barrier_p")
            for g in range(TG):
                scalar.activation(csqs[:, 0:D], x1sb[:, g * D:(g + 1) * D], AF.Square,
                                  accum_out=cstat[:, 3 * g + 1:3 * g + 2])
            inc(scalar.copy(csqs[:, 0:1], cstat[:, 0:1]), "s_a")
            mark("csxx" + tg, "s_a")

        @blk.vector
        def _(vector):
            wt(vector, "csxx" + tg)
            for g in range(TG):
                mcol = cstat[:, 3 * g:3 * g + 1]
                vcol = cstat[:, 3 * g + 1:3 * g + 2]
                m2col = cstat[:, 3 * g + 2:3 * g + 3]
                inc(vector.tensor_scalar(mcol, mcol, 1.0 / D, None, ALU.mult), "s_v")
                vector.wait_ge(sems["s_v"], cnt["s_v"])
                inc(vector.tensor_tensor(m2col, mcol, mcol, ALU.mult), "s_v")
                vector.wait_ge(sems["s_v"], cnt["s_v"])
                vector.scalar_tensor_tensor(vcol, vcol, 1.0 / D, m2col, ALU.mult,
                                            ALU.subtract)
            inc(vector.tensor_copy(csqs[:, 0:1], cstat[:, 0:1]), "s_v")
            mark("cvar" + tg, "s_v")

        @blk.scalar
        def _(scalar):
            wt(scalar, "cvar" + tg)
            for g in range(TG):
                vcol = cstat[:, 3 * g + 1:3 * g + 2]
                scalar.activation(vcol, vcol, AF.Ln, bias=LNEPS)
                scalar.activation(vcol, vcol, AF.Exp, scale=-0.5)
            inc(scalar.copy(csqs[:, 0:1], cstat[:, 0:1]), "s_a")
            mark("crstd" + tg, "s_a")

        @blk.vector
        def _(vector):
            wt(vector, "crstd" + tg)
            for g in range(TG):
                vector.tensor_scalar(x2n_tm[:, g * D:(g + 1) * D],
                                     x1sb[:, g * D:(g + 1) * D],
                                     cstat[:, 3 * g:3 * g + 1],
                                     cstat[:, 3 * g + 1:3 * g + 2],
                                     ALU.subtract, ALU.mult)
            inc(vector.tensor_copy(csqs[:, 0:1], cstat[:, 0:1]), "s_v")
            mark("cx2n" + tg, "s_v")

        @blk.sync
        def _(sync):
            wt(sync, "cx2n" + tg)
            xo = x2n_dram[t0:t0 + TILE, :].rearrange("(g p) d -> p g d", p=128)
            inc(sync.dma_start(out=xo, in_=x2n_tm[:].rearrange("p (g d) -> p g d", d=D)),
                "s_ld", 16)
            mark("cx2st" + tg, "s_ld")

        mark("c1done_%d" % tile, "s_ld")

    for tile in range(NT):
        phase_c(tile)
    mark("c1out", "s_ld")
    pc.close()

    pc = ExitStack()
    w_sg_sb = csb("w_sg_sb2", [128, 8 * FSH], BF16)
    w_su_sb = csb("w_su_sb2", [128, 8 * FSH], BF16)
    w_sd_sb = csb("w_sd_sb2", [128, 16 * D], BF16)
    x2n_tm = csb("x2n_tm2", [128, TG * D], BF16)
    x2n_fm = csb("x2n_fm2", [128, 8 * TILE], BF16)
    hsh = csb("hsh", [128, 16 * TILE], BF16)
    x1sb = csb("x1sb2", [128, 2 * D])
    zstile = csb("zstile2", [128, TILE])
    csqs = csb("csqs2", [128, TILE])
    cstat = csb("cstat2", [128, 4])
    scem_sb = csb("scem_sb", [16, TILE])
    ps_c1 = cps("ps_c12", [128, 512])
    ps_c2 = cps("ps_c22", [128, 512])
    ps_tm = cps("ps_tm2", [128, 1024])
    ps_g16 = cps("ps_g162", [128, 16])
    ps_str = cps("ps_str2", [16, 128])

    @blk.gpsimd
    def _(gpsimd):
        wt(gpsimd, "c1out")
        for k in range(8):
            kk = slice(k * 128, (k + 1) * 128)
            inc(gpsimd.dma_start(out=w_sg_sb[:, k * FSH:(k + 1) * FSH],
                                 in_=wdr["w_sg_t"][kk, :]), "s_out", 16)
            inc(gpsimd.dma_start(out=w_su_sb[:, k * FSH:(k + 1) * FSH],
                                 in_=wdr["w_su_t"][kk, :]), "s_out", 16)
        for k in range(16):
            inc(gpsimd.dma_start(out=w_sd_sb[:, k * D:(k + 1) * D],
                                 in_=wdr["w_sd_t"][k * 128:(k + 1) * 128, :]), "s_out", 16)
        mark("cshw", "s_out")

    def phase_c2(tile):
        t0 = tile * TILE
        tg = "_c%d" % tile

        @blk.sync
        def _(sync):
            if tile == 0:
                wt(sync, "c1out")
            else:
                wt(sync, "cdone_%d" % (tile - 1))
            xo = x2n_dram[t0:t0 + TILE, :].rearrange("(g p) d -> p g d", p=128)
            inc(sync.dma_start(out=x2n_tm[:].rearrange("p (g d) -> p g d", d=D),
                               in_=xo), "s_ld", 16)
            mark("cx2ld" + tg, "s_ld")
        mark("cx2n" + tg, ev["cx2ld" + tg][0]) if False else None
        ev["cx2n" + tg] = ev["cx2ld" + tg]

        vbase = cnt["s_v"]

        @blk.tensor
        def _(tensor):
            wt(tensor, "cx2n" + tg)
            for i in range(32):
                pst = ps_c1 if i % 2 == 0 else ps_c2
                if i >= 2:
                    tensor.wait_ge(sems["s_v"], vbase + i - 1)
                g, k = divmod(i, 8)
                inc(tensor.transpose(pst[:].bitcast(BF16)[:, 0:128],
                                     x2n_tm[:, g * D + k * 128:g * D + (k + 1) * 128],
                                     ident_bf[:]), "s_p")
            mark("ctr" + tg, "s_p")

        pbase2 = ev["ctr" + tg][1] - 32

        @blk.vector
        def _(vector):
            for i in range(32):
                pst = ps_c1 if i % 2 == 0 else ps_c2
                vector.wait_ge(sems["s_p"], pbase2 + i + 1)
                g, k = divmod(i, 8)
                inc(vector.tensor_copy(
                    x2n_fm[:, k * TILE + g * 128:k * TILE + (g + 1) * 128],
                    pst[:].bitcast(BF16)[:, 0:128]), "s_v")
            mark("cx2fm" + tg, "s_v")

        CGV0 = cnt["s_v"]

        @blk.tensor
        def _(tensor):
            wt(tensor, "cx2fm" + tg)
            if tile == 0:
                tensor.wait_ge(sems["s_out"], ev["cshw"][1])
            for g in range(TG):
                if g >= 1:
                    tensor.wait_ge(sems["s_v"], CGV0 + 5 * (g - 1) + 3)
                for k in range(8):
                    inc(tensor.matmul(ps_g16[:],
                                      x2n_fm[:, k * TILE + g * 128:k * TILE + (g + 1) * 128],
                                      gate_sb[:, k * E:(k + 1) * E],
                                      start=(k == 0), stop=(k == 7)), "s_p")
            mark("cgmm" + tg, "s_p")

        pbase3 = ev["cgmm" + tg][1] - TG * 8
        abase = cnt["s_a"]

        @blk.vector
        def _(vector):
            for g in range(TG):
                gcol = (tile * TG + g) * E
                vector.wait_ge(sems["s_p"], pbase3 + (g + 1) * 8)
                inc(vector.tensor_reduce(cstat[:, 0:1], ps_g16[:], AX.X, ALU.max),
                    "s_v")
                vector.wait_ge(sems["s_v"], cnt["s_v"])
                inc(vector.tensor_scalar(cstat[:, 1:2], cstat[:, 0:1], -1.0, None,
                                         ALU.mult), "s_v")
                vector.wait_ge(sems["s_v"], cnt["s_v"])
                inc(vector.tensor_scalar(sco_tm[:, gcol:gcol + E], ps_g16[:],
                                         cstat[:, 1:2], None, ALU.add), "s_v")
                vector.wait_ge(sems["s_a"], abase + g + 1)
                inc(vector.reciprocal(cstat[:, 2:3], cstat[:, 1:2]), "s_v")
                vector.wait_ge(sems["s_v"], cnt["s_v"])
                inc(vector.tensor_scalar(sco_tm[:, gcol:gcol + E],
                                         sco_tm[:, gcol:gcol + E],
                                         cstat[:, 2:3], None, ALU.mult), "s_v")
            mark("csm" + tg, "s_v")

        @blk.scalar
        def _(scalar):
            for g in range(TG):
                gcol = (tile * TG + g) * E
                scalar.wait_ge(sems["s_v"], CGV0 + 5 * g + 3)
                inc(scalar.activation(sco_tm[:, gcol:gcol + E], sco_tm[:, gcol:gcol + E],
                                      AF.Exp, accum_out=cstat[:, 1:2]), "s_a")
            mark("cexp" + tg, "s_a")

        CEMV0 = cnt["s_v"]

        @blk.tensor
        def _(tensor):
            wt(tensor, "csm" + tg)
            for g in range(TG):
                if g >= 1:
                    tensor.wait_ge(sems["s_v"], CEMV0 + g)
                gcol = (tile * TG + g) * E
                inc(tensor.transpose(ps_str[:], sco_tm[:, gcol:gcol + E], cs["ident"][:]),
                    "s_p")
            mark("cstr" + tg, "s_p")

        pbase4 = ev["cstr" + tg][1] - TG

        @blk.vector
        def _(vector):
            for g in range(TG):
                vector.wait_ge(sems["s_p"], pbase4 + g + 1)
                inc(vector.tensor_copy(scem_sb[:, g * 128:(g + 1) * 128], ps_str[:]),
                    "s_v")
            mark("cem" + tg, "s_v")

        @blk.sync
        def _(sync):
            wt(sync, "cem" + tg)
            inc(sync.dma_start(out=scem_dram[:, t0:t0 + TILE], in_=scem_sb[:]),
                "s_ld", 16)
            mark("cemst" + tg, "s_ld")

        CHSH0 = cnt["s_v"]

        @blk.tensor
        def _(tensor):
            wt(tensor, "cx2fm" + tg)
            wt(tensor, "cstr" + tg)
            for fc in range(16):
                if fc >= 1:
                    tensor.wait_ge(sems["s_v"], CHSH0 + fc)
                for k in range(8):
                    inc(tensor.matmul(ps_c1[:],
                                      w_sg_sb[:, k * FSH + fc * 128:k * FSH + (fc + 1) * 128],
                                      x2n_fm[:, k * TILE:(k + 1) * TILE],
                                      start=(k == 0), stop=(k == 7)), "s_p")
                for k in range(8):
                    inc(tensor.matmul(ps_c2[:],
                                      w_su_sb[:, k * FSH + fc * 128:k * FSH + (fc + 1) * 128],
                                      x2n_fm[:, k * TILE:(k + 1) * TILE],
                                      start=(k == 0), stop=(k == 7)), "s_p")
            mark("cgu" + tg, "s_p")

        pbase5 = ev["cgu" + tg][1] - 16 * 16
        abase2 = cnt["s_a"]

        @blk.scalar
        def _(scalar):
            for fc in range(16):
                scalar.wait_ge(sems["s_p"], pbase5 + fc * 16 + 8)
                if fc >= 1:
                    scalar.wait_ge(sems["s_v"], CHSH0 + fc)
                inc(scalar.activation(csqs[:, 0:TILE], ps_c1[:], AF.Gelu), "s_a")
            mark("cerf" + tg, "s_a")

        @blk.vector
        def _(vector):
            for fc in range(16):
                vector.wait_ge(sems["s_a"], abase2 + fc + 1)
                vector.wait_ge(sems["s_p"], pbase5 + (fc + 1) * 16)
                inc(vector.tensor_tensor(hsh[:, fc * TILE:(fc + 1) * TILE],
                                         csqs[:, 0:TILE], ps_c2[:], ALU.mult), "s_v")
            mark("chsh" + tg, "s_v")

        CBASE0 = cnt["s_v"]

        @blk.tensor
        def _(tensor):
            wt(tensor, "chsh" + tg)
            for g in range(TG):
                if g >= 1:
                    tensor.wait_ge(sems["s_v"], CBASE0 + g)
                for h2 in range(2):
                    for k in range(16):
                        inc(tensor.matmul(ps_tm[:, h2 * 512:(h2 + 1) * 512],
                                          hsh[:, k * TILE + g * 128:k * TILE + (g + 1) * 128],
                                          w_sd_sb[:, k * D + h2 * 512:k * D + (h2 + 1) * 512],
                                          start=(k == 0), stop=(k == 15)), "s_p")
            mark("csd" + tg, "s_p")

        pbase6 = ev["csd" + tg][1] - TG * 32

        st_cnt = {}

        @blk.sync
        def _(sync):
            for i in range(TG + 1):
                if i < TG:
                    g = i
                    if g >= 2:
                        sync.wait_ge(sems["s_v"], CBASE0 + g - 1)
                        sync.wait_ge(sems["s_ld"], st_cnt[g - 2])
                    inc(sync.dma_start(out=x1sb[:, (g % 2) * D:(g % 2 + 1) * D],
                                       in_=x1_dram[t0 + g * 128:t0 + (g + 1) * 128, :]),
                        "s_ld", 16)
                    mark("cx1r%d" % g + tg, "s_ld")
                if i >= 1:
                    g = i - 1
                    sync.wait_ge(sems["s_v"], CBASE0 + g + 1)
                    inc(sync.dma_start(out=out[t0 + g * 128:t0 + (g + 1) * 128, :],
                                       in_=x1sb[:, (g % 2) * D:(g % 2 + 1) * D]),
                        "s_ld", 16)
                    st_cnt[g] = cnt["s_ld"]
            mark("cdone_%d" % tile, "s_ld")

        @blk.vector
        def _(vector):
            for g in range(TG):
                vector.wait_ge(sems["s_p"], pbase6 + (g + 1) * 32)
                wt(vector, "cx1r%d" % g + tg)
                inc(vector.tensor_tensor(x1sb[:, (g % 2) * D:(g % 2 + 1) * D],
                                         x1sb[:, (g % 2) * D:(g % 2 + 1) * D],
                                         ps_tm[:], ALU.add), "s_v")
            mark("cbase" + tg, "s_v")

    for tile in range(NT):
        phase_c2(tile)
    mark("baseout", "s_ld")
    pc.close()

    pc = ExitStack()
    TE = T // 8
    assert TE % 128 == 0, "routing phase needs T to be a multiple of 1024"
    RBLK = TE // 128

    seb = csb("seb", [128, TE])
    mwork = csb("mwork", [128, TE])
    msku = csb("msku", [128, TE], mybir.dt.uint32)
    prefx = csb("prefx", [128, TE])
    zrow = csb("zrow", [128, TE])
    theta = csb("theta", [16, 4])
    cnt128 = csb("cnt128", [128, 1])
    rkm = csb("rkm", [128, TE])
    rkt = csb("rkt", [128, RBLK * 128])
    idc = csb("idc", [128, 2])
    rhs2 = csb("rhs2", [128, 2])
    ps_th = cps("ps_th", [128, 1])
    ps_ct = cps("ps_ct", [16, 1])
    ps_slb = [cps("ps_sl%d" % _b, [128, 2 * E]) for _b in range(NBLK)]
    ps_tr2 = cps("ps_tr2", [128, 128])

    @blk.sync
    def _(sync):
        wt(sync, "cemst_c%d" % (NT - 1))
        sv = scem_dram[:, :].rearrange("e (c t) -> (e c) t", c=8)
        inc(sync.dma_start(out=seb[:], in_=sv), "s_ld", 16)
        mark("sebld", "s_ld")

    @blk.vector
    def _(vector):
        wt(vector, "sebld")
        vector.memset(theta[:, 0:1], 0.5)
        inc(vector.memset(zrow[:], 0.0), "s_v")
        mark("bis0", "s_v")

    NITER = 45
    BISV0 = cnt["s_v"]

    @blk.tensor
    def _(tensor):
        wt(tensor, "bis0")
        for it in range(NITER):
            if it > 0:
                tensor.wait_ge(sems["s_v"], BISV0 + 4 * it)
            inc(tensor.matmul(ps_th[:], cs["repmask16"][:], theta[:, 0:1],
                              start=True, stop=True), "s_p")
            tensor.wait_ge(sems["s_v"], BISV0 + 4 * it + 1)
            inc(tensor.matmul(ps_ct[:], cs["smask16"][:], cnt128[:],
                              start=True, stop=True), "s_p")
        mark("bismm", "s_p")

    pbb = ev["bismm"][1] - 2 * NITER

    @blk.vector
    def _(vector):
        delta = 0.25
        for it in range(NITER):
            vector.wait_ge(sems["s_p"], pbb + 2 * it + 1)
            inc(vector.tensor_scalar(mwork[:], seb[:], ps_th[:, 0:1], 0.0,
                                     ALU.is_gt, ALU.add, accum_out=cnt128[:]), "s_v")
            vector.wait_ge(sems["s_p"], pbb + 2 * it + 2)
            inc(vector.tensor_scalar(theta[:, 3:4], ps_ct[:, 0:1], float(T / 8.0),
                                     None, ALU.is_gt), "s_v")
            vector.wait_ge(sems["s_v"], cnt["s_v"])
            inc(vector.scalar_tensor_tensor(theta[:, 1:2], theta[:, 3:4], 2.0 * delta,
                                            theta[:, 0:1], ALU.mult, ALU.add), "s_v")
            vector.wait_ge(sems["s_v"], cnt["s_v"])
            inc(vector.tensor_scalar(theta[:, 0:1], theta[:, 1:2], -delta, None,
                                     ALU.add), "s_v")
            delta *= 0.5
        vector.wait_ge(sems["s_v"], cnt["s_v"])
        inc(vector.tensor_scalar(theta[:, 0:1], theta[:, 0:1], 8.0 * delta, None,
                                 ALU.add), "s_v")
        mark("bisend", "s_v")

    @blk.tensor
    def _(tensor):
        wt(tensor, "bisend")
        inc(tensor.matmul(ps_th[:], cs["repmask16"][:], theta[:, 0:1],
                          start=True, stop=True), "s_p")
        mark("thbc", "s_p")

    @blk.vector
    def _(vector):
        wt(vector, "thbc")
        inc(vector.tensor_scalar(mwork[:], seb[:], ps_th[:, 0:1], None, ALU.is_gt),
            "s_v")
        vector.tensor_scalar(msku[:], seb[:], ps_th[:, 0:1], None, ALU.is_gt)
        vector.wait_ge(sems["s_v"], cnt["s_v"])
        inc(vector.tensor_tensor_scan(prefx[:], mwork[:], zrow[:], 0.0,
                                      ALU.add, ALU.add), "s_v")
        vector.wait_ge(sems["s_v"], cnt["s_v"])
        inc(vector.tensor_copy(cnt128[:], prefx[:, TE - 1:TE]), "s_v")
        mark("pfx", "s_v")

    @blk.tensor
    def _(tensor):
        wt(tensor, "pfx")
        inc(tensor.matmul(ps_th[:], cs["trig8"][:], cnt128[:], start=True, stop=True),
            "s_p")
        mark("offs", "s_p")

    @blk.vector
    def _(vector):
        wt(vector, "offs")
        vector.memset(zrow[:], DUMP)
        inc(vector.tensor_scalar(prefx[:], prefx[:], ps_th[:, 0:1], -1.0,
                                 ALU.add, ALU.add), "s_v")
        vector.wait_ge(sems["s_v"], cnt["s_v"])
        inc(vector.select(rkm[:], msku[:], prefx[:], zrow[:]), "s_v")
        mark("rkm", "s_v")

    RKTV0 = cnt["s_v"]

    @blk.tensor
    def _(tensor):
        wt(tensor, "rkm")
        for b2 in range(RBLK):
            w2 = min(128, TE - b2 * 128)
            if b2 >= 1:
                tensor.wait_ge(sems["s_v"], RKTV0 + b2)
            inc(tensor.transpose(ps_tr2[0:w2, :], rkm[:, b2 * 128:b2 * 128 + w2],
                                 cs["ident"][:]), "s_p")
        mark("rktr", "s_p")

    pbt = ev["rktr"][1] - RBLK

    @blk.vector
    def _(vector):
        for b2 in range(RBLK):
            w2 = min(128, TE - b2 * 128)
            vector.wait_ge(sems["s_p"], pbt + b2 + 1)
            inc(vector.tensor_copy(rkt[:, b2 * 128:(b2 + 1) * 128],
                                   ps_tr2[0:w2, :].to_broadcast([128, 128])
                                   if False else ps_tr2[:, 0:128]), "s_v")
        mark("rkt_done", "s_v")

    NG = T // 128
    oh2 = csb("oh2", [128, 2 * 512])
    rhs4 = csb("rhs4", [128, 4])

    VOH0 = cnt["s_v"]
    POH0 = cnt["s_p"]

    @blk.vector
    def _(vector):
        wt(vector, "rkt_done")
        for e in range(E):
            for g in range(NG):
                tc8, b2 = divmod(g, RBLK)
                it = e * NG + g
                par = it % 2
                if it >= 2:
                    vector.wait_ge(sems["s_p"], POH0 + (it - 1) * NBLK)
                rcol = rkt[:, b2 * 128 + e * 8 + tc8:b2 * 128 + e * 8 + tc8 + 1]
                vector.tensor_scalar(rhs4[:, 2 * par:2 * par + 1], cs["iotacol"][:],
                                     float(tc8 * TE + b2 * 128), None, ALU.add)
                vector.wait_ge(sems["s_v"], cnt["s_v"])
                inc(vector.tensor_copy(rhs4[:, 2 * par + 1:2 * par + 2],
                                       sco_tm[:, g * E + e:g * E + e + 1]), "s_v")
                inc(vector.tensor_scalar(oh2[:, par * 512:par * 512 + 512],
                                         cs["iota512"][:], rcol, None,
                                         ALU.is_equal), "s_v")
        mark("ohv", "s_v")

    @blk.tensor
    def _(tensor):
        for e in range(E):
            for g in range(NG):
                it = e * NG + g
                par = it % 2
                tensor.wait_ge(sems["s_v"], VOH0 + 2 * (it + 1))
                for b3 in range(NBLK):
                    inc(tensor.matmul(ps_slb[b3][:, 2 * e:2 * e + 2],
                                      oh2[:, par * 512 + b3 * 128:par * 512 + (b3 + 1) * 128],
                                      rhs4[:, 2 * par:2 * par + 2],
                                      start=(g == 0), stop=(g == NG - 1)), "s_p")
        mark("slotmm", "s_p")

    @blk.scalar
    def _(scalar):
        wt(scalar, "slotmm")
        for b3 in range(NBLK):
            inc(scalar.activation(slots[:, b3 * 2 * E:(b3 + 1) * 2 * E],
                                  ps_slb[b3][:], AF.Identity), "s_a")
        mark("slots_sb", "s_a")

    @blk.vector
    def _(vector):
        wt(vector, "slots_sb")
        iv = slots[:, 0:2 * E * NBLK].rearrange("p (q two) -> p q two", two=2)[:, :, 0:1]
        inc(vector.tensor_copy(idint[:, 0:E * NBLK].unsqueeze(2), iv), "s_v")
        mark("idint", "s_v")

    mark("routing_done", "s_v")
    pc.close()

    pc = ExitStack()
    CAPP = NBLK * 128
    xtok_tm = csb("xtok_tm", [128, NBLK * D], BF16)
    xtok_fm = csb("xtok_fm", [128, 8 * CAPP], BF16)
    wgu = csb("wgu", [128, 2 * 2 * 8 * 128], BF16)
    wdd = csb("wdd", [128, 16 * D], BF16)
    hex_ = csb("hex", [128, 16 * CAPP], BF16)
    erfsc = csb("erfsc", [128, CAPP])
    eo = csb("eo", [128, D])
    ps_e1 = cps("ps_e1", [128, CAPP])
    ps_e2 = cps("ps_e2", [128, CAPP])
    ps_eo = cps("ps_eo", [128, D])
    ps_tr3 = cps("ps_tr3", [128, 128])

    def expert(e):
        tg = "_e%d" % e

        @blk.gpsimd
        def _(gpsimd):
            if e == 0:
                wt(gpsimd, "idint")
                wt(gpsimd, "baseout")
            else:
                wt(gpsimd, "escat_%d" % (e - 1))
            for b3 in range(NBLK):
                inc(gpsimd.indirect_dma_start(
                    xtok_tm[:, b3 * D:(b3 + 1) * D], None, x2n_dram,
                    IndirectOffsetOnAxis(idint[:, b3 * E + e:b3 * E + e + 1], 0)),
                    "s_out", 16)
            mark("egath" + tg, "s_out")

        vb = cnt["s_v"]

        @blk.tensor
        def _(tensor):
            wt(tensor, "egath" + tg)
            if e > 0:
                tensor.wait_ge(sems["s_v"], ev["exv_%d" % (e - 1)][1])
            for i in range(NBLK * 8):
                b3, k = divmod(i, 8)
                if i >= 1:
                    tensor.wait_ge(sems["s_v"], vb + i)
                inc(tensor.transpose(ps_tr3[:].bitcast(BF16)[:, 0:128],
                                     xtok_tm[:, b3 * D + k * 128:b3 * D + (k + 1) * 128],
                                     ident_bf[:]), "s_p")
            mark("etr" + tg, "s_p")

        pbe = ev["etr" + tg][1] - NBLK * 8

        @blk.vector
        def _(vector):
            for i in range(NBLK * 8):
                b3, k = divmod(i, 8)
                vector.wait_ge(sems["s_p"], pbe + i + 1)
                inc(vector.tensor_copy(
                    xtok_fm[:, k * CAPP + b3 * 128:k * CAPP + (b3 + 1) * 128],
                    ps_tr3[:].bitcast(BF16)[:, 0:128]), "s_v")
            mark("exv" + tg, "s_v")

        ev["exv_%d" % e] = ("s_v", cnt["s_v"])

        PGU0 = cnt["s_p"]

        @blk.gpsimd
        def _(gpsimd):
            wt(gpsimd, "egath" + tg)
            for fc in range(16):
                par = fc % 2
                if fc >= 2:
                    gpsimd.wait_ge(sems["s_p"], PGU0 + (fc - 1) * 16)
                gv = wdr["wg_t"][e].rearrange("(k p) f -> p k f", p=128)[
                    :, :, fc * 128:(fc + 1) * 128]
                ov = wgu[:, par * 2048:par * 2048 + 1024].rearrange(
                    "p (k f) -> p k f", f=128)
                inc(gpsimd.dma_start(out=ov, in_=gv), "s_out", 16)
                uv = wdr["wu_t"][e].rearrange("(k p) f -> p k f", p=128)[
                    :, :, fc * 128:(fc + 1) * 128]
                ov2 = wgu[:, par * 2048 + 1024:par * 2048 + 2048].rearrange(
                    "p (k f) -> p k f", f=128)
                inc(gpsimd.dma_start(out=ov2, in_=uv), "s_out", 16)
            mark("eguld" + tg, "s_out")

        gb = cnt["s_out"] - 16 * 2 * 16

        EHSH0 = cnt["s_v"]

        @blk.tensor
        def _(tensor):
            wt(tensor, "exv" + tg)
            for fc in range(16):
                par = fc % 2
                tensor.wait_ge(sems["s_out"], gb + (fc + 1) * 32)
                if fc >= 1:
                    tensor.wait_ge(sems["s_v"], EHSH0 + fc)
                for k in range(8):
                    inc(tensor.matmul(ps_e1[:],
                                      wgu[:, par * 2048 + k * 128:par * 2048 + (k + 1) * 128],
                                      xtok_fm[:, k * CAPP:(k + 1) * CAPP],
                                      start=(k == 0), stop=(k == 7)), "s_p")
                for k in range(8):
                    inc(tensor.matmul(ps_e2[:],
                                      wgu[:, par * 2048 + 1024 + k * 128:par * 2048 + 1024 + (k + 1) * 128],
                                      xtok_fm[:, k * CAPP:(k + 1) * CAPP],
                                      start=(k == 0), stop=(k == 7)), "s_p")
            mark("egu" + tg, "s_p")

        pbe2 = ev["egu" + tg][1] - 16 * 16
        abe = cnt["s_a"]

        @blk.scalar
        def _(scalar):
            for fc in range(16):
                scalar.wait_ge(sems["s_p"], pbe2 + fc * 16 + 8)
                if fc >= 1:
                    scalar.wait_ge(sems["s_v"], EHSH0 + fc)
                inc(scalar.activation(erfsc[:], ps_e1[:], AF.Gelu), "s_a")
            mark("eerf" + tg, "s_a")

        @blk.vector
        def _(vector):
            for fc in range(16):
                vector.wait_ge(sems["s_a"], abe + fc + 1)
                vector.wait_ge(sems["s_p"], pbe2 + (fc + 1) * 16)
                inc(vector.tensor_tensor(hex_[:, fc * CAPP:(fc + 1) * CAPP],
                                         erfsc[:], ps_e2[:], ALU.mult), "s_v")
            mark("ehsh" + tg, "s_v")

        @blk.gpsimd
        def _(gpsimd):
            wt(gpsimd, "eguld" + tg)
            if e > 0:
                gpsimd.wait_ge(sems["s_p"], ev["ewd_%d" % (e - 1)][1])
            dv = wdr["wd_t"][e].rearrange("(kk p) d -> p kk d", p=128)
            ov = wdd[:].rearrange("p (kk d) -> p kk d", d=D)
            inc(gpsimd.dma_start(out=ov, in_=dv), "s_out", 16)
            mark("ewdld" + tg, "s_out")

        EOV0 = cnt["s_v"]

        @blk.tensor
        def _(tensor):
            wt(tensor, "ehsh" + tg)
            wt(tensor, "ewdld" + tg)
            for b3 in range(NBLK):
                if b3 >= 1:
                    tensor.wait_ge(sems["s_v"], EOV0 + b3)
                for h2 in range(2):
                    for fidx in range(16):
                        inc(tensor.matmul(ps_eo[:, h2 * 512:(h2 + 1) * 512],
                                          hex_[:, fidx * CAPP + b3 * 128:fidx * CAPP + (b3 + 1) * 128],
                                          wdd[:, fidx * D + h2 * 512:fidx * D + (h2 + 1) * 512],
                                          start=(fidx == 0), stop=(fidx == 15)), "s_p")
            mark("ewd" + tg, "s_p")
        ev["ewd_%d" % e] = ev["ewd" + tg]

        pbe3 = ev["ewd" + tg][1] - NBLK * 32
        ESC0 = cnt["s_out"]

        @blk.vector
        def _(vector):
            for b3 in range(NBLK):
                vector.wait_ge(sems["s_p"], pbe3 + (b3 + 1) * 32)
                if b3 >= 1:
                    vector.wait_ge(sems["s_out"], ESC0 + b3 * 16)
                wcol = slots[:, 2 * (b3 * E + e) + 1:2 * (b3 * E + e) + 2]
                inc(vector.tensor_scalar(eo[:, 0:D] if NBLK == 1 else
                                         eo[:, 0:D], ps_eo[:], wcol, None, ALU.mult),
                    "s_v")
            mark("eos" + tg, "s_v")

        vbe = ev["eos" + tg][1] - NBLK

        @blk.gpsimd
        def _(gpsimd):
            wt(gpsimd, "ewdld" + tg)
            for b3 in range(NBLK):
                gpsimd.wait_ge(sems["s_v"], vbe + b3 + 1)
                inc(gpsimd.indirect_dma_start(
                    out, IndirectOffsetOnAxis(idint[:, b3 * E + e:b3 * E + e + 1], 0),
                    eo[:, 0:D], None, compute_op=ALU.add), "s_out", 16)
                gpsimd.wait_ge(sems["s_out"], cnt["s_out"])
            mark("escat_%d" % e, "s_out")


    for e in range(E):
        expert(e)

    @blk.sync
    def _(sync):
        wt(sync, "escat_%d" % (E - 1))
        wt(sync, "baseout")
        sync.wait_ge(sems["s_out"], cnt["s_out"])
        sync.wait_ge(sems["s_ld"], cnt["s_ld"])
        if dbg:
            for nm, (src, dst) in dbg_mirror.items():
                inc(sync.dma_start(out=dst, in_=src), "s_ld", 16)
            sync.wait_ge(sems["s_ld"], cnt["s_ld"])
    pc.close()

    return nc



def kernel(**inputs):
    T = int(np.asarray(inputs["L0_attn_x"]).shape[1])
    x = np.ascontiguousarray(np.asarray(inputs["L0_attn_x"], dtype=np.float32))
    w = stage_weights(inputs)
    c = host_consts()
    nc = build(T)
    in_map = {}
    for k, v in w.items():
        in_map[k] = np.ascontiguousarray(v.astype(np.float32))
    for k, v in c.items():
        in_map["c_" + k] = np.ascontiguousarray(v.astype(np.float32))
    in_maps = [dict(in_map, x_tm=np.ascontiguousarray(x[b])) for b in range(8)]
    res = run_bass_kernel_spmd(nc, in_maps, core_ids=list(range(8)))
    return np.stack([res.results[b]["out"] for b in range(8)], axis=0)


# version 07 (final)
# speedup vs baseline: 1.0225x; 1.0225x over previous
import math
import numpy as np
import concourse.bass as bass
import concourse.mybir as mybir
from concourse.bass_utils import run_bass_kernel_spmd
from concourse.bass import IndirectOffsetOnAxis
from contextlib import ExitStack

F32 = mybir.dt.float32
BF16 = mybir.dt.bfloat16
I32 = mybir.dt.int32
AF = mybir.ActivationFunctionType
ALU = mybir.AluOpType
AX = mybir.AxisListType

D = 1024
S = 16
E = 16
FEXP = 2048
FSH = 2048
LN20 = math.log(20.0)
SQ20 = math.sqrt(20.0)
CAVG = 0.01 / 1024.0
LNEPS = 1e-5
DUMP = 600.0
FCONST_VALS = [1e-8, 1e-20, LN20, LNEPS, 1e-30, 1.0, 0.0, 2.0]


def host_consts():
    sp = np.arange(128) // 8
    c8p = np.arange(128) % 8
    c = {}
    m = np.zeros((128, 16), np.float32)
    m[np.arange(128), sp] = 1.0
    c["smask16"] = m
    c["repmask16"] = np.ascontiguousarray(m.T)
    m8 = np.zeros((8, 128), np.float32)
    m8[c8p, np.arange(128)] = 1.0
    c["mask8"] = m8
    mc = np.zeros((128, 8), np.float32)
    mc[np.arange(128), c8p] = 1.0
    c["c8maskT"] = mc
    for j, (nm, sg) in enumerate([("bre", 1.0), ("bim", 1.0), ("cre", 1.0), ("cim", -1.0)]):
        mm = np.zeros((64, 128), np.float32)
        mm[j * 16 + sp, np.arange(128)] = sg
        c["bcm_" + nm] = mm
    ph = np.linspace(0.0, 2 * math.pi * (1 - 1.0 / S), S)
    c["d0"] = (0.01 * np.stack([np.cos(ph), np.sin(ph)], -1)).astype(np.float32)
    ep = np.zeros((16, 2), np.float32)
    ep[:, 0] = 1e-10
    c["epsC"] = ep
    c["ident"] = np.eye(128, dtype=np.float32)
    c["iota512"] = np.ascontiguousarray(
        np.broadcast_to(np.arange(512, dtype=np.float32), (128, 512)))
    c["iotacol"] = np.arange(128, dtype=np.float32).reshape(128, 1)
    c["tri128"] = np.ascontiguousarray(np.triu(np.ones((128, 128), np.float32)))
    c["onesrow"] = np.ones((1, 128), np.float32)
    ee = np.arange(128) // 8
    tc = np.arange(128) % 8
    trig = ((ee[:, None] == ee[None, :]) & (tc[:, None] < tc[None, :])).astype(np.float32)
    c["trig8"] = np.ascontiguousarray(trig)
    fv = np.zeros((128, 8), np.float32)
    for i, v in enumerate(FCONST_VALS):
        fv[:, i] = v
    c["fvals"] = fv
    c["clim"] = np.ascontiguousarray(
        np.broadcast_to(np.array([SQ20, 20.0], np.float32), (16, 2)))
    return c


CONST_SHAPES = {"smask16": [128, 16], "repmask16": [16, 128], "mask8": [8, 128],
                "c8maskT": [128, 8], "bcm_bre": [64, 128], "bcm_bim": [64, 128],
                "bcm_cre": [64, 128], "bcm_cim": [64, 128], "d0": [16, 2],
                "epsC": [16, 2], "ident": [128, 128], "iota512": [128, 512],
                "iotacol": [128, 1], "tri128": [128, 128], "onesrow": [1, 128],
                "clim": [16, 2], "trig8": [128, 128], "fvals": [128, 8]}

WEIGHT_SHAPES = {"w_in_t": [D, 2 * D], "w_xp_t": [D, 128], "w_dt_fm": [65, D],
                 "w_res_t": [D, 64], "w_out_aug": [D + 1, D], "gate_t": [D, E],
                 "w_sg_t": [D, FSH], "w_su_t": [D, FSH], "w_sd_t": [FSH, D],
                 "wg_t": [E, D, FEXP], "wu_t": [E, D, FEXP], "wd_t": [E, FEXP, D],
                 "a_log": [1, 16]}


def stage_weights(inp):
    w = {}
    w["w_in_t"] = np.ascontiguousarray(np.asarray(inp["L0_attn_W_in"]).T)
    Wx = np.asarray(inp["L0_attn_W_xproj"])
    perm = list(range(64)) + [64 + 2 * s for s in range(16)] + \
        [64 + 2 * s + 1 for s in range(16)] + [96 + 2 * s for s in range(16)] + \
        [96 + 2 * s + 1 for s in range(16)]
    w["w_xp_t"] = np.ascontiguousarray(Wx[perm].T)
    Wdt = np.asarray(inp["L0_attn_W_dt"])
    bdt = np.asarray(inp["L0_attn_b_dt"])
    w["w_dt_fm"] = np.ascontiguousarray(np.vstack([Wdt.T, bdt[None, :]]))
    Wres = np.asarray(inp["L0_attn_W_res"])
    wr = np.zeros((1024, 64), np.float32)
    wr[:, 0:16] = Wres[0::2].T
    wr[:, 32:48] = Wres[1::2].T
    w["w_res_t"] = wr
    w["w_out_aug"] = np.ascontiguousarray(
        np.vstack([np.asarray(inp["L0_attn_W_out"]).T,
                   np.asarray(inp["L0_attn_b_out"])[None, :]]))
    w["gate_t"] = np.ascontiguousarray(np.asarray(inp["L0_ffn_gate_w"]).T)
    w["w_sg_t"] = np.ascontiguousarray(np.asarray(inp["L0_ffn_Ws_g"]).T)
    w["w_su_t"] = np.ascontiguousarray(np.asarray(inp["L0_ffn_Ws_u"]).T)
    w["w_sd_t"] = np.ascontiguousarray(np.asarray(inp["L0_ffn_Ws_d"]).T)
    w["wg_t"] = np.ascontiguousarray(np.asarray(inp["L0_ffn_Wg"]).transpose(0, 2, 1))
    w["wu_t"] = np.ascontiguousarray(np.asarray(inp["L0_ffn_Wu"]).transpose(0, 2, 1))
    w["wd_t"] = np.ascontiguousarray(np.asarray(inp["L0_ffn_Wd"]).transpose(0, 2, 1))
    w["a_log"] = np.ascontiguousarray(np.asarray(inp["L0_attn_A_log"]).reshape(1, 16))
    return w


def build(T, dbg=False):
    TILE = 512
    NT = T // TILE
    TG = TILE // 128
    SUB = 32
    NSUB = T // SUB
    YC = 64
    CAP = T // 8
    NBLK = max(CAP // 128, 1)

    nc = bass.Bass("TRN2", target_bir_lowering=False, debug=False,
                   detect_race_conditions=False)
    es = ExitStack()

    x_tm = nc.dram_tensor("x_tm", [T, D], F32, kind="ExternalInput").ap()
    wdr = {k: nc.dram_tensor(k, v, F32, kind="ExternalInput").ap()
           for k, v in WEIGHT_SHAPES.items()}
    cdr = {k: nc.dram_tensor("c_" + k, v, F32, kind="ExternalInput").ap()
           for k, v in CONST_SHAPES.items()}
    out = nc.dram_tensor("out", [T, D], F32, kind="ExternalOutput").ap()

    dt_dram = nc.dram_tensor("dt_dram", [D, T], F32, kind="Internal").ap()
    dtx_dram = nc.dram_tensor("dtx_dram", [D, T], F32, kind="Internal").ap()
    zsil_dram = nc.dram_tensor("zsil_dram", [D, T], F32, kind="Internal").ap()
    bc_dram = nc.dram_tensor("bc_dram", [512, T], F32, kind="Internal").ap()
    xvn_dram = nc.dram_tensor("xvn_dram", [16, 2 * T], F32, kind="Internal").ap()
    y_dram = nc.dram_tensor("y_dram", [128, 8 * T], F32, kind="Internal").ap()
    x1_dram = nc.dram_tensor("x1_dram", [T, D], F32, kind="Internal").ap()
    x2n_dram = nc.dram_tensor("x2n_dram", [T, D], BF16, kind="Internal").ap()
    scem_dram = nc.dram_tensor("scem_dram", [E, T], F32, kind="Internal").ap()
    rkm_dram = nc.dram_tensor("rkm_dram", [E * 8, T // 8], F32, kind="Internal").ap()
    a_dram = nc.dram_tensor("a_dram", [16, 1], F32, kind="Internal").ap()
    dbg_mirror = {}
    if dbg:
        for nm, src in [("dt_dram", dt_dram), ("dtx_dram", dtx_dram),
                        ("zsil_dram", zsil_dram), ("bc_dram", bc_dram),
                        ("xvn_dram", xvn_dram), ("y_dram", y_dram),
                        ("x1_dram", x1_dram), ("scem_dram", scem_dram),
                        ("a_dram", a_dram)]:
            dbg_mirror[nm] = (src, nc.dram_tensor(
                "dbg_" + nm, list(src.tensor.shape), F32, kind="ExternalOutput").ap())
        dbg_mirror["x2n_dram"] = (x2n_dram, nc.dram_tensor(
            "dbg_x2n_dram", [T, D], BF16, kind="ExternalOutput").ap())
    if dbg:
        xn_dbg = nc.dram_tensor("dbg_xn", [T, D], BF16, kind="ExternalOutput").ap()
        xnfm_dbg = nc.dram_tensor("dbg_xnfm", [128, (T // 512) * 4096], BF16,
                                  kind="ExternalOutput").ap()
        stat_dbg = nc.dram_tensor("dbg_stat", [128, 12], F32, kind="ExternalOutput").ap()

    sems = {}
    cnt = {}
    for nm in ["s_ld", "s_gld", "s_v", "s_a", "s_g", "s_p", "s_pb", "s_al",
               "s_ye", "s_out", "s_sc"]:
        sems[nm] = es.enter_context(nc.semaphore(nm))
        cnt[nm] = 0

    nc._dbg_sems = {k: v for k, v in sems.items()}

    def inc(inst, nm, n=1):
        inst.then_inc(sems[nm], n)
        cnt[nm] += n
        return cnt[nm]

    ev = {}

    def mark(name, nm):
        ev[name] = (nm, cnt[nm])

    def wt(eng, name):
        nm, c = ev[name]
        if c > 0:
            eng.wait_ge(sems[nm], c)

    def sb(name, shape, dt=F32):
        return es.enter_context(nc.sbuf_tensor(name, shape, dt))

    cs = {k: sb("cs_" + k, v) for k, v in CONST_SHAPES.items()}
    for i, v in enumerate(FCONST_VALS):
        nc.const_aps.register(F32, v, cs["fvals"][:, i:i + 1])
    ident_bf = sb("identbf", [128, 128], BF16)
    acol = sb("acol", [128, 1])
    arow = sb("arow", [1, 16])
    w_xp_sb = sb("w_xp_sb", [128, 8 * 128], BF16)
    w_dt_sb = sb("w_dt_sb", [65, D], BF16)
    w_res_sb = sb("w_res_sb", [128, 8 * 64], BF16)
    gate_sb = sb("gate_sb", [128, 8 * E], BF16)
    w_out_sb = sb("w_out_sb", [128, 8 * 1024], BF16)
    b_out_sb = sb("b_out_sb", [1, 1024], BF16)
    sco_tm = sb("sco_tm", [128, (T // 128) * E])
    slots = sb("slots", [128, 128])
    idint = sb("idint", [128, 64], I32)
    ones_bf = sb("ones_bf", [1, 128], BF16)

    blk = es.enter_context(nc.Block())

    @blk.sync
    def _(sync):
        for k in CONST_SHAPES:
            inc(sync.dma_start(out=cs[k][:], in_=cdr[k]), "s_ld", 16)
        inc(sync.dma_start(out=arow[:], in_=wdr["a_log"]), "s_ld", 16)
        mark("consts", "s_ld")

    @blk.gpsimd
    def _(gpsimd):
        wt(gpsimd, "consts")
        inc(gpsimd.dma_start(out=ident_bf[:], in_=cdr["ident"]), "s_gld", 16)
        inc(gpsimd.dma_start(out=w_dt_sb[:], in_=wdr["w_dt_fm"]), "s_gld", 16)
        inc(gpsimd.dma_start(out=b_out_sb[:], in_=wdr["w_out_aug"][D:D + 1, :]), "s_gld", 16)
        for k in range(8):
            kk = slice(k * 128, (k + 1) * 128)
            inc(gpsimd.dma_start(out=w_xp_sb[:, k * 128:(k + 1) * 128],
                                 in_=wdr["w_xp_t"][kk, :]), "s_gld", 16)
            inc(gpsimd.dma_start(out=w_res_sb[:, k * 64:(k + 1) * 64],
                                 in_=wdr["w_res_t"][kk, :]), "s_gld", 16)
            inc(gpsimd.dma_start(out=gate_sb[:, k * E:(k + 1) * E],
                                 in_=wdr["gate_t"][kk, :]), "s_gld", 16)
            inc(gpsimd.dma_start(out=w_out_sb[:, k * 1024:(k + 1) * 1024],
                                 in_=wdr["w_out_aug"][kk, :]), "s_gld", 16)
        mark("resw", "s_gld")

    @blk.scalar
    def _(scalar):
        wt(scalar, "consts")
        scalar.activation(arow[:], arow[:], AF.Exp)
        inc(scalar.mul(arow[:], arow[:], -1.0), "s_a")
        mark("arow", "s_a")

    @blk.sync
    def _(sync):
        wt(sync, "arow")
        inc(sync.dma_start(out=a_dram.rearrange("s one -> one s"), in_=arow[:]), "s_ld", 16)
        sync.wait_ge(sems["s_ld"], cnt["s_ld"])
        for s in range(16):
            inc(sync.dma_start(out=acol[8 * s:8 * (s + 1), 0:1],
                               in_=a_dram[s:s + 1, 0:1].to_broadcast([8, 1])), "s_ld", 16)
        mark("acol", "s_ld")

    pa = ExitStack()

    def asb(name, shape, dt=F32):
        return pa.enter_context(nc.sbuf_tensor(name, shape, dt))

    def aps(name, shape, dt=F32):
        return pa.enter_context(nc.psum_tensor(name, shape, dt))

    w_in_sb = asb("w_in_sb", [128, 8 * 2048], BF16)
    xt = asb("xt", [128, TG * D])
    stat = asb("stat", [128, 3 * TG])
    sqs = asb("sqs", [128, D])
    xn_tm = asb("xn_tm", [128, TG * D], BF16)
    xn_fm = asb("xn_fm", [128, 8 * TILE], BF16)
    xin_fm32 = asb("xin_fm32", [128, 8 * TILE])
    xin_fm = asb("xin_fm", [128, 8 * TILE], BF16)
    zs_fm = asb("zs_fm", [128, 8 * TILE])
    dtr_aug = asb("dtr_aug", [128, TILE], BF16)
    bcsb = asb("bcsb", [64, TILE])
    bccol_s = asb("bccol_s", [128, 4 * TILE])
    dt_f = asb("dt_f", [128, 8 * TILE])
    spa = asb("spa", [128, 2 * TILE])
    spb = asb("spb", [128, 2 * TILE])
    dtx_f = asb("dtx_f", [128, 8 * TILE])
    xvt = asb("xvt", [16, 2 * TILE])
    xvsc = asb("xvsc", [16, 2 * TILE])
    ps_a = aps("ps_a", [128, 512])
    ps_b = aps("ps_b", [128, 512])
    ps_c = aps("ps_c", [128, 512])
    ps_xc = aps("ps_xc", [64, 512])

    @blk.gpsimd
    def _(gpsimd):
        wt(gpsimd, "resw")
        for k in range(8):
            kk = slice(k * 128, (k + 1) * 128)
            inc(gpsimd.dma_start(out=w_in_sb[:, k * 2048:(k + 1) * 2048],
                                 in_=wdr["w_in_t"][kk, :]), "s_gld", 16)
        mark("winld", "s_gld")

    def phase_a(tile):
        t0 = tile * TILE
        tg = "_%d" % tile

        @blk.sync
        def _(sync):
            if tile == 0:
                wt(sync, "acol")
            else:
                wt(sync, "pa" + "_%d" % (tile - 1))
            for g in range(TG):
                inc(sync.dma_start(out=xt[:, g * D:(g + 1) * D],
                                   in_=x_tm[t0 + g * 128:t0 + (g + 1) * 128, :]), "s_ld", 16)
            mark("xld" + tg, "s_ld")

        @blk.vector
        def _(vector):
            wt(vector, "xld" + tg)
            for g in range(TG):
                vector.tensor_scalar(sqs[:, 0:D], xt[:, g * D:(g + 1) * D], 1.0, 0.0,
                                     ALU.mult, ALU.add, accum_out=stat[:, 3 * g:3 * g + 1])
            inc(vector.tensor_copy(sqs[:, 0:1], stat[:, 0:1]), "s_v")
            mark("asx" + tg, "s_v")

        @blk.scalar
        def _(scalar):
            wt(scalar, "xld" + tg)
            wt(scalar, "asx" + tg)
            for g in range(TG):
                scalar.activation(sqs[:, 0:D], xt[:, g * D:(g + 1) * D], AF.Square,
                                  accum_out=stat[:, 3 * g + 1:3 * g + 2])
            inc(scalar.copy(sqs[:, 0:1], stat[:, 0:1]), "s_a")
            mark("asxx" + tg, "s_a")

        @blk.vector
        def _(vector):
            wt(vector, "asxx" + tg)
            for g in range(TG):
                mcol = stat[:, 3 * g:3 * g + 1]
                vcol = stat[:, 3 * g + 1:3 * g + 2]
                m2col = stat[:, 3 * g + 2:3 * g + 3]
                inc(vector.tensor_scalar(mcol, mcol, 1.0 / D, None, ALU.mult), "s_v")
                vector.wait_ge(sems["s_v"], cnt["s_v"])
                inc(vector.tensor_tensor(m2col, mcol, mcol, ALU.mult), "s_v")
                vector.wait_ge(sems["s_v"], cnt["s_v"])
                vector.scalar_tensor_tensor(vcol, vcol, 1.0 / D, m2col, ALU.mult,
                                            ALU.subtract)
            inc(vector.tensor_copy(sqs[:, 0:1], stat[:, 0:1]), "s_v")
            mark("avar" + tg, "s_v")

        @blk.scalar
        def _(scalar):
            wt(scalar, "avar" + tg)
            for g in range(TG):
                vcol = stat[:, 3 * g + 1:3 * g + 2]
                scalar.activation(vcol, vcol, AF.Ln, bias=LNEPS)
                scalar.activation(vcol, vcol, AF.Exp, scale=-0.5)
            inc(scalar.copy(sqs[:, 0:1], stat[:, 0:1]), "s_a")
            mark("arstd" + tg, "s_a")

        @blk.vector
        def _(vector):
            wt(vector, "arstd" + tg)
            for g in range(TG):
                vector.tensor_scalar(xn_tm[:, g * D:(g + 1) * D], xt[:, g * D:(g + 1) * D],
                                     stat[:, 3 * g:3 * g + 1], stat[:, 3 * g + 1:3 * g + 2],
                                     ALU.subtract, ALU.mult)
            inc(vector.tensor_copy(sqs[:, 0:1], stat[:, 0:1]), "s_v")
            mark("axn" + tg, "s_v")

        if dbg:
            @blk.sync
            def _(sync):
                wt(sync, "axn" + tg)
                inc(sync.dma_start(
                    out=xn_dbg[t0:t0 + TILE, :].rearrange("(g p) d -> p g d", p=128),
                    in_=xn_tm[:].rearrange("p (g d) -> p g d", d=D)), "s_ld", 16)
                if tile == 0:
                    inc(sync.dma_start(out=stat_dbg, in_=stat[:]), "s_ld", 16)

        vbase = cnt["s_v"]

        @blk.tensor
        def _(tensor):
            wt(tensor, "axn" + tg)
            if tile == 0:
                wt(tensor, "resw")
            for i in range(32):
                pst = ps_a if i % 2 == 0 else ps_b
                if i >= 2:
                    tensor.wait_ge(sems["s_v"], vbase + i - 1)
                g, k = divmod(i, 8)
                inc(tensor.transpose(pst[:].bitcast(BF16)[:, 0:128],
                                     xn_tm[:, g * D + k * 128:g * D + (k + 1) * 128],
                                     ident_bf[:]), "s_p")
            mark("atr" + tg, "s_p")

        pbase = ev["atr" + tg][1] - 32

        @blk.vector
        def _(vector):
            for i in range(32):
                pst = ps_a if i % 2 == 0 else ps_b
                vector.wait_ge(sems["s_p"], pbase + i + 1)
                g, k = divmod(i, 8)
                inc(vector.tensor_copy(xn_fm[:, k * TILE + g * 128:k * TILE + (g + 1) * 128],
                                       pst[:].bitcast(BF16)[:, 0:128]), "s_v")
            mark("axnfm" + tg, "s_v")

        if dbg:
            @blk.sync
            def _(sync):
                wt(sync, "axnfm" + tg)
                inc(sync.dma_start(out=xnfm_dbg[:, tile * 4096:(tile + 1) * 4096],
                                   in_=xn_fm[:]), "s_ld", 16)

        abase = cnt["s_a"]

        @blk.tensor
        def _(tensor):
            wt(tensor, "axnfm" + tg)
            if tile == 0:
                wt(tensor, "winld")
            for o in range(16):
                pst = ps_a if o % 2 == 0 else ps_b
                if o >= 2:
                    tensor.wait_ge(sems["s_a"], abase + o - 1)
                for k in range(8):
                    inc(tensor.matmul(pst[:],
                                      w_in_sb[:, k * 2048 + o * 128:k * 2048 + (o + 1) * 128],
                                      xn_fm[:, k * TILE:(k + 1) * TILE],
                                      start=(k == 0), stop=(k == 7)), "s_p")
            mark("axz" + tg, "s_p")

        pbase2 = ev["axz" + tg][1] - 128

        @blk.scalar
        def _(scalar):
            for o in range(16):
                pst = ps_a if o % 2 == 0 else ps_b
                scalar.wait_ge(sems["s_p"], pbase2 + (o + 1) * 8)
                if o < 8:
                    inc(scalar.activation(xin_fm32[:, o * TILE:(o + 1) * TILE], pst[:],
                                          AF.Silu), "s_a")
                else:
                    inc(scalar.activation(zs_fm[:, (o - 8) * TILE:(o - 7) * TILE], pst[:],
                                          AF.Silu), "s_a")
            mark("asilu" + tg, "s_a")

        @blk.vector
        def _(vector):
            wt(vector, "asilu" + tg)
            for o in range(8):
                inc(vector.tensor_copy(xin_fm[:, o * TILE:(o + 1) * TILE],
                                       xin_fm32[:, o * TILE:(o + 1) * TILE]), "s_v")
            inc(vector.memset(dtr_aug[64:65, :], 1.0), "s_v")
            mark("axinbf" + tg, "s_v")

        @blk.sync
        def _(sync):
            wt(sync, "asilu" + tg)
            for o in range(8):
                inc(sync.dma_start(out=zsil_dram[o * 128:(o + 1) * 128, t0:t0 + TILE],
                                   in_=zs_fm[:, o * TILE:(o + 1) * TILE]), "s_ld", 16)
            mark("azsil" + tg, "s_ld")

        @blk.tensor
        def _(tensor):
            wt(tensor, "axinbf" + tg)
            for k in range(8):
                inc(tensor.matmul(ps_c[:], w_xp_sb[:, k * 128:(k + 1) * 128],
                                  xin_fm[:, k * TILE:(k + 1) * TILE],
                                  start=(k == 0), stop=(k == 7)), "s_p")
            mark("axp" + tg, "s_p")

        @blk.scalar
        def _(scalar):
            wt(scalar, "axp" + tg)
            inc(scalar.activation(dtr_aug[0:64, :].bitcast(BF16), ps_c[0:64, :],
                                  AF.Identity), "s_a")
            inc(scalar.activation(bcsb[:], ps_c[64:128, :], AF.Identity), "s_a")
            mark("adtr" + tg, "s_a")

        ABCV0 = cnt["s_v"]

        @blk.tensor
        def _(tensor):
            wt(tensor, "adtr" + tg)
            for j, nm in enumerate(["bre", "bim", "cre", "cim"]):
                if j >= 1:
                    tensor.wait_ge(sems["s_v"], ABCV0 + j)
                inc(tensor.matmul(ps_c[:], cs["bcm_" + nm][:], bcsb[:],
                                  start=True, stop=True), "s_p")
            mark("abcmm" + tg, "s_p")

        pbase3 = ev["abcmm" + tg][1] - 4

        @blk.vector
        def _(vector):
            for j in range(4):
                vector.wait_ge(sems["s_p"], pbase3 + j + 1)
                inc(vector.tensor_copy(bccol_s[:, j * TILE:(j + 1) * TILE], ps_c[:]), "s_v")
            mark("abcv" + tg, "s_v")

        @blk.sync
        def _(sync):
            wt(sync, "abcv" + tg)
            for j in range(4):
                inc(sync.dma_start(out=bc_dram[j * 128:(j + 1) * 128, t0:t0 + TILE],
                                   in_=bccol_s[:, j * TILE:(j + 1) * TILE]), "s_ld", 16)
            mark("abcst" + tg, "s_ld")

        VDT0 = cnt["s_v"]
        abase2 = cnt["s_a"]

        @blk.tensor
        def _(tensor):
            wt(tensor, "adtr" + tg)
            wt(tensor, "abcv" + tg)
            for c in range(8):
                pst = ps_a if c % 2 == 0 else ps_b
                if c >= 2:
                    tensor.wait_ge(sems["s_v"], VDT0 + 4 * (c - 2) + 2)
                inc(tensor.matmul(pst[:], w_dt_sb[:, c * 128:(c + 1) * 128],
                                  dtr_aug[0:65, :], start=True, stop=True), "s_p")
            mark("adtmm" + tg, "s_p")

        pbase4 = ev["adtmm" + tg][1] - 8

        @blk.scalar
        def _(scalar):
            for c in range(8):
                par = c % 2
                scalar.wait_ge(sems["s_v"], VDT0 + 4 * c + 2)
                if c >= 2:
                    scalar.wait_ge(sems["s_v"], VDT0 + 4 * (c - 2) + 3)
                scalar.activation(spb[:, par * TILE:(par + 1) * TILE],
                                  spa[:, par * TILE:(par + 1) * TILE], AF.Exp)
                inc(scalar.activation(spb[:, par * TILE:(par + 1) * TILE],
                                      spb[:, par * TILE:(par + 1) * TILE],
                                      AF.Ln, bias=1.0), "s_a")
            mark("adtsp" + tg, "s_a")

        @blk.vector
        def _(vector):
            for c in range(8):
                par = c % 2
                pst = ps_a if c % 2 == 0 else ps_b
                vector.wait_ge(sems["s_p"], pbase4 + c + 1)
                inc(vector.tensor_scalar(dt_f[:, c * TILE:(c + 1) * TILE], pst[:],
                                         0.0, None, ALU.max), "s_v")
                vector.wait_ge(sems["s_v"], cnt["s_v"])
                inc(vector.scalar_tensor_tensor(spa[:, par * TILE:(par + 1) * TILE],
                                                dt_f[:, c * TILE:(c + 1) * TILE],
                                                -2.0, pst[:],
                                                ALU.mult, ALU.add), "s_v")
                vector.wait_ge(sems["s_a"], abase2 + c + 1)
                dtc = dt_f[:, c * TILE:(c + 1) * TILE]
                inc(vector.tensor_tensor(dtc, dtc, spb[:, par * TILE:(par + 1) * TILE],
                                         ALU.add), "s_v")
                inc(vector.tensor_tensor(dtx_f[:, c * TILE:(c + 1) * TILE], dtc,
                                         xin_fm32[:, c * TILE:(c + 1) * TILE], ALU.mult), "s_v")
            mark("adtx" + tg, "s_v")

        @blk.sync
        def _(sync):
            wt(sync, "adtx" + tg)
            for c in range(8):
                inc(sync.dma_start(out=dt_dram[c * 128:(c + 1) * 128, t0:t0 + TILE],
                                   in_=dt_f[:, c * TILE:(c + 1) * TILE]), "s_ld", 16)
                inc(sync.dma_start(out=dtx_dram[c * 128:(c + 1) * 128, t0:t0 + TILE],
                                   in_=dtx_f[:, c * TILE:(c + 1) * TILE]), "s_ld", 16)
            mark("adtst" + tg, "s_ld")

        @blk.tensor
        def _(tensor):
            wt(tensor, "axinbf" + tg)
            wt(tensor, "adtmm" + tg)
            for k in range(8):
                inc(tensor.matmul(ps_xc[:], w_res_sb[:, k * 64:(k + 1) * 64],
                                  xin_fm[:, k * TILE:(k + 1) * TILE],
                                  start=(k == 0), stop=(k == 7)), "s_p")
            mark("axc" + tg, "s_p")

        @blk.vector
        def _(vector):
            wt(vector, "axc" + tg)
            xv_re = xvt[:].rearrange("p (t r) -> p t r", r=2)[:, :, 0:1]
            xv_im = xvt[:].rearrange("p (t r) -> p t r", r=2)[:, :, 1:2]
            inc(vector.tensor_scalar(xv_re, ps_xc[0:16, :].unsqueeze(2),
                                     1e-10, None, ALU.add), "s_v")
            inc(vector.tensor_copy(xv_im, ps_xc[32:48, :].unsqueeze(2)), "s_v")
            mark("axv1" + tg, "s_v")

        @blk.scalar
        def _(scalar):
            wt(scalar, "axv1" + tg)
            inc(scalar.activation(xvsc[:], xvt[:], AF.Square), "s_a")
            mark("axv2" + tg, "s_a")

        @blk.vector
        def _(vector):
            wt(vector, "axv2" + tg)
            inc(vector.tensor_reduce(xvsc[:, 0:TILE],
                                     xvsc[:].rearrange("p (t r) -> p t r", r=2),
                                     AX.X, ALU.add), "s_v")
            mark("axv3" + tg, "s_v")

        @blk.scalar
        def _(scalar):
            wt(scalar, "axv3" + tg)
            scalar.activation(xvsc[:, 0:TILE], xvsc[:, 0:TILE], AF.Ln, bias=1e-30)
            inc(scalar.activation(xvsc[:, 0:TILE], xvsc[:, 0:TILE], AF.Exp, scale=-0.5), "s_a")
            mark("axv4" + tg, "s_a")

        @blk.vector
        def _(vector):
            wt(vector, "axv4" + tg)
            xv_re = xvt[:].rearrange("p (t r) -> p t r", r=2)[:, :, 0:1]
            xv_im = xvt[:].rearrange("p (t r) -> p t r", r=2)[:, :, 1:2]
            rx = xvsc[:, 0:TILE].unsqueeze(2)
            inc(vector.tensor_tensor(xv_re, xv_re, rx, ALU.mult), "s_v")
            inc(vector.tensor_tensor(xv_im, xv_im, rx, ALU.mult), "s_v")
            mark("axv5" + tg, "s_v")

        @blk.sync
        def _(sync):
            wt(sync, "axv5" + tg)
            inc(sync.dma_start(out=xvn_dram[:, 2 * t0:2 * (t0 + TILE)], in_=xvt[:]), "s_ld", 16)
            mark("pa" + tg, "s_ld")

    for tile in range(NT):
        phase_a(tile)
    mark("phaseA", "s_ld")
    pa.close()

    pb_scope = ExitStack()

    def bsb(name, shape, dt=F32):
        return pb_scope.enter_context(nc.sbuf_tensor(name, shape, dt))

    def bps(name, shape, dt=F32):
        return pb_scope.enter_context(nc.psum_tensor(name, shape, dt))

    h2 = bsb("h2", [128, 512])
    alph = bsb("alph", [128, 2 * 512])
    relt = bsb("relt", [128, 256])
    sqt = bsb("sqt", [128, 256])
    Mt = bsb("Mt", [128, 128])
    ddt = bsb("ddt", [128, 128])
    lnt = bsb("lnt", [128, 128])
    ldt = bsb("ldt", [128, 128])
    t3t = bsb("t3t", [128, 128])
    rt = bsb("rt", [128, 128])
    compt = bsb("compt", [128, 256])
    clipt = bsb("clipt", [128, 256])
    davg = bsb("davg", [16, 2])
    dtiny = bsb("dtiny", [16, 32])
    surf = bsb("surf", [16, 2])
    yrhs = bsb("yrhs", [128, 16])
    bcc = bsb("bcc", [128, 2 * 4 * SUB])
    xvns = bsb("xvns", [16, 2 * 2 * SUB])
    slab_t = bsb("slab_t", [8, 2 * SUB * 128])
    slab_x = bsb("slab_x", [8, 2 * SUB * 128])
    ysb = bsb("ysb", [128, 2 * 512])

    ps_bc2 = [bps("ps_bc2_%d" % i, [128, 1024]) for i in range(2)]
    ps_y = bps("ps_y", [128, 512])
    ps_red = bps("ps_red", [16, 256])
    ps_surf = bps("ps_surf", [128, 2])

    NB = T // 4
    V0 = cnt["s_v"]
    A0 = cnt["s_a"]
    G0 = cnt["s_g"]
    P0 = cnt["s_p"]
    PB0 = cnt["s_pb"]
    AL0 = cnt["s_al"]
    GLD0 = cnt["s_sc"]
    YE0 = cnt["s_ye"]

    OV = {"yrhs": 2, "hdec": 4, "h": 8, "dd": 9, "t3": 10, "clip": 12, "red": 13,
          "ssq": 16, "dd2": 22, "u": 25, "surf": 32}
    V_PER = 32
    OA = {"sq": 1, "lm": 2, "ld": 3, "r": 4, "lnn": 5, "e1": 6, "lnd": 7, "exps": 8}
    A_PER = 8
    OG = {"rel": 1, "M": 2, "cim": 3}
    G_PER = 3
    OP = {"surfb": 1, "y": 3, "smask": 4}
    P_PER = 4
    GLD_PER = 64

    def vv(t, name):
        return V0 + V_PER * t + OV[name]

    def aa(t, name):
        return A0 + A_PER * t + OA[name]

    def gg(t, name):
        return G0 + G_PER * t + OG[name]

    def pp(t, name):
        return P0 + P_PER * t + OP[name]

    def pbn(batch):
        return PB0 + 2 * (batch + 1)

    def aln(batch):
        return AL0 + batch + 1

    def gldn(sub):
        return GLD0 + GLD_PER * (sub + 1)

    OUTY0 = cnt["s_out"]

    def ystn(rnd):
        return OUTY0 + 16 * (rnd + 1)

    def yen(rnd):
        return YE0 + rnd + 1

    def dtb_view(t):
        par = (t // 4) % 2
        return ps_bc2[par][:, (t % 4) * 128:(t % 4) * 128 + 128]

    def dtxb_view(t):
        par = (t // 4) % 2
        return ps_bc2[par][:, 512 + (t % 4) * 128:512 + (t % 4) * 128 + 128]

    def bcc_col(t, j):
        par = (t // SUB) % 2
        return bcc[:, par * 4 * SUB + j * SUB + (t % SUB):par * 4 * SUB + j * SUB + (t % SUB) + 1]

    def xvn_view(t):
        par = (t // SUB) % 2
        o = par * 2 * SUB + 2 * (t % SUB)
        return xvns[:, o:o + 2]

    def h_re(t):
        return h2[:, (t % 2) * 256:(t % 2) * 256 + 128]

    def h_im(t):
        return h2[:, (t % 2) * 256 + 128:(t % 2) * 256 + 256]

    def h_full(t):
        return h2[:, (t % 2) * 256:(t % 2) * 256 + 256]

    @blk.sync
    def _(sync):
        wt(sync, "phaseA")
        inc(sync.dma_start(out=dtiny[:, 0:2], in_=cdr["d0"]), "s_ld", 16)
        mark("d0ld", "s_ld")

    @blk.sync
    def _(sync):
        wt(sync, "phaseA")
        for sub in range(NSUB):
            par = sub % 2
            if sub >= 2:
                sync.wait_ge(sems["s_v"], V0 + V_PER * ((sub - 1) * SUB))
                sync.wait_ge(sems["s_pb"], PB0 + 2 * ((sub - 1) * SUB // 4))
            t0 = sub * SUB
            sv = dt_dram[:, t0:t0 + SUB].rearrange("(k dp) t -> k dp t", k=8)
            ov = slab_t[:, par * SUB * 128:(par + 1) * SUB * 128].rearrange(
                "k (dp t) -> k dp t", t=SUB)
            inc(sync.dma_start(out=ov, in_=sv), "s_sc", 16)
            sv2 = dtx_dram[:, t0:t0 + SUB].rearrange("(k dp) t -> k dp t", k=8)
            ov2 = slab_x[:, par * SUB * 128:(par + 1) * SUB * 128].rearrange(
                "k (dp t) -> k dp t", t=SUB)
            inc(sync.dma_start(out=ov2, in_=sv2), "s_sc", 16)
            bv = bc_dram[:, t0:t0 + SUB].rearrange("(j p) t -> p j t", j=4)
            obv = bcc[:, par * 4 * SUB:(par + 1) * 4 * SUB].rearrange(
                "p (j t) -> p j t", t=SUB)
            inc(sync.dma_start(out=obv, in_=bv), "s_sc", 16)
            inc(sync.dma_start(out=xvns[:, par * 2 * SUB:(par + 1) * 2 * SUB],
                               in_=xvn_dram[:, 2 * t0:2 * (t0 + SUB)]), "s_sc", 16)
        mark("scanloads", "s_sc")

    PRO_V = [0]

    @blk.vector
    def _(vector):
        wt(vector, "d0ld")
        vector.wait_ge(sems["s_sc"], gldn(0))
        vector.memset(h2[:, 256:512], 0.0)
        vector.memset(ones_bf[:], 1.0)
        d0c = dtiny[:, 0:2]
        inc(vector.tensor_tensor(dtiny[:, 4:6], d0c, d0c, ALU.mult), "s_v")
        vector.wait_ge(sems["s_v"], cnt["s_v"])
        inc(vector.tensor_reduce(dtiny[:, 6:7], dtiny[:, 4:6], AX.X, ALU.add), "s_v")
        mark("pro_ssq", "s_v")

    @blk.scalar
    def _(scalar):
        wt(scalar, "pro_ssq")
        inc(scalar.activation(dtiny[:, 7:8], dtiny[:, 6:7], AF.Ln, bias=1e-8), "s_a")
        mark("pro_lnn", "s_a")

    @blk.vector
    def _(vector):
        wt(vector, "pro_lnn")
        inc(vector.tensor_tensor(dtiny[:, 10:12], dtiny[:, 0:2], cs["epsC"][:],
                                 ALU.add), "s_v")
        vector.wait_ge(sems["s_v"], cnt["s_v"])
        inc(vector.tensor_tensor(dtiny[:, 12:14], dtiny[:, 10:12], dtiny[:, 10:12],
                                 ALU.mult), "s_v")
        inc(vector.tensor_tensor(dtiny[:, 14:16], xvns[:, 0:2], dtiny[:, 10:12],
                                 ALU.mult), "s_v")
        vector.wait_ge(sems["s_v"], cnt["s_v"])
        inc(vector.tensor_reduce(dtiny[:, 16:18], dtiny[:, 12:16].rearrange(
            "p (a b) -> p a b", a=2), AX.X, ALU.add), "s_v")
        inc(vector.tensor_copy(dtiny[:, 2:3], dtiny[:, 0:1]), "s_v")
        mark("pro_dd2", "s_v")

    @blk.scalar
    def _(scalar):
        wt(scalar, "pro_dd2")
        inc(scalar.activation(dtiny[:, 18:19], dtiny[:, 16:17], AF.Ln, bias=1e-20), "s_a")
        mark("pro_lnd", "s_a")

    @blk.vector
    def _(vector):
        wt(vector, "pro_lnd")
        vector.tensor_scalar(dtiny[:, 19:20], dtiny[:, 18:19], -0.5, None, ALU.mult)
        vector.tensor_scalar(dtiny[:, 20:21], dtiny[:, 7:8], 0.25, None, ALU.mult)
        inc(vector.tensor_scalar(dtiny[:, 21:22], dtiny[:, 7:8], 0.5, None, ALU.mult), "s_v")
        mark("pro_u", "s_v")

    @blk.scalar
    def _(scalar):
        wt(scalar, "pro_u")
        inc(scalar.activation(dtiny[:, 22:25], dtiny[:, 19:22], AF.Exp), "s_a")
        mark("pro_exps", "s_a")

    @blk.vector
    def _(vector):
        wt(vector, "pro_exps")
        inc(vector.tensor_tensor(dtiny[:, 25:27], dtiny[:, 23:25], cs["clim"][:],
                                 ALU.min), "s_v")
        vector.wait_ge(sems["s_v"], cnt["s_v"])
        inc(vector.tensor_scalar(dtiny[:, 27:28], dtiny[:, 26:27], 1e-8, None,
                                 ALU.add), "s_v")
        vector.wait_ge(sems["s_v"], cnt["s_v"])
        inc(vector.reciprocal(dtiny[:, 28:29], dtiny[:, 27:28]), "s_v")
        inc(vector.scalar_tensor_tensor(dtiny[:, 29:30], dtiny[:, 17:18], 0.5,
                                        dtiny[:, 22:23], ALU.mult, ALU.mult), "s_v")
        vector.wait_ge(sems["s_v"], cnt["s_v"])
        inc(vector.scalar_tensor_tensor(dtiny[:, 30:31], dtiny[:, 29:30], 0.5,
                                        dtiny[:, 25:26], ALU.add, ALU.mult), "s_v")
        vector.wait_ge(sems["s_v"], cnt["s_v"])
        inc(vector.tensor_tensor(dtiny[:, 31:32], dtiny[:, 30:31],
                                 dtiny[:, 28:29], ALU.mult), "s_v")
        vector.wait_ge(sems["s_v"], cnt["s_v"])
        inc(vector.tensor_scalar(surf[:], dtiny[:, 0:2], dtiny[:, 31:32], None,
                                 ALU.mult), "s_v")
        mark("pro_surf", "s_v")

    PROSURF = ev["pro_surf"][1]

    V0 = cnt["s_v"]
    A0 = cnt["s_a"]
    G0 = cnt["s_g"]
    P0 = cnt["s_p"]

    @blk.tensor
    def _(tensor):
        tensor.wait_ge(sems["s_v"], PROSURF)
        for t in range(T):
            if t % 4 == 0:
                batch = t // 4
                par = batch % 2
                tensor.wait_ge(sems["s_sc"], gldn(t // SUB))
                if batch >= 2:
                    tensor.wait_ge(sems["s_al"], aln(batch - 2))
                    tensor.wait_ge(sems["s_v"], vv(4 * (batch - 1) - 1, "h"))
                spar = (t // SUB) % 2
                svt = slab_t[:, spar * SUB * 128:(spar + 1) * SUB * 128].rearrange(
                    "k (dp tt) -> k dp tt", tt=SUB)[:, :, (t % SUB):(t % SUB) + 4]
                svx = slab_x[:, spar * SUB * 128:(spar + 1) * SUB * 128].rearrange(
                    "k (dp tt) -> k dp tt", tt=SUB)[:, :, (t % SUB):(t % SUB) + 4]
                svt = svt.transpose([0, 2, 1])
                svx = svx.transpose([0, 2, 1])
                inc(tensor.matmul(ps_bc2[par][:, 0:512], cs["mask8"][:], svt,
                                  start=True, stop=True), "s_pb")
                inc(tensor.matmul(ps_bc2[par][:, 512:1024], cs["mask8"][:], svx,
                                  start=True, stop=True), "s_pb")
            if t > 0:
                tensor.wait_ge(sems["s_v"], vv(t - 1, "surf"))
            inc(tensor.matmul(ps_surf[:], cs["repmask16"][:], surf[:],
                              start=True, stop=True), "s_p")
            if t % YC == 0 and t // YC >= 1:
                tensor.wait_ge(sems["s_ye"], yen(t // YC - 1))
            tensor.wait_ge(sems["s_v"], vv(t, "h"))
            yslot = ps_y[:, (t % YC) * 8:(t % YC) * 8 + 8]
            inc(tensor.matmul(yslot, h_re(t), yrhs[:, 0:8], start=True, stop=False), "s_p")
            inc(tensor.matmul(yslot, h_im(t), yrhs[:, 8:16], start=False, stop=True), "s_p")
            tensor.wait_ge(sems["s_v"], vv(t, "clip"))
            inc(tensor.matmul(ps_red[:], cs["smask16"][:], clipt[:],
                              start=True, stop=True), "s_p")
        mark("pe_scan", "s_p")

    @blk.scalar
    def _(scalar):
        scalar.wait_ge(sems["s_v"], PROSURF)
        for t in range(T):
            if t % 4 == 0:
                batch = t // 4
                par = batch % 2
                scalar.wait_ge(sems["s_pb"], pbn(batch))
                if batch >= 2:
                    scalar.wait_ge(sems["s_v"], vv(4 * (batch - 1) - 1, "hdec"))
                inc(scalar.activation(alph[:, par * 512:(par + 1) * 512],
                                      ps_bc2[par][:, 0:512], AF.Exp,
                                      scale=acol[:, 0:1]), "s_al")
            scalar.wait_ge(sems["s_g"], gg(t, "rel"))
            if t > 0:
                scalar.wait_ge(sems["s_g"], gg(t - 1, "M"))
            inc(scalar.activation(sqt[:], relt[:], AF.Square), "s_a")
            scalar.wait_ge(sems["s_g"], gg(t, "M"))
            if t > 0:
                scalar.wait_ge(sems["s_v"], vv(t - 1, "t3"))
            inc(scalar.activation(lnt[:], Mt[:], AF.Ln, bias=1e-8), "s_a")
            scalar.wait_ge(sems["s_v"], vv(t, "dd"))
            inc(scalar.activation(ldt[:], ddt[:], AF.Ln, bias=1e-20), "s_a")
            scalar.wait_ge(sems["s_v"], vv(t, "t3"))
            if t > 0:
                scalar.wait_ge(sems["s_v"], vv(t - 1, "clip"))
                scalar.wait_ge(sems["s_g"], gg(t - 1, "cim"))
            inc(scalar.activation(rt[:], t3t[:], AF.Exp, scale=1.0 / 3.0), "s_a")
            scalar.wait_ge(sems["s_v"], vv(t, "ssq"))
            inc(scalar.activation(dtiny[:, 7:8], dtiny[:, 6:7], AF.Ln, bias=1e-8), "s_a")
            inc(scalar.activation(dtiny[:, 8:9], dtiny[:, 7:8], AF.Exp,
                                  scale=-0.5, bias=LN20), "s_a")
            scalar.wait_ge(sems["s_v"], vv(t, "dd2"))
            inc(scalar.activation(dtiny[:, 18:19], dtiny[:, 16:17], AF.Ln,
                                  bias=1e-20), "s_a")
            scalar.wait_ge(sems["s_v"], vv(t, "u"))
            inc(scalar.activation(dtiny[:, 22:25], dtiny[:, 19:22], AF.Exp), "s_a")
            if t % YC == YC - 1:
                scalar.wait_ge(sems["s_p"], pp(t, "y"))
                rnd = t // YC
                if rnd >= 2:
                    scalar.wait_ge(sems["s_out"], ystn(rnd - 2))
                inc(scalar.activation(ysb[:, (rnd % 2) * 512:(rnd % 2) * 512 + 512],
                                      ps_y[:], AF.Identity), "s_ye")
        mark("act_scan", "s_a")

    @blk.gpsimd
    def _(gpsimd):
        gpsimd.wait_ge(sems["s_v"], PROSURF)
        for t in range(T):
            gpsimd.wait_ge(sems["s_v"], vv(t, "hdec"))
            if t > 0:
                gpsimd.wait_ge(sems["s_v"], vv(t - 1, "clip"))
                gpsimd.wait_ge(sems["s_a"], aa(t - 1, "sq"))
            inc(gpsimd.tensor_tensor(relt[:], h_full(t + 1), compt[:], ALU.subtract),
                "s_g")
            gpsimd.wait_ge(sems["s_a"], aa(t, "sq"))
            if t > 0:
                gpsimd.wait_ge(sems["s_a"], aa(t - 1, "lm"))
                gpsimd.wait_ge(sems["s_v"], vv(t - 1, "dd"))
            inc(gpsimd.tensor_tensor(Mt[:], sqt[:, 0:128], sqt[:, 128:256], ALU.add),
                "s_g")
            gpsimd.wait_ge(sems["s_a"], aa(t, "r"))
            inc(gpsimd.tensor_tensor(compt[:, 128:256], relt[:, 128:256], rt[:],
                                     ALU.mult), "s_g")
            if t % YC == YC - 1:
                rnd = t // YC
                gpsimd.wait_ge(sems["s_ye"], yen(rnd))
                inc(gpsimd.dma_start(out=y_dram[:, rnd * YC * 8:(rnd + 1) * YC * 8],
                                     in_=ysb[:, (rnd % 2) * 512:(rnd % 2) * 512 + 512]),
                    "s_out", 16)
        mark("pool_scan", "s_g")
        mark("ylast", "s_out")


    @blk.vector
    def _(vector):
        for t in range(T):
            vector.wait_ge(sems["s_sc"], gldn(t // SUB))
            if t > 0:
                vector.wait_ge(sems["s_p"], pp(t - 1, "y"))
            inc(vector.tensor_scalar(yrhs[:, 0:8], cs["c8maskT"][:], bcc_col(t, 2),
                                     None, ALU.mult), "s_v")
            inc(vector.tensor_scalar(yrhs[:, 8:16], cs["c8maskT"][:], bcc_col(t, 3),
                                     None, ALU.mult), "s_v")
            vector.wait_ge(sems["s_al"], aln(t // 4))
            if t > 0:
                vector.wait_ge(sems["s_g"], gg(t - 1, "rel"))
            apar = (t // 4) % 2
            av = alph[:, apar * 512 + (t % 4) * 128:apar * 512 + (t % 4) * 128 + 128]
            inc(vector.tensor_tensor(compt[:, 0:128], h_re(t + 1), av, ALU.mult), "s_v")
            inc(vector.tensor_tensor(compt[:, 128:256], h_im(t + 1), av, ALU.mult), "s_v")
            vector.wait_ge(sems["s_p"], pp(t, "surfb"))
            vector.wait_ge(sems["s_v"], cnt["s_v"])
            inc(vector.scalar_tensor_tensor(h_re(t), dtb_view(t), ps_surf[:, 0:1],
                                            compt[:, 0:128], ALU.mult, ALU.add), "s_v")
            inc(vector.scalar_tensor_tensor(h_im(t), dtb_view(t), ps_surf[:, 1:2],
                                            compt[:, 128:256], ALU.mult, ALU.add), "s_v")
            vector.wait_ge(sems["s_v"], cnt["s_v"])
            inc(vector.scalar_tensor_tensor(h_re(t), dtxb_view(t), bcc_col(t, 0),
                                            h_re(t), ALU.mult, ALU.add), "s_v")
            inc(vector.scalar_tensor_tensor(h_im(t), dtxb_view(t), bcc_col(t, 1),
                                            h_im(t), ALU.mult, ALU.add), "s_v")
            vector.wait_ge(sems["s_g"], gg(t, "M"))
            inc(vector.scalar_tensor_tensor(ddt[:], relt[:, 0:128], 2e-10, Mt[:],
                                            ALU.mult, ALU.add), "s_v")
            vector.wait_ge(sems["s_a"], aa(t, "ld"))
            inc(vector.scalar_tensor_tensor(t3t[:], ldt[:], -1.5, lnt[:],
                                            ALU.mult, ALU.add), "s_v")
            vector.wait_ge(sems["s_a"], aa(t, "r"))
            inc(vector.scalar_tensor_tensor(compt[:, 0:128], relt[:, 0:128], 1e-10,
                                            rt[:], ALU.add, ALU.mult), "s_v")
            vector.wait_ge(sems["s_g"], gg(t, "cim"))
            vector.wait_ge(sems["s_v"], cnt["s_v"])
            inc(vector.tensor_scalar(clipt[:], compt[:], 10.0, -10.0,
                                     ALU.min, ALU.max), "s_v")
            vector.wait_ge(sems["s_p"], pp(t, "smask"))
            inc(vector.tensor_reduce(davg[:], ps_red[:].rearrange(
                "p (r dp) -> p r dp", r=2), AX.X, ALU.add), "s_v")
            if t > 0:
                vector.wait_ge(sems["s_a"], aa(t - 1, "lnn"))
            vector.wait_ge(sems["s_v"], cnt["s_v"])
            inc(vector.scalar_tensor_tensor(dtiny[:, 2:4], davg[:], CAVG,
                                            dtiny[:, 0:2], ALU.mult, ALU.add), "s_v")
            vector.wait_ge(sems["s_v"], cnt["s_v"])
            inc(vector.tensor_tensor(dtiny[:, 4:6], dtiny[:, 2:4], dtiny[:, 2:4],
                                     ALU.mult), "s_v")
            vector.wait_ge(sems["s_v"], cnt["s_v"])
            inc(vector.tensor_reduce(dtiny[:, 6:7], dtiny[:, 4:6], AX.X, ALU.add),
                "s_v")
            vector.wait_ge(sems["s_a"], aa(t, "e1"))
            vector.wait_ge(sems["s_sc"], gldn(min((t + 1) // SUB, NSUB - 1)))
            inc(vector.tensor_scalar(dtiny[:, 9:10], dtiny[:, 8:9], 1.0, None,
                                     ALU.min), "s_v")
            vector.wait_ge(sems["s_v"], cnt["s_v"])
            inc(vector.tensor_scalar(dtiny[:, 0:2], dtiny[:, 2:4], dtiny[:, 9:10],
                                     None, ALU.mult), "s_v")
            vector.wait_ge(sems["s_v"], cnt["s_v"])
            inc(vector.tensor_tensor(dtiny[:, 10:12], dtiny[:, 0:2], cs["epsC"][:],
                                     ALU.add), "s_v")
            vector.wait_ge(sems["s_v"], cnt["s_v"])
            inc(vector.tensor_tensor(dtiny[:, 12:14], dtiny[:, 10:12],
                                     dtiny[:, 10:12], ALU.mult), "s_v")
            inc(vector.tensor_tensor(dtiny[:, 14:16], xvn_view(t + 1),
                                     dtiny[:, 10:12], ALU.mult), "s_v")
            vector.wait_ge(sems["s_v"], cnt["s_v"])
            inc(vector.tensor_reduce(dtiny[:, 16:18], dtiny[:, 12:16].rearrange(
                "p (a b) -> p a b", a=2), AX.X, ALU.add), "s_v")
            vector.wait_ge(sems["s_a"], aa(t, "lnd"))
            inc(vector.tensor_scalar(dtiny[:, 19:20], dtiny[:, 18:19], -0.5, None,
                                     ALU.mult), "s_v")
            inc(vector.tensor_scalar(dtiny[:, 20:21], dtiny[:, 7:8], 0.25, None,
                                     ALU.mult), "s_v")
            inc(vector.tensor_scalar(dtiny[:, 21:22], dtiny[:, 7:8], 0.5, None,
                                     ALU.mult), "s_v")
            vector.wait_ge(sems["s_a"], aa(t, "exps"))
            inc(vector.tensor_tensor(dtiny[:, 25:27], dtiny[:, 23:25], cs["clim"][:],
                                     ALU.min), "s_v")
            vector.wait_ge(sems["s_v"], cnt["s_v"])
            inc(vector.tensor_scalar(dtiny[:, 27:28], dtiny[:, 26:27], 1e-8, None,
                                     ALU.add), "s_v")
            vector.wait_ge(sems["s_v"], cnt["s_v"])
            inc(vector.reciprocal(dtiny[:, 28:29], dtiny[:, 27:28]), "s_v")
            inc(vector.scalar_tensor_tensor(dtiny[:, 29:30], dtiny[:, 17:18], 0.5,
                                            dtiny[:, 22:23], ALU.mult, ALU.mult), "s_v")
            vector.wait_ge(sems["s_v"], cnt["s_v"])
            inc(vector.scalar_tensor_tensor(dtiny[:, 30:31], dtiny[:, 29:30], 0.5,
                                            dtiny[:, 25:26], ALU.add, ALU.mult), "s_v")
            vector.wait_ge(sems["s_v"], cnt["s_v"])
            inc(vector.tensor_tensor(dtiny[:, 31:32], dtiny[:, 30:31],
                                     dtiny[:, 28:29], ALU.mult), "s_v")
            vector.wait_ge(sems["s_v"], cnt["s_v"])
            inc(vector.tensor_scalar(surf[:], dtiny[:, 0:2], dtiny[:, 31:32], None,
                                     ALU.mult), "s_v")
        mark("dve_scan", "s_v")

    mark("scan_done_v", "s_v")
    mark("scan_done_a", "s_a")
    mark("scan_done_g", "s_g")
    mark("scan_done_p", "s_p")
    mark("scan_done_out", "s_out")

    @blk.vector
    def _(vector):
        for evn in ["scan_done_a", "scan_done_g", "scan_done_p", "scan_done_out"]:
            wt(vector, evn)
        inc(vector.tensor_copy(dtiny[:, 0:1], dtiny[:, 1:2]), "s_v")
        mark("barrier_v", "s_v")

    @blk.scalar
    def _(scalar):
        for evn in ["barrier_v"]:
            wt(scalar, evn)
        inc(scalar.copy(dtiny[:, 0:1], dtiny[:, 1:2]), "s_a")
        mark("barrier_a", "s_a")

    @blk.gpsimd
    def _(gpsimd):
        wt(gpsimd, "barrier_v")
        inc(gpsimd.tensor_copy(dtiny[:, 0:1], dtiny[:, 1:2]), "s_g")
        mark("barrier_g", "s_g")

    @blk.tensor
    def _(tensor):
        wt(tensor, "barrier_v")
        inc(tensor.matmul(ps_surf[0:16, 0:1], cs["repmask16"][:, 0:16], dtiny[0:16, 0:1],
                          start=True, stop=True), "s_p")
        mark("barrier_p", "s_p")

    @blk.sync
    def _(sync):
        wt(sync, "barrier_v")
        sync.wait_ge(sems["s_ld"], cnt["s_ld"])
        inc(sync.dma_start(out=dtiny[:, 0:2], in_=cdr["d0"]), "s_ld", 16)
        mark("barrier_s", "s_ld")

    pb_scope.close()

    pc = ExitStack()

    def csb(name, shape, dt=F32):
        return pc.enter_context(nc.sbuf_tensor(name, shape, dt))

    def cps(name, shape, dt=F32):
        return pc.enter_context(nc.psum_tensor(name, shape, dt))

    ytile = csb("ytile", [128, 8 * TILE])
    zstile = csb("zstile", [128, 2 * TILE])
    yz = csb("yz", [128, 8 * TILE], BF16)
    xres = csb("xres", [128, 2 * D])
    x1sb = csb("x1sb", [128, TG * D])
    cstat = csb("cstat", [128, 3 * TG])
    csqs = csb("csqs", [128, D])
    x2n_tm = csb("x2n_tm", [128, TG * D], BF16)
    ps_c1 = cps("ps_c1", [128, 512])
    ps_c2 = cps("ps_c2", [128, 512])
    ps_tm = cps("ps_tm", [128, 1024])
    ps_g16 = cps("ps_g16", [128, 16])
    ps_str = cps("ps_str", [16, 128])

    def phase_c(tile):
        t0 = tile * TILE
        tg = "_c%d" % tile

        CYZV0 = cnt["s_v"]

        @blk.sync
        def _(sync):
            if tile == 0:
                wt(sync, "barrier_s")
            else:
                wt(sync, "c1done_%d" % (tile - 1))
            inc(sync.dma_start(out=ytile[:], in_=y_dram[:, t0 * 8:(t0 + TILE) * 8]),
                "s_ld", 16)
            for c8 in range(8):
                if c8 >= 2:
                    sync.wait_ge(sems["s_v"], CYZV0 + c8 - 1)
                inc(sync.dma_start(out=zstile[:, (c8 % 2) * TILE:(c8 % 2 + 1) * TILE],
                                   in_=zsil_dram[c8 * 128:(c8 + 1) * 128, t0:t0 + TILE]),
                    "s_ld", 16)
                mark("czs%d" % c8 + tg, "s_ld")
            mark("cyld" + tg, "s_ld")

        @blk.vector
        def _(vector):
            if tile == 0:
                wt(vector, "barrier_a")
                wt(vector, "barrier_g")
                wt(vector, "barrier_p")
            for c8 in range(8):
                wt(vector, "czs%d" % c8 + tg)
                yv = ytile[:].rearrange("p (t c) -> p t c", c=8)[:, :, c8:c8 + 1]
                inc(vector.tensor_tensor(
                    yz[:, c8 * TILE:(c8 + 1) * TILE].unsqueeze(2),
                    yv, zstile[:, (c8 % 2) * TILE:(c8 % 2 + 1) * TILE].unsqueeze(2),
                    ALU.mult), "s_v")
            mark("cyz" + tg, "s_v")

        CX1V0 = cnt["s_v"]

        @blk.tensor
        def _(tensor):
            wt(tensor, "cyz" + tg)
            if tile == 0:
                wt(tensor, "barrier_a")
                wt(tensor, "barrier_g")
            for g in range(TG):
                if g >= 1:
                    tensor.wait_ge(sems["s_v"], CX1V0 + g)
                for h2 in range(2):
                    for c8 in range(8):
                        inc(tensor.matmul(ps_tm[:, h2 * 512:(h2 + 1) * 512],
                                          yz[:, c8 * TILE + g * 128:c8 * TILE + (g + 1) * 128],
                                          w_out_sb[:, c8 * 1024 + h2 * 512:c8 * 1024 + (h2 + 1) * 512],
                                          start=(c8 == 0), stop=False), "s_p")
                    inc(tensor.matmul(ps_tm[:, h2 * 512:(h2 + 1) * 512], ones_bf[0:1, :],
                                      b_out_sb[:, h2 * 512:(h2 + 1) * 512],
                                      start=False, stop=True), "s_p")
            mark("cwout" + tg, "s_p")

        pbase = ev["cwout" + tg][1] - TG * 18

        @blk.sync
        def _(sync):
            for g in range(TG):
                if g >= 2:
                    sync.wait_ge(sems["s_v"], CX1V0 + g - 1)
                xv = x_tm[t0 + g * 128:t0 + (g + 1) * 128, :]
                inc(sync.dma_start(out=xres[:, (g % 2) * D:(g % 2 + 1) * D], in_=xv),
                    "s_ld", 16)
                mark("cxr%d" % g + tg, "s_ld")

        @blk.vector
        def _(vector):
            for g in range(TG):
                vector.wait_ge(sems["s_p"], pbase + (g + 1) * 18)
                wt(vector, "cxr%d" % g + tg)
                inc(vector.tensor_tensor(x1sb[:, g * D:(g + 1) * D],
                                         xres[:, (g % 2) * D:(g % 2 + 1) * D],
                                         ps_tm[:], ALU.add), "s_v")
            mark("cx1" + tg, "s_v")

        @blk.sync
        def _(sync):
            wt(sync, "cx1" + tg)
            xo = x1_dram[t0:t0 + TILE, :].rearrange("(g p) d -> p g d", p=128)
            inc(sync.dma_start(out=xo, in_=x1sb[:].rearrange("p (g d) -> p g d", d=D)),
                "s_ld", 16)
            mark("cx1st" + tg, "s_ld")

        @blk.vector
        def _(vector):
            wt(vector, "cx1" + tg)
            for g in range(TG):
                vector.tensor_scalar(csqs[:, 0:D], x1sb[:, g * D:(g + 1) * D], 1.0,
                                     0.0, ALU.mult, ALU.add, accum_out=cstat[:, 3 * g:3 * g + 1])
            inc(vector.tensor_copy(csqs[:, 0:1], cstat[:, 0:1]), "s_v")
            mark("csx" + tg, "s_v")

        @blk.scalar
        def _(scalar):
            wt(scalar, "csx" + tg)
            if tile == 0:
                wt(scalar, "barrier_g")
                wt(scalar, "barrier_p")
            for g in range(TG):
                scalar.activation(csqs[:, 0:D], x1sb[:, g * D:(g + 1) * D], AF.Square,
                                  accum_out=cstat[:, 3 * g + 1:3 * g + 2])
            inc(scalar.copy(csqs[:, 0:1], cstat[:, 0:1]), "s_a")
            mark("csxx" + tg, "s_a")

        @blk.vector
        def _(vector):
            wt(vector, "csxx" + tg)
            for g in range(TG):
                mcol = cstat[:, 3 * g:3 * g + 1]
                vcol = cstat[:, 3 * g + 1:3 * g + 2]
                m2col = cstat[:, 3 * g + 2:3 * g + 3]
                inc(vector.tensor_scalar(mcol, mcol, 1.0 / D, None, ALU.mult), "s_v")
                vector.wait_ge(sems["s_v"], cnt["s_v"])
                inc(vector.tensor_tensor(m2col, mcol, mcol, ALU.mult), "s_v")
                vector.wait_ge(sems["s_v"], cnt["s_v"])
                vector.scalar_tensor_tensor(vcol, vcol, 1.0 / D, m2col, ALU.mult,
                                            ALU.subtract)
            inc(vector.tensor_copy(csqs[:, 0:1], cstat[:, 0:1]), "s_v")
            mark("cvar" + tg, "s_v")

        @blk.scalar
        def _(scalar):
            wt(scalar, "cvar" + tg)
            for g in range(TG):
                vcol = cstat[:, 3 * g + 1:3 * g + 2]
                scalar.activation(vcol, vcol, AF.Ln, bias=LNEPS)
                scalar.activation(vcol, vcol, AF.Exp, scale=-0.5)
            inc(scalar.copy(csqs[:, 0:1], cstat[:, 0:1]), "s_a")
            mark("crstd" + tg, "s_a")

        @blk.vector
        def _(vector):
            wt(vector, "crstd" + tg)
            for g in range(TG):
                vector.tensor_scalar(x2n_tm[:, g * D:(g + 1) * D],
                                     x1sb[:, g * D:(g + 1) * D],
                                     cstat[:, 3 * g:3 * g + 1],
                                     cstat[:, 3 * g + 1:3 * g + 2],
                                     ALU.subtract, ALU.mult)
            inc(vector.tensor_copy(csqs[:, 0:1], cstat[:, 0:1]), "s_v")
            mark("cx2n" + tg, "s_v")

        @blk.sync
        def _(sync):
            wt(sync, "cx2n" + tg)
            xo = x2n_dram[t0:t0 + TILE, :].rearrange("(g p) d -> p g d", p=128)
            inc(sync.dma_start(out=xo, in_=x2n_tm[:].rearrange("p (g d) -> p g d", d=D)),
                "s_ld", 16)
            mark("cx2st" + tg, "s_ld")

        mark("c1done_%d" % tile, "s_ld")

    for tile in range(NT):
        phase_c(tile)
    mark("c1out", "s_ld")
    pc.close()

    pc = ExitStack()
    w_sg_sb = csb("w_sg_sb2", [128, 8 * FSH], BF16)
    w_su_sb = csb("w_su_sb2", [128, 8 * FSH], BF16)
    w_sd_sb = csb("w_sd_sb2", [128, 16 * D], BF16)
    x2n_tm = csb("x2n_tm2", [128, TG * D], BF16)
    x2n_fm = csb("x2n_fm2", [128, 8 * TILE], BF16)
    hsh = csb("hsh", [128, 16 * TILE], BF16)
    x1sb = csb("x1sb2", [128, 2 * D])
    zstile = csb("zstile2", [128, TILE])
    csqs = csb("csqs2", [128, TILE])
    cstat = csb("cstat2", [128, 4])
    scem_sb = csb("scem_sb", [16, TILE])
    ps_c1 = cps("ps_c12", [128, 512])
    ps_c2 = cps("ps_c22", [128, 512])
    ps_tm = cps("ps_tm2", [128, 1024])
    ps_g16 = cps("ps_g162", [128, 16])
    ps_str = cps("ps_str2", [16, 128])

    @blk.gpsimd
    def _(gpsimd):
        wt(gpsimd, "c1out")
        for k in range(8):
            kk = slice(k * 128, (k + 1) * 128)
            inc(gpsimd.dma_start(out=w_sg_sb[:, k * FSH:(k + 1) * FSH],
                                 in_=wdr["w_sg_t"][kk, :]), "s_out", 16)
            inc(gpsimd.dma_start(out=w_su_sb[:, k * FSH:(k + 1) * FSH],
                                 in_=wdr["w_su_t"][kk, :]), "s_out", 16)
        for k in range(16):
            inc(gpsimd.dma_start(out=w_sd_sb[:, k * D:(k + 1) * D],
                                 in_=wdr["w_sd_t"][k * 128:(k + 1) * 128, :]), "s_out", 16)
        mark("cshw", "s_out")

    def phase_c2(tile):
        t0 = tile * TILE
        tg = "_c%d" % tile

        @blk.sync
        def _(sync):
            if tile == 0:
                wt(sync, "c1out")
            else:
                wt(sync, "cdone_%d" % (tile - 1))
            xo = x2n_dram[t0:t0 + TILE, :].rearrange("(g p) d -> p g d", p=128)
            inc(sync.dma_start(out=x2n_tm[:].rearrange("p (g d) -> p g d", d=D),
                               in_=xo), "s_ld", 16)
            mark("cx2ld" + tg, "s_ld")
        mark("cx2n" + tg, ev["cx2ld" + tg][0]) if False else None
        ev["cx2n" + tg] = ev["cx2ld" + tg]

        vbase = cnt["s_v"]

        @blk.tensor
        def _(tensor):
            wt(tensor, "cx2n" + tg)
            for i in range(32):
                pst = ps_c1 if i % 2 == 0 else ps_c2
                if i >= 2:
                    tensor.wait_ge(sems["s_v"], vbase + i - 1)
                g, k = divmod(i, 8)
                inc(tensor.transpose(pst[:].bitcast(BF16)[:, 0:128],
                                     x2n_tm[:, g * D + k * 128:g * D + (k + 1) * 128],
                                     ident_bf[:]), "s_p")
            mark("ctr" + tg, "s_p")

        pbase2 = ev["ctr" + tg][1] - 32

        @blk.vector
        def _(vector):
            for i in range(32):
                pst = ps_c1 if i % 2 == 0 else ps_c2
                vector.wait_ge(sems["s_p"], pbase2 + i + 1)
                g, k = divmod(i, 8)
                inc(vector.tensor_copy(
                    x2n_fm[:, k * TILE + g * 128:k * TILE + (g + 1) * 128],
                    pst[:].bitcast(BF16)[:, 0:128]), "s_v")
            mark("cx2fm" + tg, "s_v")

        CGV0 = cnt["s_v"]

        @blk.tensor
        def _(tensor):
            wt(tensor, "cx2fm" + tg)
            if tile == 0:
                tensor.wait_ge(sems["s_out"], ev["cshw"][1])
            for g in range(TG):
                if g >= 1:
                    tensor.wait_ge(sems["s_v"], CGV0 + 5 * (g - 1) + 3)
                for k in range(8):
                    inc(tensor.matmul(ps_g16[:],
                                      x2n_fm[:, k * TILE + g * 128:k * TILE + (g + 1) * 128],
                                      gate_sb[:, k * E:(k + 1) * E],
                                      start=(k == 0), stop=(k == 7)), "s_p")
            mark("cgmm" + tg, "s_p")

        pbase3 = ev["cgmm" + tg][1] - TG * 8
        abase = cnt["s_a"]

        @blk.vector
        def _(vector):
            for g in range(TG):
                gcol = (tile * TG + g) * E
                vector.wait_ge(sems["s_p"], pbase3 + (g + 1) * 8)
                inc(vector.tensor_reduce(cstat[:, 0:1], ps_g16[:], AX.X, ALU.max),
                    "s_v")
                vector.wait_ge(sems["s_v"], cnt["s_v"])
                inc(vector.tensor_scalar(cstat[:, 1:2], cstat[:, 0:1], -1.0, None,
                                         ALU.mult), "s_v")
                vector.wait_ge(sems["s_v"], cnt["s_v"])
                inc(vector.tensor_scalar(sco_tm[:, gcol:gcol + E], ps_g16[:],
                                         cstat[:, 1:2], None, ALU.add), "s_v")
                vector.wait_ge(sems["s_a"], abase + g + 1)
                inc(vector.reciprocal(cstat[:, 2:3], cstat[:, 1:2]), "s_v")
                vector.wait_ge(sems["s_v"], cnt["s_v"])
                inc(vector.tensor_scalar(sco_tm[:, gcol:gcol + E],
                                         sco_tm[:, gcol:gcol + E],
                                         cstat[:, 2:3], None, ALU.mult), "s_v")
            mark("csm" + tg, "s_v")

        @blk.scalar
        def _(scalar):
            for g in range(TG):
                gcol = (tile * TG + g) * E
                scalar.wait_ge(sems["s_v"], CGV0 + 5 * g + 3)
                inc(scalar.activation(sco_tm[:, gcol:gcol + E], sco_tm[:, gcol:gcol + E],
                                      AF.Exp, accum_out=cstat[:, 1:2]), "s_a")
            mark("cexp" + tg, "s_a")

        CEMV0 = cnt["s_v"]

        @blk.tensor
        def _(tensor):
            wt(tensor, "csm" + tg)
            for g in range(TG):
                if g >= 1:
                    tensor.wait_ge(sems["s_v"], CEMV0 + g)
                gcol = (tile * TG + g) * E
                inc(tensor.transpose(ps_str[:], sco_tm[:, gcol:gcol + E], cs["ident"][:]),
                    "s_p")
            mark("cstr" + tg, "s_p")

        pbase4 = ev["cstr" + tg][1] - TG

        @blk.vector
        def _(vector):
            for g in range(TG):
                vector.wait_ge(sems["s_p"], pbase4 + g + 1)
                inc(vector.tensor_copy(scem_sb[:, g * 128:(g + 1) * 128], ps_str[:]),
                    "s_v")
            mark("cem" + tg, "s_v")

        @blk.sync
        def _(sync):
            wt(sync, "cem" + tg)
            inc(sync.dma_start(out=scem_dram[:, t0:t0 + TILE], in_=scem_sb[:]),
                "s_ld", 16)
            mark("cemst" + tg, "s_ld")

        CHSH0 = cnt["s_v"]

        @blk.tensor
        def _(tensor):
            wt(tensor, "cx2fm" + tg)
            wt(tensor, "cstr" + tg)
            for fc in range(16):
                if fc >= 1:
                    tensor.wait_ge(sems["s_v"], CHSH0 + fc)
                for k in range(8):
                    inc(tensor.matmul(ps_c1[:],
                                      w_sg_sb[:, k * FSH + fc * 128:k * FSH + (fc + 1) * 128],
                                      x2n_fm[:, k * TILE:(k + 1) * TILE],
                                      start=(k == 0), stop=(k == 7)), "s_p")
                for k in range(8):
                    inc(tensor.matmul(ps_c2[:],
                                      w_su_sb[:, k * FSH + fc * 128:k * FSH + (fc + 1) * 128],
                                      x2n_fm[:, k * TILE:(k + 1) * TILE],
                                      start=(k == 0), stop=(k == 7)), "s_p")
            mark("cgu" + tg, "s_p")

        pbase5 = ev["cgu" + tg][1] - 16 * 16
        abase2 = cnt["s_a"]

        @blk.scalar
        def _(scalar):
            for fc in range(16):
                scalar.wait_ge(sems["s_p"], pbase5 + fc * 16 + 8)
                if fc >= 1:
                    scalar.wait_ge(sems["s_v"], CHSH0 + fc)
                inc(scalar.activation(csqs[:, 0:TILE], ps_c1[:], AF.Gelu), "s_a")
            mark("cerf" + tg, "s_a")

        @blk.vector
        def _(vector):
            for fc in range(16):
                vector.wait_ge(sems["s_a"], abase2 + fc + 1)
                vector.wait_ge(sems["s_p"], pbase5 + (fc + 1) * 16)
                inc(vector.tensor_tensor(hsh[:, fc * TILE:(fc + 1) * TILE],
                                         csqs[:, 0:TILE], ps_c2[:], ALU.mult), "s_v")
            mark("chsh" + tg, "s_v")

        CBASE0 = cnt["s_v"]

        @blk.tensor
        def _(tensor):
            wt(tensor, "chsh" + tg)
            for g in range(TG):
                if g >= 1:
                    tensor.wait_ge(sems["s_v"], CBASE0 + g)
                for h2 in range(2):
                    for k in range(16):
                        inc(tensor.matmul(ps_tm[:, h2 * 512:(h2 + 1) * 512],
                                          hsh[:, k * TILE + g * 128:k * TILE + (g + 1) * 128],
                                          w_sd_sb[:, k * D + h2 * 512:k * D + (h2 + 1) * 512],
                                          start=(k == 0), stop=(k == 15)), "s_p")
            mark("csd" + tg, "s_p")

        pbase6 = ev["csd" + tg][1] - TG * 32

        st_cnt = {}

        @blk.sync
        def _(sync):
            for i in range(TG + 1):
                if i < TG:
                    g = i
                    if g >= 2:
                        sync.wait_ge(sems["s_v"], CBASE0 + g - 1)
                        sync.wait_ge(sems["s_ld"], st_cnt[g - 2])
                    inc(sync.dma_start(out=x1sb[:, (g % 2) * D:(g % 2 + 1) * D],
                                       in_=x1_dram[t0 + g * 128:t0 + (g + 1) * 128, :]),
                        "s_ld", 16)
                    mark("cx1r%d" % g + tg, "s_ld")
                if i >= 1:
                    g = i - 1
                    sync.wait_ge(sems["s_v"], CBASE0 + g + 1)
                    inc(sync.dma_start(out=out[t0 + g * 128:t0 + (g + 1) * 128, :],
                                       in_=x1sb[:, (g % 2) * D:(g % 2 + 1) * D]),
                        "s_ld", 16)
                    st_cnt[g] = cnt["s_ld"]
            mark("cdone_%d" % tile, "s_ld")

        @blk.vector
        def _(vector):
            for g in range(TG):
                vector.wait_ge(sems["s_p"], pbase6 + (g + 1) * 32)
                wt(vector, "cx1r%d" % g + tg)
                inc(vector.tensor_tensor(x1sb[:, (g % 2) * D:(g % 2 + 1) * D],
                                         x1sb[:, (g % 2) * D:(g % 2 + 1) * D],
                                         ps_tm[:], ALU.add), "s_v")
            mark("cbase" + tg, "s_v")

    for tile in range(NT):
        phase_c2(tile)
    mark("baseout", "s_ld")
    pc.close()

    pc = ExitStack()
    TE = T // 8
    assert TE % 128 == 0, "routing phase needs T to be a multiple of 1024"
    RBLK = TE // 128

    seb = csb("seb", [128, TE])
    mwork = csb("mwork", [128, TE])
    msku = csb("msku", [128, TE], mybir.dt.uint32)
    prefx = csb("prefx", [128, TE])
    zrow = csb("zrow", [128, TE])
    theta = csb("theta", [16, 4])
    cnt128 = csb("cnt128", [128, 1])
    rkm = csb("rkm", [128, TE])
    rkt = csb("rkt", [128, RBLK * 128])
    idc = csb("idc", [128, 2])
    rhs2 = csb("rhs2", [128, 2])
    ps_th = cps("ps_th", [128, 1])
    ps_ct = cps("ps_ct", [16, 1])
    ps_slb = [cps("ps_sl%d" % _b, [128, 2 * E]) for _b in range(NBLK)]
    ps_tr2 = cps("ps_tr2", [128, 128])

    @blk.sync
    def _(sync):
        wt(sync, "cemst_c%d" % (NT - 1))
        sv = scem_dram[:, :].rearrange("e (c t) -> (e c) t", c=8)
        inc(sync.dma_start(out=seb[:], in_=sv), "s_ld", 16)
        mark("sebld", "s_ld")

    @blk.vector
    def _(vector):
        wt(vector, "sebld")
        vector.memset(theta[:, 0:1], 0.5)
        inc(vector.memset(zrow[:], 0.0), "s_v")
        mark("bis0", "s_v")

    NITER = 45
    BISV0 = cnt["s_v"]

    @blk.tensor
    def _(tensor):
        wt(tensor, "bis0")
        for it in range(NITER):
            if it > 0:
                tensor.wait_ge(sems["s_v"], BISV0 + 4 * it)
            inc(tensor.matmul(ps_th[:], cs["repmask16"][:], theta[:, 0:1],
                              start=True, stop=True), "s_p")
            tensor.wait_ge(sems["s_v"], BISV0 + 4 * it + 1)
            inc(tensor.matmul(ps_ct[:], cs["smask16"][:], cnt128[:],
                              start=True, stop=True), "s_p")
        mark("bismm", "s_p")

    pbb = ev["bismm"][1] - 2 * NITER

    @blk.vector
    def _(vector):
        delta = 0.25
        for it in range(NITER):
            vector.wait_ge(sems["s_p"], pbb + 2 * it + 1)
            inc(vector.tensor_scalar(mwork[:], seb[:], ps_th[:, 0:1], 0.0,
                                     ALU.is_gt, ALU.add, accum_out=cnt128[:]), "s_v")
            vector.wait_ge(sems["s_p"], pbb + 2 * it + 2)
            inc(vector.tensor_scalar(theta[:, 3:4], ps_ct[:, 0:1], float(T / 8.0),
                                     None, ALU.is_gt), "s_v")
            vector.wait_ge(sems["s_v"], cnt["s_v"])
            inc(vector.scalar_tensor_tensor(theta[:, 1:2], theta[:, 3:4], 2.0 * delta,
                                            theta[:, 0:1], ALU.mult, ALU.add), "s_v")
            vector.wait_ge(sems["s_v"], cnt["s_v"])
            inc(vector.tensor_scalar(theta[:, 0:1], theta[:, 1:2], -delta, None,
                                     ALU.add), "s_v")
            delta *= 0.5
        vector.wait_ge(sems["s_v"], cnt["s_v"])
        inc(vector.tensor_scalar(theta[:, 0:1], theta[:, 0:1], 8.0 * delta, None,
                                 ALU.add), "s_v")
        mark("bisend", "s_v")

    @blk.tensor
    def _(tensor):
        wt(tensor, "bisend")
        inc(tensor.matmul(ps_th[:], cs["repmask16"][:], theta[:, 0:1],
                          start=True, stop=True), "s_p")
        mark("thbc", "s_p")

    @blk.vector
    def _(vector):
        wt(vector, "thbc")
        inc(vector.tensor_scalar(mwork[:], seb[:], ps_th[:, 0:1], None, ALU.is_gt),
            "s_v")
        vector.tensor_scalar(msku[:], seb[:], ps_th[:, 0:1], None, ALU.is_gt)
        vector.wait_ge(sems["s_v"], cnt["s_v"])
        inc(vector.tensor_tensor_scan(prefx[:], mwork[:], zrow[:], 0.0,
                                      ALU.add, ALU.add), "s_v")
        vector.wait_ge(sems["s_v"], cnt["s_v"])
        inc(vector.tensor_copy(cnt128[:], prefx[:, TE - 1:TE]), "s_v")
        mark("pfx", "s_v")

    @blk.tensor
    def _(tensor):
        wt(tensor, "pfx")
        inc(tensor.matmul(ps_th[:], cs["trig8"][:], cnt128[:], start=True, stop=True),
            "s_p")
        mark("offs", "s_p")

    @blk.vector
    def _(vector):
        wt(vector, "offs")
        vector.memset(zrow[:], DUMP)
        inc(vector.tensor_scalar(prefx[:], prefx[:], ps_th[:, 0:1], -1.0,
                                 ALU.add, ALU.add), "s_v")
        vector.wait_ge(sems["s_v"], cnt["s_v"])
        inc(vector.select(rkm[:], msku[:], prefx[:], zrow[:]), "s_v")
        mark("rkm", "s_v")

    RKTV0 = cnt["s_v"]

    @blk.tensor
    def _(tensor):
        wt(tensor, "rkm")
        for b2 in range(RBLK):
            w2 = min(128, TE - b2 * 128)
            if b2 >= 1:
                tensor.wait_ge(sems["s_v"], RKTV0 + b2)
            inc(tensor.transpose(ps_tr2[0:w2, :], rkm[:, b2 * 128:b2 * 128 + w2],
                                 cs["ident"][:]), "s_p")
        mark("rktr", "s_p")

    pbt = ev["rktr"][1] - RBLK

    @blk.vector
    def _(vector):
        for b2 in range(RBLK):
            w2 = min(128, TE - b2 * 128)
            vector.wait_ge(sems["s_p"], pbt + b2 + 1)
            inc(vector.tensor_copy(rkt[:, b2 * 128:(b2 + 1) * 128],
                                   ps_tr2[0:w2, :].to_broadcast([128, 128])
                                   if False else ps_tr2[:, 0:128]), "s_v")
        mark("rkt_done", "s_v")

    NG = T // 128
    oh2 = csb("oh2", [128, 2 * 512])
    rhs4 = csb("rhs4", [128, 4])

    VOH0 = cnt["s_v"]
    POH0 = cnt["s_p"]

    @blk.vector
    def _(vector):
        wt(vector, "rkt_done")
        for e in range(E):
            for g in range(NG):
                tc8, b2 = divmod(g, RBLK)
                it = e * NG + g
                par = it % 2
                if it >= 2:
                    vector.wait_ge(sems["s_p"], POH0 + (it - 1) * NBLK)
                rcol = rkt[:, b2 * 128 + e * 8 + tc8:b2 * 128 + e * 8 + tc8 + 1]
                vector.tensor_scalar(rhs4[:, 2 * par:2 * par + 1], cs["iotacol"][:],
                                     float(tc8 * TE + b2 * 128), None, ALU.add)
                vector.wait_ge(sems["s_v"], cnt["s_v"])
                inc(vector.tensor_copy(rhs4[:, 2 * par + 1:2 * par + 2],
                                       sco_tm[:, g * E + e:g * E + e + 1]), "s_v")
                inc(vector.tensor_scalar(oh2[:, par * 512:par * 512 + 512],
                                         cs["iota512"][:], rcol, None,
                                         ALU.is_equal), "s_v")
        mark("ohv", "s_v")

    @blk.tensor
    def _(tensor):
        for e in range(E):
            for g in range(NG):
                it = e * NG + g
                par = it % 2
                tensor.wait_ge(sems["s_v"], VOH0 + 2 * (it + 1))
                for b3 in range(NBLK):
                    inc(tensor.matmul(ps_slb[b3][:, 2 * e:2 * e + 2],
                                      oh2[:, par * 512 + b3 * 128:par * 512 + (b3 + 1) * 128],
                                      rhs4[:, 2 * par:2 * par + 2],
                                      start=(g == 0), stop=(g == NG - 1)), "s_p")
        mark("slotmm", "s_p")

    @blk.scalar
    def _(scalar):
        wt(scalar, "slotmm")
        for b3 in range(NBLK):
            inc(scalar.activation(slots[:, b3 * 2 * E:(b3 + 1) * 2 * E],
                                  ps_slb[b3][:], AF.Identity), "s_a")
        mark("slots_sb", "s_a")

    @blk.vector
    def _(vector):
        wt(vector, "slots_sb")
        iv = slots[:, 0:2 * E * NBLK].rearrange("p (q two) -> p q two", two=2)[:, :, 0:1]
        inc(vector.tensor_copy(idint[:, 0:E * NBLK].unsqueeze(2), iv), "s_v")
        mark("idint", "s_v")

    mark("routing_done", "s_v")
    pc.close()

    pc = ExitStack()
    CAPP = NBLK * 128
    xtok_tm = csb("xtok_tm", [128, NBLK * D], BF16)
    xtok_fm = csb("xtok_fm", [128, 8 * CAPP], BF16)
    wgu = csb("wgu", [128, 2 * 2 * 8 * 128], BF16)
    wdd = csb("wdd", [128, 16 * D], BF16)
    hex_ = csb("hex", [128, 16 * CAPP], BF16)
    erfsc = csb("erfsc", [128, CAPP])
    eo = csb("eo", [128, D])
    ps_e1 = cps("ps_e1", [128, CAPP])
    ps_e2 = cps("ps_e2", [128, CAPP])
    ps_eo = cps("ps_eo", [128, D])
    ps_tr3 = cps("ps_tr3", [128, 128])

    def expert(e):
        tg = "_e%d" % e

        @blk.gpsimd
        def _(gpsimd):
            if e == 0:
                wt(gpsimd, "idint")
                wt(gpsimd, "baseout")
            else:
                wt(gpsimd, "escat_%d" % (e - 1))
            for b3 in range(NBLK):
                inc(gpsimd.indirect_dma_start(
                    xtok_tm[:, b3 * D:(b3 + 1) * D], None, x2n_dram,
                    IndirectOffsetOnAxis(idint[:, b3 * E + e:b3 * E + e + 1], 0)),
                    "s_out", 16)
            mark("egath" + tg, "s_out")

        vb = cnt["s_v"]

        @blk.tensor
        def _(tensor):
            wt(tensor, "egath" + tg)
            if e > 0:
                tensor.wait_ge(sems["s_v"], ev["exv_%d" % (e - 1)][1])
            for i in range(NBLK * 8):
                b3, k = divmod(i, 8)
                if i >= 1:
                    tensor.wait_ge(sems["s_v"], vb + i)
                inc(tensor.transpose(ps_tr3[:].bitcast(BF16)[:, 0:128],
                                     xtok_tm[:, b3 * D + k * 128:b3 * D + (k + 1) * 128],
                                     ident_bf[:]), "s_p")
            mark("etr" + tg, "s_p")

        pbe = ev["etr" + tg][1] - NBLK * 8

        @blk.vector
        def _(vector):
            for i in range(NBLK * 8):
                b3, k = divmod(i, 8)
                vector.wait_ge(sems["s_p"], pbe + i + 1)
                inc(vector.tensor_copy(
                    xtok_fm[:, k * CAPP + b3 * 128:k * CAPP + (b3 + 1) * 128],
                    ps_tr3[:].bitcast(BF16)[:, 0:128]), "s_v")
            mark("exv" + tg, "s_v")

        ev["exv_%d" % e] = ("s_v", cnt["s_v"])

        PGU0 = cnt["s_p"]

        @blk.gpsimd
        def _(gpsimd):
            wt(gpsimd, "egath" + tg)
            for fc in range(16):
                par = fc % 2
                if fc >= 2:
                    gpsimd.wait_ge(sems["s_p"], PGU0 + (fc - 1) * 16)
                gv = wdr["wg_t"][e].rearrange("(k p) f -> p k f", p=128)[
                    :, :, fc * 128:(fc + 1) * 128]
                ov = wgu[:, par * 2048:par * 2048 + 1024].rearrange(
                    "p (k f) -> p k f", f=128)
                inc(gpsimd.dma_start(out=ov, in_=gv), "s_out", 16)
                uv = wdr["wu_t"][e].rearrange("(k p) f -> p k f", p=128)[
                    :, :, fc * 128:(fc + 1) * 128]
                ov2 = wgu[:, par * 2048 + 1024:par * 2048 + 2048].rearrange(
                    "p (k f) -> p k f", f=128)
                inc(gpsimd.dma_start(out=ov2, in_=uv), "s_out", 16)
            mark("eguld" + tg, "s_out")

        gb = cnt["s_out"] - 16 * 2 * 16

        EHSH0 = cnt["s_v"]

        @blk.tensor
        def _(tensor):
            wt(tensor, "exv" + tg)
            for fc in range(16):
                par = fc % 2
                tensor.wait_ge(sems["s_out"], gb + (fc + 1) * 32)
                if fc >= 1:
                    tensor.wait_ge(sems["s_v"], EHSH0 + fc)
                for k in range(8):
                    inc(tensor.matmul(ps_e1[:],
                                      wgu[:, par * 2048 + k * 128:par * 2048 + (k + 1) * 128],
                                      xtok_fm[:, k * CAPP:(k + 1) * CAPP],
                                      start=(k == 0), stop=(k == 7)), "s_p")
                for k in range(8):
                    inc(tensor.matmul(ps_e2[:],
                                      wgu[:, par * 2048 + 1024 + k * 128:par * 2048 + 1024 + (k + 1) * 128],
                                      xtok_fm[:, k * CAPP:(k + 1) * CAPP],
                                      start=(k == 0), stop=(k == 7)), "s_p")
            mark("egu" + tg, "s_p")

        pbe2 = ev["egu" + tg][1] - 16 * 16
        abe = cnt["s_a"]

        @blk.scalar
        def _(scalar):
            for fc in range(16):
                scalar.wait_ge(sems["s_p"], pbe2 + fc * 16 + 8)
                if fc >= 1:
                    scalar.wait_ge(sems["s_v"], EHSH0 + fc)
                inc(scalar.activation(erfsc[:], ps_e1[:], AF.Gelu), "s_a")
            mark("eerf" + tg, "s_a")

        @blk.vector
        def _(vector):
            for fc in range(16):
                vector.wait_ge(sems["s_a"], abe + fc + 1)
                vector.wait_ge(sems["s_p"], pbe2 + (fc + 1) * 16)
                inc(vector.tensor_tensor(hex_[:, fc * CAPP:(fc + 1) * CAPP],
                                         erfsc[:], ps_e2[:], ALU.mult), "s_v")
            mark("ehsh" + tg, "s_v")

        @blk.gpsimd
        def _(gpsimd):
            wt(gpsimd, "eguld" + tg)
            if e > 0:
                gpsimd.wait_ge(sems["s_p"], ev["ewd_%d" % (e - 1)][1])
            dv = wdr["wd_t"][e].rearrange("(kk p) d -> p kk d", p=128)
            ov = wdd[:].rearrange("p (kk d) -> p kk d", d=D)
            inc(gpsimd.dma_start(out=ov, in_=dv), "s_out", 16)
            mark("ewdld" + tg, "s_out")

        EOV0 = cnt["s_v"]

        @blk.tensor
        def _(tensor):
            wt(tensor, "ehsh" + tg)
            wt(tensor, "ewdld" + tg)
            for b3 in range(NBLK):
                if b3 >= 1:
                    tensor.wait_ge(sems["s_v"], EOV0 + b3)
                for h2 in range(2):
                    for fidx in range(16):
                        inc(tensor.matmul(ps_eo[:, h2 * 512:(h2 + 1) * 512],
                                          hex_[:, fidx * CAPP + b3 * 128:fidx * CAPP + (b3 + 1) * 128],
                                          wdd[:, fidx * D + h2 * 512:fidx * D + (h2 + 1) * 512],
                                          start=(fidx == 0), stop=(fidx == 15)), "s_p")
            mark("ewd" + tg, "s_p")
        ev["ewd_%d" % e] = ev["ewd" + tg]

        pbe3 = ev["ewd" + tg][1] - NBLK * 32
        ESC0 = cnt["s_out"]

        @blk.vector
        def _(vector):
            for b3 in range(NBLK):
                vector.wait_ge(sems["s_p"], pbe3 + (b3 + 1) * 32)
                if b3 >= 1:
                    vector.wait_ge(sems["s_out"], ESC0 + b3 * 16)
                wcol = slots[:, 2 * (b3 * E + e) + 1:2 * (b3 * E + e) + 2]
                inc(vector.tensor_scalar(eo[:, 0:D] if NBLK == 1 else
                                         eo[:, 0:D], ps_eo[:], wcol, None, ALU.mult),
                    "s_v")
            mark("eos" + tg, "s_v")

        vbe = ev["eos" + tg][1] - NBLK

        @blk.gpsimd
        def _(gpsimd):
            wt(gpsimd, "ewdld" + tg)
            for b3 in range(NBLK):
                gpsimd.wait_ge(sems["s_v"], vbe + b3 + 1)
                inc(gpsimd.indirect_dma_start(
                    out, IndirectOffsetOnAxis(idint[:, b3 * E + e:b3 * E + e + 1], 0),
                    eo[:, 0:D], None, compute_op=ALU.add), "s_out", 16)
                gpsimd.wait_ge(sems["s_out"], cnt["s_out"])
            mark("escat_%d" % e, "s_out")


    for e in range(E):
        expert(e)

    @blk.sync
    def _(sync):
        wt(sync, "escat_%d" % (E - 1))
        wt(sync, "baseout")
        sync.wait_ge(sems["s_out"], cnt["s_out"])
        sync.wait_ge(sems["s_ld"], cnt["s_ld"])
        if dbg:
            for nm, (src, dst) in dbg_mirror.items():
                inc(sync.dma_start(out=dst, in_=src), "s_ld", 16)
            sync.wait_ge(sems["s_ld"], cnt["s_ld"])
    pc.close()

    return nc



def kernel(**inputs):
    T = int(np.asarray(inputs["L0_attn_x"]).shape[1])
    x = np.ascontiguousarray(np.asarray(inputs["L0_attn_x"], dtype=np.float32))
    w = stage_weights(inputs)
    c = host_consts()
    nc = build(T)
    in_map = {}
    for k, v in w.items():
        in_map[k] = np.ascontiguousarray(v.astype(np.float32))
    for k, v in c.items():
        in_map["c_" + k] = np.ascontiguousarray(v.astype(np.float32))
    in_maps = [dict(in_map, x_tm=np.ascontiguousarray(x[b])) for b in range(8)]
    res = run_bass_kernel_spmd(nc, in_maps, core_ids=list(range(8)))
    return np.stack([res.results[b]["out"] for b in range(8)], axis=0)


# version 08
# speedup vs baseline: 1.0593x; 1.0360x over previous
import math
import numpy as np
import concourse.bass as bass
import concourse.mybir as mybir
from concourse.bass_utils import run_bass_kernel_spmd
from concourse.bass import IndirectOffsetOnAxis
from contextlib import ExitStack

F32 = mybir.dt.float32
BF16 = mybir.dt.bfloat16
I32 = mybir.dt.int32
AF = mybir.ActivationFunctionType
ALU = mybir.AluOpType
AX = mybir.AxisListType

D = 1024
S = 16
E = 16
FEXP = 2048
FSH = 2048
LN20 = math.log(20.0)
SQ20 = math.sqrt(20.0)
CAVG = 0.01 / 1024.0
LNEPS = 1e-5
DUMP = 600.0
FCONST_VALS = [1e-8, 1e-20, LN20, LNEPS, 1e-30, 1.0, 0.0, 2.0]


def host_consts():
    sp = np.arange(128) // 8
    c8p = np.arange(128) % 8
    c = {}
    m = np.zeros((128, 16), np.float32)
    m[np.arange(128), sp] = 1.0
    c["smask16"] = m
    c["repmask16"] = np.ascontiguousarray(m.T)
    m8 = np.zeros((8, 128), np.float32)
    m8[c8p, np.arange(128)] = 1.0
    c["mask8"] = m8
    mc = np.zeros((128, 8), np.float32)
    mc[np.arange(128), c8p] = 1.0
    c["c8maskT"] = mc
    for j, (nm, sg) in enumerate([("bre", 1.0), ("bim", 1.0), ("cre", 1.0), ("cim", -1.0)]):
        mm = np.zeros((64, 128), np.float32)
        mm[j * 16 + sp, np.arange(128)] = sg
        c["bcm_" + nm] = mm
    ph = np.linspace(0.0, 2 * math.pi * (1 - 1.0 / S), S)
    c["d0"] = (0.01 * np.stack([np.cos(ph), np.sin(ph)], -1)).astype(np.float32)
    ep = np.zeros((16, 2), np.float32)
    ep[:, 0] = 1e-10
    c["epsC"] = ep
    c["ident"] = np.eye(128, dtype=np.float32)
    c["iota512"] = np.ascontiguousarray(
        np.broadcast_to(np.arange(512, dtype=np.float32), (128, 512)))
    c["iotacol"] = np.arange(128, dtype=np.float32).reshape(128, 1)
    c["tri128"] = np.ascontiguousarray(np.triu(np.ones((128, 128), np.float32)))
    c["onesrow"] = np.ones((1, 128), np.float32)
    ee = np.arange(128) // 8
    tc = np.arange(128) % 8
    trig = ((ee[:, None] == ee[None, :]) & (tc[:, None] < tc[None, :])).astype(np.float32)
    c["trig8"] = np.ascontiguousarray(trig)
    fv = np.zeros((128, 8), np.float32)
    for i, v in enumerate(FCONST_VALS):
        fv[:, i] = v
    c["fvals"] = fv
    c["clim"] = np.ascontiguousarray(
        np.broadcast_to(np.array([SQ20, 20.0], np.float32), (16, 2)))
    return c


CONST_SHAPES = {"smask16": [128, 16], "repmask16": [16, 128], "mask8": [8, 128],
                "c8maskT": [128, 8], "bcm_bre": [64, 128], "bcm_bim": [64, 128],
                "bcm_cre": [64, 128], "bcm_cim": [64, 128], "d0": [16, 2],
                "epsC": [16, 2], "ident": [128, 128], "iota512": [128, 512],
                "iotacol": [128, 1], "tri128": [128, 128], "onesrow": [1, 128],
                "clim": [16, 2], "trig8": [128, 128], "fvals": [128, 8]}

WEIGHT_SHAPES = {"w_in_t": [D, 2 * D], "w_xp_t": [D, 128], "w_dt_fm": [65, D],
                 "w_res_t": [D, 64], "w_out_aug": [D + 1, D], "gate_t": [D, E],
                 "w_sg_t": [D, FSH], "w_su_t": [D, FSH], "w_sd_t": [FSH, D],
                 "wg_t": [E, D, FEXP], "wu_t": [E, D, FEXP], "wd_t": [E, FEXP, D],
                 "a_log": [1, 16]}


def stage_weights(inp):
    w = {}
    w["w_in_t"] = np.ascontiguousarray(np.asarray(inp["L0_attn_W_in"]).T)
    Wx = np.asarray(inp["L0_attn_W_xproj"])
    perm = list(range(64)) + [64 + 2 * s for s in range(16)] + \
        [64 + 2 * s + 1 for s in range(16)] + [96 + 2 * s for s in range(16)] + \
        [96 + 2 * s + 1 for s in range(16)]
    w["w_xp_t"] = np.ascontiguousarray(Wx[perm].T)
    Wdt = np.asarray(inp["L0_attn_W_dt"])
    bdt = np.asarray(inp["L0_attn_b_dt"])
    w["w_dt_fm"] = np.ascontiguousarray(np.vstack([Wdt.T, bdt[None, :]]))
    Wres = np.asarray(inp["L0_attn_W_res"])
    wr = np.zeros((1024, 64), np.float32)
    wr[:, 0:16] = Wres[0::2].T
    wr[:, 32:48] = Wres[1::2].T
    w["w_res_t"] = wr
    w["w_out_aug"] = np.ascontiguousarray(
        np.vstack([np.asarray(inp["L0_attn_W_out"]).T,
                   np.asarray(inp["L0_attn_b_out"])[None, :]]))
    w["gate_t"] = np.ascontiguousarray(np.asarray(inp["L0_ffn_gate_w"]).T)
    w["w_sg_t"] = np.ascontiguousarray(np.asarray(inp["L0_ffn_Ws_g"]).T)
    w["w_su_t"] = np.ascontiguousarray(np.asarray(inp["L0_ffn_Ws_u"]).T)
    w["w_sd_t"] = np.ascontiguousarray(np.asarray(inp["L0_ffn_Ws_d"]).T)
    w["wg_t"] = np.ascontiguousarray(np.asarray(inp["L0_ffn_Wg"]).transpose(0, 2, 1))
    w["wu_t"] = np.ascontiguousarray(np.asarray(inp["L0_ffn_Wu"]).transpose(0, 2, 1))
    w["wd_t"] = np.ascontiguousarray(np.asarray(inp["L0_ffn_Wd"]).transpose(0, 2, 1))
    w["a_log"] = np.ascontiguousarray(np.asarray(inp["L0_attn_A_log"]).reshape(1, 16))
    return w


def build(T, dbg=False):
    TILE = 512
    NT = T // TILE
    TG = TILE // 128
    SUB = 32
    NSUB = T // SUB
    YC = 64
    CAP = T // 8
    NBLK = max(CAP // 128, 1)

    nc = bass.Bass("TRN2", target_bir_lowering=False, debug=False,
                   detect_race_conditions=False)
    es = ExitStack()

    x_tm = nc.dram_tensor("x_tm", [T, D], F32, kind="ExternalInput").ap()
    wdr = {k: nc.dram_tensor(k, v, F32, kind="ExternalInput").ap()
           for k, v in WEIGHT_SHAPES.items()}
    cdr = {k: nc.dram_tensor("c_" + k, v, F32, kind="ExternalInput").ap()
           for k, v in CONST_SHAPES.items()}
    out = nc.dram_tensor("out", [T, D], F32, kind="ExternalOutput").ap()

    dt_dram = nc.dram_tensor("dt_dram", [D, T], F32, kind="Internal").ap()
    dtx_dram = nc.dram_tensor("dtx_dram", [D, T], F32, kind="Internal").ap()
    zsil_dram = nc.dram_tensor("zsil_dram", [D, T], F32, kind="Internal").ap()
    bc_dram = nc.dram_tensor("bc_dram", [512, T], F32, kind="Internal").ap()
    xvn_dram = nc.dram_tensor("xvn_dram", [16, 2 * T], F32, kind="Internal").ap()
    y_dram = nc.dram_tensor("y_dram", [128, 8 * T], F32, kind="Internal").ap()
    x1_dram = nc.dram_tensor("x1_dram", [T, D], F32, kind="Internal").ap()
    x2n_dram = nc.dram_tensor("x2n_dram", [T, D], BF16, kind="Internal").ap()
    scem_dram = nc.dram_tensor("scem_dram", [E, T], F32, kind="Internal").ap()
    rkm_dram = nc.dram_tensor("rkm_dram", [E * 8, T // 8], F32, kind="Internal").ap()
    a_dram = nc.dram_tensor("a_dram", [16, 1], F32, kind="Internal").ap()
    dbg_mirror = {}
    if dbg:
        for nm, src in [("dt_dram", dt_dram), ("dtx_dram", dtx_dram),
                        ("zsil_dram", zsil_dram), ("bc_dram", bc_dram),
                        ("xvn_dram", xvn_dram), ("y_dram", y_dram),
                        ("x1_dram", x1_dram), ("scem_dram", scem_dram),
                        ("a_dram", a_dram)]:
            dbg_mirror[nm] = (src, nc.dram_tensor(
                "dbg_" + nm, list(src.tensor.shape), F32, kind="ExternalOutput").ap())
        dbg_mirror["x2n_dram"] = (x2n_dram, nc.dram_tensor(
            "dbg_x2n_dram", [T, D], BF16, kind="ExternalOutput").ap())
    if dbg:
        xn_dbg = nc.dram_tensor("dbg_xn", [T, D], BF16, kind="ExternalOutput").ap()
        xnfm_dbg = nc.dram_tensor("dbg_xnfm", [128, (T // 512) * 4096], BF16,
                                  kind="ExternalOutput").ap()
        stat_dbg = nc.dram_tensor("dbg_stat", [128, 12], F32, kind="ExternalOutput").ap()

    sems = {}
    cnt = {}
    for nm in ["s_ld", "s_gld", "s_v", "s_a", "s_g", "s_p", "s_pb", "s_al",
               "s_ye", "s_out", "s_sc"]:
        sems[nm] = es.enter_context(nc.semaphore(nm))
        cnt[nm] = 0

    nc._dbg_sems = {k: v for k, v in sems.items()}

    def inc(inst, nm, n=1):
        inst.then_inc(sems[nm], n)
        cnt[nm] += n
        return cnt[nm]

    ev = {}

    def mark(name, nm):
        ev[name] = (nm, cnt[nm])

    def wt(eng, name):
        nm, c = ev[name]
        if c > 0:
            eng.wait_ge(sems[nm], c)

    def sb(name, shape, dt=F32):
        return es.enter_context(nc.sbuf_tensor(name, shape, dt))

    cs = {k: sb("cs_" + k, v) for k, v in CONST_SHAPES.items()}
    for i, v in enumerate(FCONST_VALS):
        nc.const_aps.register(F32, v, cs["fvals"][:, i:i + 1])
    ident_bf = sb("identbf", [128, 128], BF16)
    acol = sb("acol", [128, 1])
    arow = sb("arow", [1, 16])
    w_xp_sb = sb("w_xp_sb", [128, 8 * 128], BF16)
    w_dt_sb = sb("w_dt_sb", [65, D], BF16)
    w_res_sb = sb("w_res_sb", [128, 8 * 64], BF16)
    gate_sb = sb("gate_sb", [128, 8 * E], BF16)
    w_out_sb = sb("w_out_sb", [128, 8 * 1024], BF16)
    b_out_sb = sb("b_out_sb", [1, 1024], BF16)
    sco_tm = sb("sco_tm", [128, (T // 128) * E])
    slots = sb("slots", [128, 128])
    idint = sb("idint", [128, 64], I32)
    ones_bf = sb("ones_bf", [1, 128], BF16)

    blk = es.enter_context(nc.Block())

    @blk.sync
    def _(sync):
        for k in CONST_SHAPES:
            inc(sync.dma_start(out=cs[k][:], in_=cdr[k]), "s_ld", 16)
        inc(sync.dma_start(out=arow[:], in_=wdr["a_log"]), "s_ld", 16)
        mark("consts", "s_ld")

    @blk.gpsimd
    def _(gpsimd):
        wt(gpsimd, "consts")
        inc(gpsimd.dma_start(out=ident_bf[:], in_=cdr["ident"]), "s_gld", 16)
        inc(gpsimd.dma_start(out=w_dt_sb[:], in_=wdr["w_dt_fm"]), "s_gld", 16)
        inc(gpsimd.dma_start(out=b_out_sb[:], in_=wdr["w_out_aug"][D:D + 1, :]), "s_gld", 16)
        for k in range(8):
            kk = slice(k * 128, (k + 1) * 128)
            inc(gpsimd.dma_start(out=w_xp_sb[:, k * 128:(k + 1) * 128],
                                 in_=wdr["w_xp_t"][kk, :]), "s_gld", 16)
            inc(gpsimd.dma_start(out=w_res_sb[:, k * 64:(k + 1) * 64],
                                 in_=wdr["w_res_t"][kk, :]), "s_gld", 16)
            inc(gpsimd.dma_start(out=gate_sb[:, k * E:(k + 1) * E],
                                 in_=wdr["gate_t"][kk, :]), "s_gld", 16)
            inc(gpsimd.dma_start(out=w_out_sb[:, k * 1024:(k + 1) * 1024],
                                 in_=wdr["w_out_aug"][kk, :]), "s_gld", 16)
        mark("resw", "s_gld")

    @blk.scalar
    def _(scalar):
        wt(scalar, "consts")
        scalar.activation(arow[:], arow[:], AF.Exp)
        inc(scalar.mul(arow[:], arow[:], -1.0), "s_a")
        mark("arow", "s_a")

    @blk.sync
    def _(sync):
        wt(sync, "arow")
        inc(sync.dma_start(out=a_dram.rearrange("s one -> one s"), in_=arow[:]), "s_ld", 16)
        sync.wait_ge(sems["s_ld"], cnt["s_ld"])
        for s in range(16):
            inc(sync.dma_start(out=acol[8 * s:8 * (s + 1), 0:1],
                               in_=a_dram[s:s + 1, 0:1].to_broadcast([8, 1])), "s_ld", 16)
        mark("acol", "s_ld")

    pa = ExitStack()

    def asb(name, shape, dt=F32):
        return pa.enter_context(nc.sbuf_tensor(name, shape, dt))

    def aps(name, shape, dt=F32):
        return pa.enter_context(nc.psum_tensor(name, shape, dt))

    w_in_sb = asb("w_in_sb", [128, 8 * 2048], BF16)
    xt = asb("xt", [128, TG * D])
    stat = asb("stat", [128, 3 * TG])
    sqs = asb("sqs", [128, D])
    xn_tm = asb("xn_tm", [128, TG * D], BF16)
    xn_fm = asb("xn_fm", [128, 8 * TILE], BF16)
    xin_fm32 = asb("xin_fm32", [128, 8 * TILE])
    xin_fm = asb("xin_fm", [128, 8 * TILE], BF16)
    zs_fm = asb("zs_fm", [128, 8 * TILE])
    dtr_aug = asb("dtr_aug", [128, TILE], BF16)
    bcsb = asb("bcsb", [64, TILE])
    bccol_s = asb("bccol_s", [128, 4 * TILE])
    dt_f = asb("dt_f", [128, 8 * TILE])
    spa = asb("spa", [128, 2 * TILE])
    spb = asb("spb", [128, 2 * TILE])
    dtx_f = asb("dtx_f", [128, 8 * TILE])
    xvt = asb("xvt", [16, 2 * TILE])
    xvsc = asb("xvsc", [16, 2 * TILE])
    ps_a = aps("ps_a", [128, 512])
    ps_b = aps("ps_b", [128, 512])
    ps_c = aps("ps_c", [128, 512])
    ps_xc = aps("ps_xc", [64, 512])

    @blk.gpsimd
    def _(gpsimd):
        wt(gpsimd, "resw")
        for k in range(8):
            kk = slice(k * 128, (k + 1) * 128)
            inc(gpsimd.dma_start(out=w_in_sb[:, k * 2048:(k + 1) * 2048],
                                 in_=wdr["w_in_t"][kk, :]), "s_gld", 16)
        mark("winld", "s_gld")

    def phase_a(tile):
        t0 = tile * TILE
        tg = "_%d" % tile

        @blk.sync
        def _(sync):
            if tile == 0:
                wt(sync, "acol")
            else:
                wt(sync, "pa" + "_%d" % (tile - 1))
            for g in range(TG):
                inc(sync.dma_start(out=xt[:, g * D:(g + 1) * D],
                                   in_=x_tm[t0 + g * 128:t0 + (g + 1) * 128, :]), "s_ld", 16)
            mark("xld" + tg, "s_ld")

        @blk.vector
        def _(vector):
            wt(vector, "xld" + tg)
            for g in range(TG):
                vector.tensor_scalar(sqs[:, 0:D], xt[:, g * D:(g + 1) * D], 1.0, 0.0,
                                     ALU.mult, ALU.add, accum_out=stat[:, 3 * g:3 * g + 1])
            inc(vector.tensor_copy(sqs[:, 0:1], stat[:, 0:1]), "s_v")
            mark("asx" + tg, "s_v")

        @blk.scalar
        def _(scalar):
            wt(scalar, "xld" + tg)
            wt(scalar, "asx" + tg)
            for g in range(TG):
                scalar.activation(sqs[:, 0:D], xt[:, g * D:(g + 1) * D], AF.Square,
                                  accum_out=stat[:, 3 * g + 1:3 * g + 2])
            inc(scalar.copy(sqs[:, 0:1], stat[:, 0:1]), "s_a")
            mark("asxx" + tg, "s_a")

        @blk.vector
        def _(vector):
            wt(vector, "asxx" + tg)
            for g in range(TG):
                mcol = stat[:, 3 * g:3 * g + 1]
                vcol = stat[:, 3 * g + 1:3 * g + 2]
                m2col = stat[:, 3 * g + 2:3 * g + 3]
                inc(vector.tensor_scalar(mcol, mcol, 1.0 / D, None, ALU.mult), "s_v")
                vector.wait_ge(sems["s_v"], cnt["s_v"])
                inc(vector.tensor_tensor(m2col, mcol, mcol, ALU.mult), "s_v")
                vector.wait_ge(sems["s_v"], cnt["s_v"])
                vector.scalar_tensor_tensor(vcol, vcol, 1.0 / D, m2col, ALU.mult,
                                            ALU.subtract)
            inc(vector.tensor_copy(sqs[:, 0:1], stat[:, 0:1]), "s_v")
            mark("avar" + tg, "s_v")

        @blk.scalar
        def _(scalar):
            wt(scalar, "avar" + tg)
            for g in range(TG):
                vcol = stat[:, 3 * g + 1:3 * g + 2]
                scalar.activation(vcol, vcol, AF.Ln, bias=LNEPS)
                scalar.activation(vcol, vcol, AF.Exp, scale=-0.5)
            inc(scalar.copy(sqs[:, 0:1], stat[:, 0:1]), "s_a")
            mark("arstd" + tg, "s_a")

        @blk.vector
        def _(vector):
            wt(vector, "arstd" + tg)
            for g in range(TG):
                vector.tensor_scalar(xn_tm[:, g * D:(g + 1) * D], xt[:, g * D:(g + 1) * D],
                                     stat[:, 3 * g:3 * g + 1], stat[:, 3 * g + 1:3 * g + 2],
                                     ALU.subtract, ALU.mult)
            inc(vector.tensor_copy(sqs[:, 0:1], stat[:, 0:1]), "s_v")
            mark("axn" + tg, "s_v")

        if dbg:
            @blk.sync
            def _(sync):
                wt(sync, "axn" + tg)
                inc(sync.dma_start(
                    out=xn_dbg[t0:t0 + TILE, :].rearrange("(g p) d -> p g d", p=128),
                    in_=xn_tm[:].rearrange("p (g d) -> p g d", d=D)), "s_ld", 16)
                if tile == 0:
                    inc(sync.dma_start(out=stat_dbg, in_=stat[:]), "s_ld", 16)

        vbase = cnt["s_v"]

        @blk.tensor
        def _(tensor):
            wt(tensor, "axn" + tg)
            if tile == 0:
                wt(tensor, "resw")
            for i in range(32):
                pst = ps_a if i % 2 == 0 else ps_b
                if i >= 2:
                    tensor.wait_ge(sems["s_v"], vbase + i - 1)
                g, k = divmod(i, 8)
                inc(tensor.transpose(pst[:].bitcast(BF16)[:, 0:128],
                                     xn_tm[:, g * D + k * 128:g * D + (k + 1) * 128],
                                     ident_bf[:]), "s_p")
            mark("atr" + tg, "s_p")

        pbase = ev["atr" + tg][1] - 32

        @blk.vector
        def _(vector):
            for i in range(32):
                pst = ps_a if i % 2 == 0 else ps_b
                vector.wait_ge(sems["s_p"], pbase + i + 1)
                g, k = divmod(i, 8)
                inc(vector.tensor_copy(xn_fm[:, k * TILE + g * 128:k * TILE + (g + 1) * 128],
                                       pst[:].bitcast(BF16)[:, 0:128]), "s_v")
            mark("axnfm" + tg, "s_v")

        if dbg:
            @blk.sync
            def _(sync):
                wt(sync, "axnfm" + tg)
                inc(sync.dma_start(out=xnfm_dbg[:, tile * 4096:(tile + 1) * 4096],
                                   in_=xn_fm[:]), "s_ld", 16)

        abase = cnt["s_a"]

        @blk.tensor
        def _(tensor):
            wt(tensor, "axnfm" + tg)
            if tile == 0:
                wt(tensor, "winld")
            for o in range(16):
                pst = ps_a if o % 2 == 0 else ps_b
                if o >= 2:
                    tensor.wait_ge(sems["s_a"], abase + o - 1)
                for k in range(8):
                    inc(tensor.matmul(pst[:],
                                      w_in_sb[:, k * 2048 + o * 128:k * 2048 + (o + 1) * 128],
                                      xn_fm[:, k * TILE:(k + 1) * TILE],
                                      start=(k == 0), stop=(k == 7)), "s_p")
            mark("axz" + tg, "s_p")

        pbase2 = ev["axz" + tg][1] - 128

        @blk.scalar
        def _(scalar):
            for o in range(16):
                pst = ps_a if o % 2 == 0 else ps_b
                scalar.wait_ge(sems["s_p"], pbase2 + (o + 1) * 8)
                if o < 8:
                    inc(scalar.activation(xin_fm32[:, o * TILE:(o + 1) * TILE], pst[:],
                                          AF.Silu), "s_a")
                else:
                    inc(scalar.activation(zs_fm[:, (o - 8) * TILE:(o - 7) * TILE], pst[:],
                                          AF.Silu), "s_a")
            mark("asilu" + tg, "s_a")

        @blk.vector
        def _(vector):
            wt(vector, "asilu" + tg)
            for o in range(8):
                inc(vector.tensor_copy(xin_fm[:, o * TILE:(o + 1) * TILE],
                                       xin_fm32[:, o * TILE:(o + 1) * TILE]), "s_v")
            inc(vector.memset(dtr_aug[64:65, :], 1.0), "s_v")
            mark("axinbf" + tg, "s_v")

        @blk.sync
        def _(sync):
            wt(sync, "asilu" + tg)
            for o in range(8):
                inc(sync.dma_start(out=zsil_dram[o * 128:(o + 1) * 128, t0:t0 + TILE],
                                   in_=zs_fm[:, o * TILE:(o + 1) * TILE]), "s_ld", 16)
            mark("azsil" + tg, "s_ld")

        @blk.tensor
        def _(tensor):
            wt(tensor, "axinbf" + tg)
            for k in range(8):
                inc(tensor.matmul(ps_c[:], w_xp_sb[:, k * 128:(k + 1) * 128],
                                  xin_fm[:, k * TILE:(k + 1) * TILE],
                                  start=(k == 0), stop=(k == 7)), "s_p")
            mark("axp" + tg, "s_p")

        @blk.scalar
        def _(scalar):
            wt(scalar, "axp" + tg)
            inc(scalar.activation(dtr_aug[0:64, :].bitcast(BF16), ps_c[0:64, :],
                                  AF.Identity), "s_a")
            inc(scalar.activation(bcsb[:], ps_c[64:128, :], AF.Identity), "s_a")
            mark("adtr" + tg, "s_a")

        ABCV0 = cnt["s_v"]

        @blk.tensor
        def _(tensor):
            wt(tensor, "adtr" + tg)
            for j, nm in enumerate(["bre", "bim", "cre", "cim"]):
                if j >= 1:
                    tensor.wait_ge(sems["s_v"], ABCV0 + j)
                inc(tensor.matmul(ps_c[:], cs["bcm_" + nm][:], bcsb[:],
                                  start=True, stop=True), "s_p")
            mark("abcmm" + tg, "s_p")

        pbase3 = ev["abcmm" + tg][1] - 4

        @blk.vector
        def _(vector):
            for j in range(4):
                vector.wait_ge(sems["s_p"], pbase3 + j + 1)
                inc(vector.tensor_copy(bccol_s[:, j * TILE:(j + 1) * TILE], ps_c[:]), "s_v")
            mark("abcv" + tg, "s_v")

        @blk.sync
        def _(sync):
            wt(sync, "abcv" + tg)
            for j in range(4):
                inc(sync.dma_start(out=bc_dram[j * 128:(j + 1) * 128, t0:t0 + TILE],
                                   in_=bccol_s[:, j * TILE:(j + 1) * TILE]), "s_ld", 16)
            mark("abcst" + tg, "s_ld")

        VDT0 = cnt["s_v"]
        abase2 = cnt["s_a"]

        @blk.tensor
        def _(tensor):
            wt(tensor, "adtr" + tg)
            wt(tensor, "abcv" + tg)
            for c in range(8):
                pst = ps_a if c % 2 == 0 else ps_b
                if c >= 2:
                    tensor.wait_ge(sems["s_v"], VDT0 + 4 * (c - 2) + 2)
                inc(tensor.matmul(pst[:], w_dt_sb[:, c * 128:(c + 1) * 128],
                                  dtr_aug[0:65, :], start=True, stop=True), "s_p")
            mark("adtmm" + tg, "s_p")

        pbase4 = ev["adtmm" + tg][1] - 8

        @blk.scalar
        def _(scalar):
            for c in range(8):
                par = c % 2
                scalar.wait_ge(sems["s_v"], VDT0 + 4 * c + 2)
                if c >= 2:
                    scalar.wait_ge(sems["s_v"], VDT0 + 4 * (c - 2) + 3)
                scalar.activation(spb[:, par * TILE:(par + 1) * TILE],
                                  spa[:, par * TILE:(par + 1) * TILE], AF.Exp)
                inc(scalar.activation(spb[:, par * TILE:(par + 1) * TILE],
                                      spb[:, par * TILE:(par + 1) * TILE],
                                      AF.Ln, bias=1.0), "s_a")
            mark("adtsp" + tg, "s_a")

        @blk.vector
        def _(vector):
            for c in range(8):
                par = c % 2
                pst = ps_a if c % 2 == 0 else ps_b
                vector.wait_ge(sems["s_p"], pbase4 + c + 1)
                inc(vector.tensor_scalar(dt_f[:, c * TILE:(c + 1) * TILE], pst[:],
                                         0.0, None, ALU.max), "s_v")
                vector.wait_ge(sems["s_v"], cnt["s_v"])
                inc(vector.scalar_tensor_tensor(spa[:, par * TILE:(par + 1) * TILE],
                                                dt_f[:, c * TILE:(c + 1) * TILE],
                                                -2.0, pst[:],
                                                ALU.mult, ALU.add), "s_v")
                vector.wait_ge(sems["s_a"], abase2 + c + 1)
                dtc = dt_f[:, c * TILE:(c + 1) * TILE]
                inc(vector.tensor_tensor(dtc, dtc, spb[:, par * TILE:(par + 1) * TILE],
                                         ALU.add), "s_v")
                inc(vector.tensor_tensor(dtx_f[:, c * TILE:(c + 1) * TILE], dtc,
                                         xin_fm32[:, c * TILE:(c + 1) * TILE], ALU.mult), "s_v")
            mark("adtx" + tg, "s_v")

        @blk.sync
        def _(sync):
            wt(sync, "adtx" + tg)
            for c in range(8):
                inc(sync.dma_start(out=dt_dram[c * 128:(c + 1) * 128, t0:t0 + TILE],
                                   in_=dt_f[:, c * TILE:(c + 1) * TILE]), "s_ld", 16)
                inc(sync.dma_start(out=dtx_dram[c * 128:(c + 1) * 128, t0:t0 + TILE],
                                   in_=dtx_f[:, c * TILE:(c + 1) * TILE]), "s_ld", 16)
            mark("adtst" + tg, "s_ld")

        @blk.tensor
        def _(tensor):
            wt(tensor, "axinbf" + tg)
            wt(tensor, "adtmm" + tg)
            for k in range(8):
                inc(tensor.matmul(ps_xc[:], w_res_sb[:, k * 64:(k + 1) * 64],
                                  xin_fm[:, k * TILE:(k + 1) * TILE],
                                  start=(k == 0), stop=(k == 7)), "s_p")
            mark("axc" + tg, "s_p")

        @blk.vector
        def _(vector):
            wt(vector, "axc" + tg)
            xv_re = xvt[:].rearrange("p (t r) -> p t r", r=2)[:, :, 0:1]
            xv_im = xvt[:].rearrange("p (t r) -> p t r", r=2)[:, :, 1:2]
            inc(vector.tensor_scalar(xv_re, ps_xc[0:16, :].unsqueeze(2),
                                     1e-10, None, ALU.add), "s_v")
            inc(vector.tensor_copy(xv_im, ps_xc[32:48, :].unsqueeze(2)), "s_v")
            mark("axv1" + tg, "s_v")

        @blk.scalar
        def _(scalar):
            wt(scalar, "axv1" + tg)
            inc(scalar.activation(xvsc[:], xvt[:], AF.Square), "s_a")
            mark("axv2" + tg, "s_a")

        @blk.vector
        def _(vector):
            wt(vector, "axv2" + tg)
            inc(vector.tensor_reduce(xvsc[:, 0:TILE],
                                     xvsc[:].rearrange("p (t r) -> p t r", r=2),
                                     AX.X, ALU.add), "s_v")
            mark("axv3" + tg, "s_v")

        @blk.scalar
        def _(scalar):
            wt(scalar, "axv3" + tg)
            scalar.activation(xvsc[:, 0:TILE], xvsc[:, 0:TILE], AF.Ln, bias=1e-30)
            inc(scalar.activation(xvsc[:, 0:TILE], xvsc[:, 0:TILE], AF.Exp, scale=-0.5), "s_a")
            mark("axv4" + tg, "s_a")

        @blk.vector
        def _(vector):
            wt(vector, "axv4" + tg)
            xv_re = xvt[:].rearrange("p (t r) -> p t r", r=2)[:, :, 0:1]
            xv_im = xvt[:].rearrange("p (t r) -> p t r", r=2)[:, :, 1:2]
            rx = xvsc[:, 0:TILE].unsqueeze(2)
            inc(vector.tensor_tensor(xv_re, xv_re, rx, ALU.mult), "s_v")
            inc(vector.tensor_tensor(xv_im, xv_im, rx, ALU.mult), "s_v")
            mark("axv5" + tg, "s_v")

        @blk.sync
        def _(sync):
            wt(sync, "axv5" + tg)
            inc(sync.dma_start(out=xvn_dram[:, 2 * t0:2 * (t0 + TILE)], in_=xvt[:]), "s_ld", 16)
            mark("pa" + tg, "s_ld")

    for tile in range(NT):
        phase_a(tile)
    mark("phaseA", "s_ld")
    pa.close()

    pb_scope = ExitStack()

    def bsb(name, shape, dt=F32):
        return pb_scope.enter_context(nc.sbuf_tensor(name, shape, dt))

    def bps(name, shape, dt=F32):
        return pb_scope.enter_context(nc.psum_tensor(name, shape, dt))

    h2 = bsb("h2", [128, 512])
    alph = bsb("alph", [128, 2 * 512])
    relt = bsb("relt", [128, 256])
    sqt = bsb("sqt", [128, 256])
    Mt = bsb("Mt", [128, 128])
    ddt = bsb("ddt", [128, 128])
    lnt = bsb("lnt", [128, 128])
    ldt = bsb("ldt", [128, 128])
    t3t = bsb("t3t", [128, 128])
    rt = bsb("rt", [128, 128])
    compt = bsb("compt", [128, 256])
    clipt = bsb("clipt", [128, 256])
    davg = bsb("davg", [16, 2])
    dtiny = bsb("dtiny", [16, 32])
    surf = bsb("surf", [16, 2])
    yrhs = bsb("yrhs", [128, 16])
    bcc = bsb("bcc", [128, 2 * 4 * SUB])
    xvns = bsb("xvns", [16, 2 * 2 * SUB])
    slab_t = bsb("slab_t", [8, 2 * SUB * 128])
    slab_x = bsb("slab_x", [8, 2 * SUB * 128])
    ysb = bsb("ysb", [128, 2 * 512])

    ps_bc2 = [bps("ps_bc2_%d" % i, [128, 1024]) for i in range(2)]
    ps_y = bps("ps_y", [128, 512])
    ps_red = bps("ps_red", [16, 256])
    ps_surf = bps("ps_surf", [128, 2])

    NB = T // 4
    V0 = cnt["s_v"]
    A0 = cnt["s_a"]
    G0 = cnt["s_g"]
    P0 = cnt["s_p"]
    PB0 = cnt["s_pb"]
    AL0 = cnt["s_al"]
    GLD0 = cnt["s_sc"]
    YE0 = cnt["s_ye"]

    OV = {"yrhs": 2, "hdec": 4, "h": 8, "dd": 9, "t3": 10, "clip": 12, "red": 13,
          "ssq": 16, "dd2": 22, "u": 25, "surf": 32}
    V_PER = 32
    OA = {"sq": 1, "lm": 2, "ld": 3, "r": 4, "lnn": 5, "e1": 6, "lnd": 7, "exps": 8}
    A_PER = 8
    OG = {"rel": 1, "M": 2, "cim": 3}
    G_PER = 3
    OP = {"surfb": 1, "y": 3, "smask": 4}
    P_PER = 4
    GLD_PER = 64

    def vv(t, name):
        return V0 + V_PER * t + OV[name]

    def aa(t, name):
        return A0 + A_PER * t + OA[name]

    def gg(t, name):
        return G0 + G_PER * t + OG[name]

    def pp(t, name):
        return P0 + P_PER * t + OP[name]

    def pbn(batch):
        return PB0 + 2 * (batch + 1)

    def aln(batch):
        return AL0 + batch + 1

    def gldn(sub):
        return GLD0 + GLD_PER * (sub + 1)

    OUTY0 = cnt["s_out"]

    def ystn(rnd):
        return OUTY0 + 16 * (rnd + 1)

    def yen(rnd):
        return YE0 + rnd + 1

    def dtb_view(t):
        par = (t // 4) % 2
        return ps_bc2[par][:, (t % 4) * 128:(t % 4) * 128 + 128]

    def dtxb_view(t):
        par = (t // 4) % 2
        return ps_bc2[par][:, 512 + (t % 4) * 128:512 + (t % 4) * 128 + 128]

    def bcc_col(t, j):
        par = (t // SUB) % 2
        return bcc[:, par * 4 * SUB + j * SUB + (t % SUB):par * 4 * SUB + j * SUB + (t % SUB) + 1]

    def xvn_view(t):
        par = (t // SUB) % 2
        o = par * 2 * SUB + 2 * (t % SUB)
        return xvns[:, o:o + 2]

    def h_re(t):
        return h2[:, (t % 2) * 256:(t % 2) * 256 + 128]

    def h_im(t):
        return h2[:, (t % 2) * 256 + 128:(t % 2) * 256 + 256]

    def h_full(t):
        return h2[:, (t % 2) * 256:(t % 2) * 256 + 256]

    @blk.sync
    def _(sync):
        wt(sync, "phaseA")
        inc(sync.dma_start(out=dtiny[:, 0:2], in_=cdr["d0"]), "s_ld", 16)
        mark("d0ld", "s_ld")

    @blk.sync
    def _(sync):
        wt(sync, "phaseA")
        for sub in range(NSUB):
            par = sub % 2
            if sub >= 2:
                sync.wait_ge(sems["s_v"], V0 + V_PER * ((sub - 1) * SUB))
                sync.wait_ge(sems["s_pb"], PB0 + 2 * ((sub - 1) * SUB // 4))
            t0 = sub * SUB
            sv = dt_dram[:, t0:t0 + SUB].rearrange("(k dp) t -> k dp t", k=8)
            ov = slab_t[:, par * SUB * 128:(par + 1) * SUB * 128].rearrange(
                "k (dp t) -> k dp t", t=SUB)
            inc(sync.dma_start(out=ov, in_=sv), "s_sc", 16)
            sv2 = dtx_dram[:, t0:t0 + SUB].rearrange("(k dp) t -> k dp t", k=8)
            ov2 = slab_x[:, par * SUB * 128:(par + 1) * SUB * 128].rearrange(
                "k (dp t) -> k dp t", t=SUB)
            inc(sync.dma_start(out=ov2, in_=sv2), "s_sc", 16)
            bv = bc_dram[:, t0:t0 + SUB].rearrange("(j p) t -> p j t", j=4)
            obv = bcc[:, par * 4 * SUB:(par + 1) * 4 * SUB].rearrange(
                "p (j t) -> p j t", t=SUB)
            inc(sync.dma_start(out=obv, in_=bv), "s_sc", 16)
            inc(sync.dma_start(out=xvns[:, par * 2 * SUB:(par + 1) * 2 * SUB],
                               in_=xvn_dram[:, 2 * t0:2 * (t0 + SUB)]), "s_sc", 16)
        mark("scanloads", "s_sc")

    PRO_V = [0]

    @blk.vector
    def _(vector):
        wt(vector, "d0ld")
        vector.wait_ge(sems["s_sc"], gldn(0))
        vector.memset(h2[:, 256:512], 0.0)
        vector.memset(ones_bf[:], 1.0)
        d0c = dtiny[:, 0:2]
        inc(vector.tensor_tensor(dtiny[:, 4:6], d0c, d0c, ALU.mult), "s_v")
        vector.wait_ge(sems["s_v"], cnt["s_v"])
        inc(vector.tensor_reduce(dtiny[:, 6:7], dtiny[:, 4:6], AX.X, ALU.add), "s_v")
        mark("pro_ssq", "s_v")

    @blk.scalar
    def _(scalar):
        wt(scalar, "pro_ssq")
        inc(scalar.activation(dtiny[:, 7:8], dtiny[:, 6:7], AF.Ln, bias=1e-8), "s_a")
        mark("pro_lnn", "s_a")

    @blk.vector
    def _(vector):
        wt(vector, "pro_lnn")
        inc(vector.tensor_tensor(dtiny[:, 10:12], dtiny[:, 0:2], cs["epsC"][:],
                                 ALU.add), "s_v")
        vector.wait_ge(sems["s_v"], cnt["s_v"])
        inc(vector.tensor_tensor(dtiny[:, 12:14], dtiny[:, 10:12], dtiny[:, 10:12],
                                 ALU.mult), "s_v")
        inc(vector.tensor_tensor(dtiny[:, 14:16], xvns[:, 0:2], dtiny[:, 10:12],
                                 ALU.mult), "s_v")
        vector.wait_ge(sems["s_v"], cnt["s_v"])
        inc(vector.tensor_reduce(dtiny[:, 16:18], dtiny[:, 12:16].rearrange(
            "p (a b) -> p a b", a=2), AX.X, ALU.add), "s_v")
        inc(vector.tensor_copy(dtiny[:, 2:3], dtiny[:, 0:1]), "s_v")
        mark("pro_dd2", "s_v")

    @blk.scalar
    def _(scalar):
        wt(scalar, "pro_dd2")
        inc(scalar.activation(dtiny[:, 18:19], dtiny[:, 16:17], AF.Ln, bias=1e-20), "s_a")
        mark("pro_lnd", "s_a")

    @blk.vector
    def _(vector):
        wt(vector, "pro_lnd")
        vector.tensor_scalar(dtiny[:, 19:20], dtiny[:, 18:19], -0.5, None, ALU.mult)
        vector.tensor_scalar(dtiny[:, 20:21], dtiny[:, 7:8], 0.25, None, ALU.mult)
        inc(vector.tensor_scalar(dtiny[:, 21:22], dtiny[:, 7:8], 0.5, None, ALU.mult), "s_v")
        mark("pro_u", "s_v")

    @blk.scalar
    def _(scalar):
        wt(scalar, "pro_u")
        inc(scalar.activation(dtiny[:, 22:25], dtiny[:, 19:22], AF.Exp), "s_a")
        mark("pro_exps", "s_a")

    @blk.vector
    def _(vector):
        wt(vector, "pro_exps")
        inc(vector.tensor_tensor(dtiny[:, 25:27], dtiny[:, 23:25], cs["clim"][:],
                                 ALU.min), "s_v")
        vector.wait_ge(sems["s_v"], cnt["s_v"])
        inc(vector.tensor_scalar(dtiny[:, 27:28], dtiny[:, 26:27], 1e-8, None,
                                 ALU.add), "s_v")
        vector.wait_ge(sems["s_v"], cnt["s_v"])
        inc(vector.reciprocal(dtiny[:, 28:29], dtiny[:, 27:28]), "s_v")
        inc(vector.scalar_tensor_tensor(dtiny[:, 29:30], dtiny[:, 17:18], 0.5,
                                        dtiny[:, 22:23], ALU.mult, ALU.mult), "s_v")
        vector.wait_ge(sems["s_v"], cnt["s_v"])
        inc(vector.scalar_tensor_tensor(dtiny[:, 30:31], dtiny[:, 29:30], 0.5,
                                        dtiny[:, 25:26], ALU.add, ALU.mult), "s_v")
        vector.wait_ge(sems["s_v"], cnt["s_v"])
        inc(vector.tensor_tensor(dtiny[:, 31:32], dtiny[:, 30:31],
                                 dtiny[:, 28:29], ALU.mult), "s_v")
        vector.wait_ge(sems["s_v"], cnt["s_v"])
        inc(vector.tensor_scalar(surf[:], dtiny[:, 0:2], dtiny[:, 31:32], None,
                                 ALU.mult), "s_v")
        mark("pro_surf", "s_v")

    PROSURF = ev["pro_surf"][1]

    V0 = cnt["s_v"]
    A0 = cnt["s_a"]
    G0 = cnt["s_g"]
    P0 = cnt["s_p"]

    @blk.tensor
    def _(tensor):
        tensor.wait_ge(sems["s_v"], PROSURF)
        for t in range(T):
            if t % 4 == 0:
                batch = t // 4
                par = batch % 2
                tensor.wait_ge(sems["s_sc"], gldn(t // SUB))
                if batch >= 2:
                    tensor.wait_ge(sems["s_al"], aln(batch - 2))
                    tensor.wait_ge(sems["s_v"], vv(4 * (batch - 1) - 1, "h"))
                spar = (t // SUB) % 2
                svt = slab_t[:, spar * SUB * 128:(spar + 1) * SUB * 128].rearrange(
                    "k (dp tt) -> k dp tt", tt=SUB)[:, :, (t % SUB):(t % SUB) + 4]
                svx = slab_x[:, spar * SUB * 128:(spar + 1) * SUB * 128].rearrange(
                    "k (dp tt) -> k dp tt", tt=SUB)[:, :, (t % SUB):(t % SUB) + 4]
                svt = svt.transpose([0, 2, 1])
                svx = svx.transpose([0, 2, 1])
                inc(tensor.matmul(ps_bc2[par][:, 0:512], cs["mask8"][:], svt,
                                  start=True, stop=True), "s_pb")
                inc(tensor.matmul(ps_bc2[par][:, 512:1024], cs["mask8"][:], svx,
                                  start=True, stop=True), "s_pb")
            if t > 0:
                tensor.wait_ge(sems["s_v"], vv(t - 1, "surf"))
            inc(tensor.matmul(ps_surf[:], cs["repmask16"][:], surf[:],
                              start=True, stop=True), "s_p")
            if t % YC == 0 and t // YC >= 1:
                tensor.wait_ge(sems["s_ye"], yen(t // YC - 1))
            tensor.wait_ge(sems["s_v"], vv(t, "h"))
            yslot = ps_y[:, (t % YC) * 8:(t % YC) * 8 + 8]
            inc(tensor.matmul(yslot, h_re(t), yrhs[:, 0:8], start=True, stop=False), "s_p")
            inc(tensor.matmul(yslot, h_im(t), yrhs[:, 8:16], start=False, stop=True), "s_p")
            tensor.wait_ge(sems["s_v"], vv(t, "clip"))
            inc(tensor.matmul(ps_red[:], cs["smask16"][:], clipt[:],
                              start=True, stop=True), "s_p")
        mark("pe_scan", "s_p")

    @blk.scalar
    def _(scalar):
        scalar.wait_ge(sems["s_v"], PROSURF)
        for t in range(T):
            if t % 4 == 0:
                batch = t // 4
                par = batch % 2
                scalar.wait_ge(sems["s_pb"], pbn(batch))
                if batch >= 2:
                    scalar.wait_ge(sems["s_v"], vv(4 * (batch - 1) - 1, "hdec"))
                inc(scalar.activation(alph[:, par * 512:(par + 1) * 512],
                                      ps_bc2[par][:, 0:512], AF.Exp,
                                      scale=acol[:, 0:1]), "s_al")
            scalar.wait_ge(sems["s_g"], gg(t, "rel"))
            inc(scalar.activation(sqt[:], relt[:], AF.Square), "s_a")
            scalar.wait_ge(sems["s_g"], gg(t, "M"))
            inc(scalar.activation(lnt[:], Mt[:], AF.Ln, bias=1e-8), "s_a")
            scalar.wait_ge(sems["s_v"], vv(t, "dd"))
            inc(scalar.activation(ldt[:], ddt[:], AF.Ln, bias=1e-20), "s_a")
            scalar.wait_ge(sems["s_v"], vv(t, "t3"))
            inc(scalar.activation(rt[:], t3t[:], AF.Exp, scale=1.0 / 3.0), "s_a")
            scalar.wait_ge(sems["s_v"], vv(t, "ssq"))
            inc(scalar.activation(dtiny[:, 7:8], dtiny[:, 6:7], AF.Ln, bias=1e-8), "s_a")
            inc(scalar.activation(dtiny[:, 8:9], dtiny[:, 7:8], AF.Exp,
                                  scale=-0.5, bias=LN20), "s_a")
            scalar.wait_ge(sems["s_v"], vv(t, "dd2"))
            inc(scalar.activation(dtiny[:, 18:19], dtiny[:, 16:17], AF.Ln,
                                  bias=1e-20), "s_a")
            scalar.wait_ge(sems["s_v"], vv(t, "u"))
            inc(scalar.activation(dtiny[:, 22:25], dtiny[:, 19:22], AF.Exp), "s_a")
            if t % YC == YC - 1:
                scalar.wait_ge(sems["s_p"], pp(t, "y"))
                rnd = t // YC
                if rnd >= 2:
                    scalar.wait_ge(sems["s_out"], ystn(rnd - 2))
                inc(scalar.activation(ysb[:, (rnd % 2) * 512:(rnd % 2) * 512 + 512],
                                      ps_y[:], AF.Identity), "s_ye")
        mark("act_scan", "s_a")

    @blk.gpsimd
    def _(gpsimd):
        gpsimd.wait_ge(sems["s_v"], PROSURF)
        for t in range(T):
            gpsimd.wait_ge(sems["s_v"], vv(t, "hdec"))
            inc(gpsimd.tensor_tensor(relt[:], h_full(t + 1), compt[:], ALU.subtract),
                "s_g")
            gpsimd.wait_ge(sems["s_a"], aa(t, "sq"))
            inc(gpsimd.tensor_tensor(Mt[:], sqt[:, 0:128], sqt[:, 128:256], ALU.add),
                "s_g")
            gpsimd.wait_ge(sems["s_a"], aa(t, "r"))
            inc(gpsimd.tensor_tensor(compt[:, 128:256], relt[:, 128:256], rt[:],
                                     ALU.mult), "s_g")
            if t % YC == YC - 1:
                rnd = t // YC
                gpsimd.wait_ge(sems["s_ye"], yen(rnd))
                inc(gpsimd.dma_start(out=y_dram[:, rnd * YC * 8:(rnd + 1) * YC * 8],
                                     in_=ysb[:, (rnd % 2) * 512:(rnd % 2) * 512 + 512]),
                    "s_out", 16)
        mark("pool_scan", "s_g")
        mark("ylast", "s_out")


    @blk.vector
    def _(vector):
        for t in range(T):
            vector.wait_ge(sems["s_sc"], gldn(t // SUB))
            if t > 0:
                vector.wait_ge(sems["s_p"], pp(t - 1, "y"))
            inc(vector.tensor_scalar(yrhs[:, 0:8], cs["c8maskT"][:], bcc_col(t, 2),
                                     None, ALU.mult), "s_v")
            inc(vector.tensor_scalar(yrhs[:, 8:16], cs["c8maskT"][:], bcc_col(t, 3),
                                     None, ALU.mult), "s_v")
            vector.wait_ge(sems["s_al"], aln(t // 4))
            if t > 0:
                vector.wait_ge(sems["s_g"], gg(t - 1, "rel"))
            apar = (t // 4) % 2
            av = alph[:, apar * 512 + (t % 4) * 128:apar * 512 + (t % 4) * 128 + 128]
            inc(vector.tensor_tensor(compt[:, 0:128], h_re(t + 1), av, ALU.mult), "s_v")
            inc(vector.tensor_tensor(compt[:, 128:256], h_im(t + 1), av, ALU.mult), "s_v")
            vector.wait_ge(sems["s_p"], pp(t, "surfb"))
            vector.wait_ge(sems["s_v"], cnt["s_v"])
            inc(vector.scalar_tensor_tensor(h_re(t), dtb_view(t), ps_surf[:, 0:1],
                                            compt[:, 0:128], ALU.mult, ALU.add), "s_v")
            inc(vector.scalar_tensor_tensor(h_im(t), dtb_view(t), ps_surf[:, 1:2],
                                            compt[:, 128:256], ALU.mult, ALU.add), "s_v")
            vector.wait_ge(sems["s_v"], cnt["s_v"])
            inc(vector.scalar_tensor_tensor(h_re(t), dtxb_view(t), bcc_col(t, 0),
                                            h_re(t), ALU.mult, ALU.add), "s_v")
            inc(vector.scalar_tensor_tensor(h_im(t), dtxb_view(t), bcc_col(t, 1),
                                            h_im(t), ALU.mult, ALU.add), "s_v")
            vector.wait_ge(sems["s_g"], gg(t, "M"))
            inc(vector.scalar_tensor_tensor(ddt[:], relt[:, 0:128], 2e-10, Mt[:],
                                            ALU.mult, ALU.add), "s_v")
            vector.wait_ge(sems["s_a"], aa(t, "ld"))
            inc(vector.scalar_tensor_tensor(t3t[:], ldt[:], -1.5, lnt[:],
                                            ALU.mult, ALU.add), "s_v")
            vector.wait_ge(sems["s_a"], aa(t, "r"))
            inc(vector.scalar_tensor_tensor(compt[:, 0:128], relt[:, 0:128], 1e-10,
                                            rt[:], ALU.add, ALU.mult), "s_v")
            vector.wait_ge(sems["s_g"], gg(t, "cim"))
            vector.wait_ge(sems["s_v"], cnt["s_v"])
            inc(vector.tensor_scalar(clipt[:], compt[:], 10.0, -10.0,
                                     ALU.min, ALU.max), "s_v")
            vector.wait_ge(sems["s_p"], pp(t, "smask"))
            inc(vector.tensor_reduce(davg[:], ps_red[:].rearrange(
                "p (r dp) -> p r dp", r=2), AX.X, ALU.add), "s_v")
            if t > 0:
                vector.wait_ge(sems["s_a"], aa(t - 1, "lnn"))
            vector.wait_ge(sems["s_v"], cnt["s_v"])
            inc(vector.scalar_tensor_tensor(dtiny[:, 2:4], davg[:], CAVG,
                                            dtiny[:, 0:2], ALU.mult, ALU.add), "s_v")
            vector.wait_ge(sems["s_v"], cnt["s_v"])
            inc(vector.tensor_tensor(dtiny[:, 4:6], dtiny[:, 2:4], dtiny[:, 2:4],
                                     ALU.mult), "s_v")
            vector.wait_ge(sems["s_v"], cnt["s_v"])
            inc(vector.tensor_reduce(dtiny[:, 6:7], dtiny[:, 4:6], AX.X, ALU.add),
                "s_v")
            vector.wait_ge(sems["s_a"], aa(t, "e1"))
            vector.wait_ge(sems["s_sc"], gldn(min((t + 1) // SUB, NSUB - 1)))
            inc(vector.tensor_scalar(dtiny[:, 9:10], dtiny[:, 8:9], 1.0, None,
                                     ALU.min), "s_v")
            vector.wait_ge(sems["s_v"], cnt["s_v"])
            inc(vector.tensor_scalar(dtiny[:, 0:2], dtiny[:, 2:4], dtiny[:, 9:10],
                                     None, ALU.mult), "s_v")
            vector.wait_ge(sems["s_v"], cnt["s_v"])
            inc(vector.tensor_tensor(dtiny[:, 10:12], dtiny[:, 0:2], cs["epsC"][:],
                                     ALU.add), "s_v")
            vector.wait_ge(sems["s_v"], cnt["s_v"])
            inc(vector.tensor_tensor(dtiny[:, 12:14], dtiny[:, 10:12],
                                     dtiny[:, 10:12], ALU.mult), "s_v")
            inc(vector.tensor_tensor(dtiny[:, 14:16], xvn_view(t + 1),
                                     dtiny[:, 10:12], ALU.mult), "s_v")
            vector.wait_ge(sems["s_v"], cnt["s_v"])
            inc(vector.tensor_reduce(dtiny[:, 16:18], dtiny[:, 12:16].rearrange(
                "p (a b) -> p a b", a=2), AX.X, ALU.add), "s_v")
            vector.wait_ge(sems["s_a"], aa(t, "lnd"))
            inc(vector.tensor_scalar(dtiny[:, 19:20], dtiny[:, 18:19], -0.5, None,
                                     ALU.mult), "s_v")
            inc(vector.tensor_scalar(dtiny[:, 20:21], dtiny[:, 7:8], 0.25, None,
                                     ALU.mult), "s_v")
            inc(vector.tensor_scalar(dtiny[:, 21:22], dtiny[:, 7:8], 0.5, None,
                                     ALU.mult), "s_v")
            vector.wait_ge(sems["s_a"], aa(t, "exps"))
            inc(vector.tensor_tensor(dtiny[:, 25:27], dtiny[:, 23:25], cs["clim"][:],
                                     ALU.min), "s_v")
            vector.wait_ge(sems["s_v"], cnt["s_v"])
            inc(vector.tensor_scalar(dtiny[:, 27:28], dtiny[:, 26:27], 1e-8, None,
                                     ALU.add), "s_v")
            vector.wait_ge(sems["s_v"], cnt["s_v"])
            inc(vector.reciprocal(dtiny[:, 28:29], dtiny[:, 27:28]), "s_v")
            inc(vector.scalar_tensor_tensor(dtiny[:, 29:30], dtiny[:, 17:18], 0.5,
                                            dtiny[:, 22:23], ALU.mult, ALU.mult), "s_v")
            vector.wait_ge(sems["s_v"], cnt["s_v"])
            inc(vector.scalar_tensor_tensor(dtiny[:, 30:31], dtiny[:, 29:30], 0.5,
                                            dtiny[:, 25:26], ALU.add, ALU.mult), "s_v")
            vector.wait_ge(sems["s_v"], cnt["s_v"])
            inc(vector.tensor_tensor(dtiny[:, 31:32], dtiny[:, 30:31],
                                     dtiny[:, 28:29], ALU.mult), "s_v")
            vector.wait_ge(sems["s_v"], cnt["s_v"])
            inc(vector.tensor_scalar(surf[:], dtiny[:, 0:2], dtiny[:, 31:32], None,
                                     ALU.mult), "s_v")
        mark("dve_scan", "s_v")

    mark("scan_done_v", "s_v")
    mark("scan_done_a", "s_a")
    mark("scan_done_g", "s_g")
    mark("scan_done_p", "s_p")
    mark("scan_done_out", "s_out")

    @blk.vector
    def _(vector):
        for evn in ["scan_done_a", "scan_done_g", "scan_done_p", "scan_done_out"]:
            wt(vector, evn)
        inc(vector.tensor_copy(dtiny[:, 0:1], dtiny[:, 1:2]), "s_v")
        mark("barrier_v", "s_v")

    @blk.scalar
    def _(scalar):
        for evn in ["barrier_v"]:
            wt(scalar, evn)
        inc(scalar.copy(dtiny[:, 0:1], dtiny[:, 1:2]), "s_a")
        mark("barrier_a", "s_a")

    @blk.gpsimd
    def _(gpsimd):
        wt(gpsimd, "barrier_v")
        inc(gpsimd.tensor_copy(dtiny[:, 0:1], dtiny[:, 1:2]), "s_g")
        mark("barrier_g", "s_g")

    @blk.tensor
    def _(tensor):
        wt(tensor, "barrier_v")
        inc(tensor.matmul(ps_surf[0:16, 0:1], cs["repmask16"][:, 0:16], dtiny[0:16, 0:1],
                          start=True, stop=True), "s_p")
        mark("barrier_p", "s_p")

    @blk.sync
    def _(sync):
        wt(sync, "barrier_v")
        sync.wait_ge(sems["s_ld"], cnt["s_ld"])
        inc(sync.dma_start(out=dtiny[:, 0:2], in_=cdr["d0"]), "s_ld", 16)
        mark("barrier_s", "s_ld")

    pb_scope.close()

    pc = ExitStack()

    def csb(name, shape, dt=F32):
        return pc.enter_context(nc.sbuf_tensor(name, shape, dt))

    def cps(name, shape, dt=F32):
        return pc.enter_context(nc.psum_tensor(name, shape, dt))

    ytile = csb("ytile", [128, 8 * TILE])
    zstile = csb("zstile", [128, 2 * TILE])
    yz = csb("yz", [128, 8 * TILE], BF16)
    xres = csb("xres", [128, 2 * D])
    x1sb = csb("x1sb", [128, TG * D])
    cstat = csb("cstat", [128, 3 * TG])
    csqs = csb("csqs", [128, D])
    x2n_tm = csb("x2n_tm", [128, TG * D], BF16)
    ps_c1 = cps("ps_c1", [128, 512])
    ps_c2 = cps("ps_c2", [128, 512])
    ps_tm = cps("ps_tm", [128, 1024])
    ps_g16 = cps("ps_g16", [128, 16])
    ps_str = cps("ps_str", [16, 128])

    def phase_c(tile):
        t0 = tile * TILE
        tg = "_c%d" % tile

        CYZV0 = cnt["s_v"]

        @blk.sync
        def _(sync):
            if tile == 0:
                wt(sync, "barrier_s")
            else:
                wt(sync, "c1done_%d" % (tile - 1))
            inc(sync.dma_start(out=ytile[:], in_=y_dram[:, t0 * 8:(t0 + TILE) * 8]),
                "s_ld", 16)
            for c8 in range(8):
                if c8 >= 2:
                    sync.wait_ge(sems["s_v"], CYZV0 + c8 - 1)
                inc(sync.dma_start(out=zstile[:, (c8 % 2) * TILE:(c8 % 2 + 1) * TILE],
                                   in_=zsil_dram[c8 * 128:(c8 + 1) * 128, t0:t0 + TILE]),
                    "s_ld", 16)
                mark("czs%d" % c8 + tg, "s_ld")
            mark("cyld" + tg, "s_ld")

        @blk.vector
        def _(vector):
            if tile == 0:
                wt(vector, "barrier_a")
                wt(vector, "barrier_g")
                wt(vector, "barrier_p")
            for c8 in range(8):
                wt(vector, "czs%d" % c8 + tg)
                yv = ytile[:].rearrange("p (t c) -> p t c", c=8)[:, :, c8:c8 + 1]
                inc(vector.tensor_tensor(
                    yz[:, c8 * TILE:(c8 + 1) * TILE].unsqueeze(2),
                    yv, zstile[:, (c8 % 2) * TILE:(c8 % 2 + 1) * TILE].unsqueeze(2),
                    ALU.mult), "s_v")
            mark("cyz" + tg, "s_v")

        CX1V0 = cnt["s_v"]

        @blk.tensor
        def _(tensor):
            wt(tensor, "cyz" + tg)
            if tile == 0:
                wt(tensor, "barrier_a")
                wt(tensor, "barrier_g")
            for g in range(TG):
                if g >= 1:
                    tensor.wait_ge(sems["s_v"], CX1V0 + g)
                for h2 in range(2):
                    for c8 in range(8):
                        inc(tensor.matmul(ps_tm[:, h2 * 512:(h2 + 1) * 512],
                                          yz[:, c8 * TILE + g * 128:c8 * TILE + (g + 1) * 128],
                                          w_out_sb[:, c8 * 1024 + h2 * 512:c8 * 1024 + (h2 + 1) * 512],
                                          start=(c8 == 0), stop=False), "s_p")
                    inc(tensor.matmul(ps_tm[:, h2 * 512:(h2 + 1) * 512], ones_bf[0:1, :],
                                      b_out_sb[:, h2 * 512:(h2 + 1) * 512],
                                      start=False, stop=True), "s_p")
            mark("cwout" + tg, "s_p")

        pbase = ev["cwout" + tg][1] - TG * 18

        @blk.sync
        def _(sync):
            for g in range(TG):
                if g >= 2:
                    sync.wait_ge(sems["s_v"], CX1V0 + g - 1)
                xv = x_tm[t0 + g * 128:t0 + (g + 1) * 128, :]
                inc(sync.dma_start(out=xres[:, (g % 2) * D:(g % 2 + 1) * D], in_=xv),
                    "s_ld", 16)
                mark("cxr%d" % g + tg, "s_ld")

        @blk.vector
        def _(vector):
            for g in range(TG):
                vector.wait_ge(sems["s_p"], pbase + (g + 1) * 18)
                wt(vector, "cxr%d" % g + tg)
                inc(vector.tensor_tensor(x1sb[:, g * D:(g + 1) * D],
                                         xres[:, (g % 2) * D:(g % 2 + 1) * D],
                                         ps_tm[:], ALU.add), "s_v")
            mark("cx1" + tg, "s_v")

        @blk.sync
        def _(sync):
            wt(sync, "cx1" + tg)
            xo = x1_dram[t0:t0 + TILE, :].rearrange("(g p) d -> p g d", p=128)
            inc(sync.dma_start(out=xo, in_=x1sb[:].rearrange("p (g d) -> p g d", d=D)),
                "s_ld", 16)
            mark("cx1st" + tg, "s_ld")

        @blk.vector
        def _(vector):
            wt(vector, "cx1" + tg)
            for g in range(TG):
                vector.tensor_scalar(csqs[:, 0:D], x1sb[:, g * D:(g + 1) * D], 1.0,
                                     0.0, ALU.mult, ALU.add, accum_out=cstat[:, 3 * g:3 * g + 1])
            inc(vector.tensor_copy(csqs[:, 0:1], cstat[:, 0:1]), "s_v")
            mark("csx" + tg, "s_v")

        @blk.scalar
        def _(scalar):
            wt(scalar, "csx" + tg)
            if tile == 0:
                wt(scalar, "barrier_g")
                wt(scalar, "barrier_p")
            for g in range(TG):
                scalar.activation(csqs[:, 0:D], x1sb[:, g * D:(g + 1) * D], AF.Square,
                                  accum_out=cstat[:, 3 * g + 1:3 * g + 2])
            inc(scalar.copy(csqs[:, 0:1], cstat[:, 0:1]), "s_a")
            mark("csxx" + tg, "s_a")

        @blk.vector
        def _(vector):
            wt(vector, "csxx" + tg)
            for g in range(TG):
                mcol = cstat[:, 3 * g:3 * g + 1]
                vcol = cstat[:, 3 * g + 1:3 * g + 2]
                m2col = cstat[:, 3 * g + 2:3 * g + 3]
                inc(vector.tensor_scalar(mcol, mcol, 1.0 / D, None, ALU.mult), "s_v")
                vector.wait_ge(sems["s_v"], cnt["s_v"])
                inc(vector.tensor_tensor(m2col, mcol, mcol, ALU.mult), "s_v")
                vector.wait_ge(sems["s_v"], cnt["s_v"])
                vector.scalar_tensor_tensor(vcol, vcol, 1.0 / D, m2col, ALU.mult,
                                            ALU.subtract)
            inc(vector.tensor_copy(csqs[:, 0:1], cstat[:, 0:1]), "s_v")
            mark("cvar" + tg, "s_v")

        @blk.scalar
        def _(scalar):
            wt(scalar, "cvar" + tg)
            for g in range(TG):
                vcol = cstat[:, 3 * g + 1:3 * g + 2]
                scalar.activation(vcol, vcol, AF.Ln, bias=LNEPS)
                scalar.activation(vcol, vcol, AF.Exp, scale=-0.5)
            inc(scalar.copy(csqs[:, 0:1], cstat[:, 0:1]), "s_a")
            mark("crstd" + tg, "s_a")

        @blk.vector
        def _(vector):
            wt(vector, "crstd" + tg)
            for g in range(TG):
                vector.tensor_scalar(x2n_tm[:, g * D:(g + 1) * D],
                                     x1sb[:, g * D:(g + 1) * D],
                                     cstat[:, 3 * g:3 * g + 1],
                                     cstat[:, 3 * g + 1:3 * g + 2],
                                     ALU.subtract, ALU.mult)
            inc(vector.tensor_copy(csqs[:, 0:1], cstat[:, 0:1]), "s_v")
            mark("cx2n" + tg, "s_v")

        @blk.sync
        def _(sync):
            wt(sync, "cx2n" + tg)
            xo = x2n_dram[t0:t0 + TILE, :].rearrange("(g p) d -> p g d", p=128)
            inc(sync.dma_start(out=xo, in_=x2n_tm[:].rearrange("p (g d) -> p g d", d=D)),
                "s_ld", 16)
            mark("cx2st" + tg, "s_ld")

        mark("c1done_%d" % tile, "s_ld")

    for tile in range(NT):
        phase_c(tile)
    mark("c1out", "s_ld")
    pc.close()

    pc = ExitStack()
    w_sg_sb = csb("w_sg_sb2", [128, 8 * FSH], BF16)
    w_su_sb = csb("w_su_sb2", [128, 8 * FSH], BF16)
    w_sd_sb = csb("w_sd_sb2", [128, 16 * D], BF16)
    x2n_tm = csb("x2n_tm2", [128, TG * D], BF16)
    x2n_fm = csb("x2n_fm2", [128, 8 * TILE], BF16)
    hsh = csb("hsh", [128, 16 * TILE], BF16)
    x1sb = csb("x1sb2", [128, 2 * D])
    zstile = csb("zstile2", [128, TILE])
    csqs = csb("csqs2", [128, TILE])
    cstat = csb("cstat2", [128, 4])
    scem_sb = csb("scem_sb", [16, TILE])
    ps_c1 = cps("ps_c12", [128, 512])
    ps_c2 = cps("ps_c22", [128, 512])
    ps_tm = cps("ps_tm2", [128, 1024])
    ps_g16 = cps("ps_g162", [128, 16])
    ps_str = cps("ps_str2", [16, 128])

    @blk.gpsimd
    def _(gpsimd):
        wt(gpsimd, "c1out")
        for k in range(8):
            kk = slice(k * 128, (k + 1) * 128)
            inc(gpsimd.dma_start(out=w_sg_sb[:, k * FSH:(k + 1) * FSH],
                                 in_=wdr["w_sg_t"][kk, :]), "s_out", 16)
            inc(gpsimd.dma_start(out=w_su_sb[:, k * FSH:(k + 1) * FSH],
                                 in_=wdr["w_su_t"][kk, :]), "s_out", 16)
        for k in range(16):
            inc(gpsimd.dma_start(out=w_sd_sb[:, k * D:(k + 1) * D],
                                 in_=wdr["w_sd_t"][k * 128:(k + 1) * 128, :]), "s_out", 16)
        mark("cshw", "s_out")

    def phase_c2(tile):
        t0 = tile * TILE
        tg = "_c%d" % tile

        @blk.sync
        def _(sync):
            if tile == 0:
                wt(sync, "c1out")
            else:
                wt(sync, "cdone_%d" % (tile - 1))
            xo = x2n_dram[t0:t0 + TILE, :].rearrange("(g p) d -> p g d", p=128)
            inc(sync.dma_start(out=x2n_tm[:].rearrange("p (g d) -> p g d", d=D),
                               in_=xo), "s_ld", 16)
            mark("cx2ld" + tg, "s_ld")
        mark("cx2n" + tg, ev["cx2ld" + tg][0]) if False else None
        ev["cx2n" + tg] = ev["cx2ld" + tg]

        vbase = cnt["s_v"]

        @blk.tensor
        def _(tensor):
            wt(tensor, "cx2n" + tg)
            for i in range(32):
                pst = ps_c1 if i % 2 == 0 else ps_c2
                if i >= 2:
                    tensor.wait_ge(sems["s_v"], vbase + i - 1)
                g, k = divmod(i, 8)
                inc(tensor.transpose(pst[:].bitcast(BF16)[:, 0:128],
                                     x2n_tm[:, g * D + k * 128:g * D + (k + 1) * 128],
                                     ident_bf[:]), "s_p")
            mark("ctr" + tg, "s_p")

        pbase2 = ev["ctr" + tg][1] - 32

        @blk.vector
        def _(vector):
            for i in range(32):
                pst = ps_c1 if i % 2 == 0 else ps_c2
                vector.wait_ge(sems["s_p"], pbase2 + i + 1)
                g, k = divmod(i, 8)
                inc(vector.tensor_copy(
                    x2n_fm[:, k * TILE + g * 128:k * TILE + (g + 1) * 128],
                    pst[:].bitcast(BF16)[:, 0:128]), "s_v")
            mark("cx2fm" + tg, "s_v")

        CGV0 = cnt["s_v"]

        @blk.tensor
        def _(tensor):
            wt(tensor, "cx2fm" + tg)
            if tile == 0:
                tensor.wait_ge(sems["s_out"], ev["cshw"][1])
            for g in range(TG):
                if g >= 1:
                    tensor.wait_ge(sems["s_v"], CGV0 + 5 * (g - 1) + 3)
                for k in range(8):
                    inc(tensor.matmul(ps_g16[:],
                                      x2n_fm[:, k * TILE + g * 128:k * TILE + (g + 1) * 128],
                                      gate_sb[:, k * E:(k + 1) * E],
                                      start=(k == 0), stop=(k == 7)), "s_p")
            mark("cgmm" + tg, "s_p")

        pbase3 = ev["cgmm" + tg][1] - TG * 8
        abase = cnt["s_a"]

        @blk.vector
        def _(vector):
            for g in range(TG):
                gcol = (tile * TG + g) * E
                vector.wait_ge(sems["s_p"], pbase3 + (g + 1) * 8)
                inc(vector.tensor_reduce(cstat[:, 0:1], ps_g16[:], AX.X, ALU.max),
                    "s_v")
                vector.wait_ge(sems["s_v"], cnt["s_v"])
                inc(vector.tensor_scalar(cstat[:, 1:2], cstat[:, 0:1], -1.0, None,
                                         ALU.mult), "s_v")
                vector.wait_ge(sems["s_v"], cnt["s_v"])
                inc(vector.tensor_scalar(sco_tm[:, gcol:gcol + E], ps_g16[:],
                                         cstat[:, 1:2], None, ALU.add), "s_v")
                vector.wait_ge(sems["s_a"], abase + g + 1)
                inc(vector.reciprocal(cstat[:, 2:3], cstat[:, 1:2]), "s_v")
                vector.wait_ge(sems["s_v"], cnt["s_v"])
                inc(vector.tensor_scalar(sco_tm[:, gcol:gcol + E],
                                         sco_tm[:, gcol:gcol + E],
                                         cstat[:, 2:3], None, ALU.mult), "s_v")
            mark("csm" + tg, "s_v")

        @blk.scalar
        def _(scalar):
            for g in range(TG):
                gcol = (tile * TG + g) * E
                scalar.wait_ge(sems["s_v"], CGV0 + 5 * g + 3)
                inc(scalar.activation(sco_tm[:, gcol:gcol + E], sco_tm[:, gcol:gcol + E],
                                      AF.Exp, accum_out=cstat[:, 1:2]), "s_a")
            mark("cexp" + tg, "s_a")

        CEMV0 = cnt["s_v"]

        @blk.tensor
        def _(tensor):
            wt(tensor, "csm" + tg)
            for g in range(TG):
                if g >= 1:
                    tensor.wait_ge(sems["s_v"], CEMV0 + g)
                gcol = (tile * TG + g) * E
                inc(tensor.transpose(ps_str[:], sco_tm[:, gcol:gcol + E], cs["ident"][:]),
                    "s_p")
            mark("cstr" + tg, "s_p")

        pbase4 = ev["cstr" + tg][1] - TG

        @blk.vector
        def _(vector):
            for g in range(TG):
                vector.wait_ge(sems["s_p"], pbase4 + g + 1)
                inc(vector.tensor_copy(scem_sb[:, g * 128:(g + 1) * 128], ps_str[:]),
                    "s_v")
            mark("cem" + tg, "s_v")

        @blk.sync
        def _(sync):
            wt(sync, "cem" + tg)
            inc(sync.dma_start(out=scem_dram[:, t0:t0 + TILE], in_=scem_sb[:]),
                "s_ld", 16)
            mark("cemst" + tg, "s_ld")

        CHSH0 = cnt["s_v"]

        @blk.tensor
        def _(tensor):
            wt(tensor, "cx2fm" + tg)
            wt(tensor, "cstr" + tg)
            for fc in range(16):
                if fc >= 1:
                    tensor.wait_ge(sems["s_v"], CHSH0 + fc)
                for k in range(8):
                    inc(tensor.matmul(ps_c1[:],
                                      w_sg_sb[:, k * FSH + fc * 128:k * FSH + (fc + 1) * 128],
                                      x2n_fm[:, k * TILE:(k + 1) * TILE],
                                      start=(k == 0), stop=(k == 7)), "s_p")
                for k in range(8):
                    inc(tensor.matmul(ps_c2[:],
                                      w_su_sb[:, k * FSH + fc * 128:k * FSH + (fc + 1) * 128],
                                      x2n_fm[:, k * TILE:(k + 1) * TILE],
                                      start=(k == 0), stop=(k == 7)), "s_p")
            mark("cgu" + tg, "s_p")

        pbase5 = ev["cgu" + tg][1] - 16 * 16
        abase2 = cnt["s_a"]

        @blk.scalar
        def _(scalar):
            for fc in range(16):
                scalar.wait_ge(sems["s_p"], pbase5 + fc * 16 + 8)
                if fc >= 1:
                    scalar.wait_ge(sems["s_v"], CHSH0 + fc)
                inc(scalar.activation(csqs[:, 0:TILE], ps_c1[:], AF.Gelu), "s_a")
            mark("cerf" + tg, "s_a")

        @blk.vector
        def _(vector):
            for fc in range(16):
                vector.wait_ge(sems["s_a"], abase2 + fc + 1)
                vector.wait_ge(sems["s_p"], pbase5 + (fc + 1) * 16)
                inc(vector.tensor_tensor(hsh[:, fc * TILE:(fc + 1) * TILE],
                                         csqs[:, 0:TILE], ps_c2[:], ALU.mult), "s_v")
            mark("chsh" + tg, "s_v")

        CBASE0 = cnt["s_v"]

        @blk.tensor
        def _(tensor):
            wt(tensor, "chsh" + tg)
            for g in range(TG):
                if g >= 1:
                    tensor.wait_ge(sems["s_v"], CBASE0 + g)
                for h2 in range(2):
                    for k in range(16):
                        inc(tensor.matmul(ps_tm[:, h2 * 512:(h2 + 1) * 512],
                                          hsh[:, k * TILE + g * 128:k * TILE + (g + 1) * 128],
                                          w_sd_sb[:, k * D + h2 * 512:k * D + (h2 + 1) * 512],
                                          start=(k == 0), stop=(k == 15)), "s_p")
            mark("csd" + tg, "s_p")

        pbase6 = ev["csd" + tg][1] - TG * 32

        st_cnt = {}

        @blk.sync
        def _(sync):
            for i in range(TG + 1):
                if i < TG:
                    g = i
                    if g >= 2:
                        sync.wait_ge(sems["s_v"], CBASE0 + g - 1)
                        sync.wait_ge(sems["s_ld"], st_cnt[g - 2])
                    inc(sync.dma_start(out=x1sb[:, (g % 2) * D:(g % 2 + 1) * D],
                                       in_=x1_dram[t0 + g * 128:t0 + (g + 1) * 128, :]),
                        "s_ld", 16)
                    mark("cx1r%d" % g + tg, "s_ld")
                if i >= 1:
                    g = i - 1
                    sync.wait_ge(sems["s_v"], CBASE0 + g + 1)
                    inc(sync.dma_start(out=out[t0 + g * 128:t0 + (g + 1) * 128, :],
                                       in_=x1sb[:, (g % 2) * D:(g % 2 + 1) * D]),
                        "s_ld", 16)
                    st_cnt[g] = cnt["s_ld"]
            mark("cdone_%d" % tile, "s_ld")

        @blk.vector
        def _(vector):
            for g in range(TG):
                vector.wait_ge(sems["s_p"], pbase6 + (g + 1) * 32)
                wt(vector, "cx1r%d" % g + tg)
                inc(vector.tensor_tensor(x1sb[:, (g % 2) * D:(g % 2 + 1) * D],
                                         x1sb[:, (g % 2) * D:(g % 2 + 1) * D],
                                         ps_tm[:], ALU.add), "s_v")
            mark("cbase" + tg, "s_v")

    for tile in range(NT):
        phase_c2(tile)
    mark("baseout", "s_ld")
    pc.close()

    pc = ExitStack()
    TE = T // 8
    assert TE % 128 == 0, "routing phase needs T to be a multiple of 1024"
    RBLK = TE // 128

    seb = csb("seb", [128, TE])
    mwork = csb("mwork", [128, TE])
    msku = csb("msku", [128, TE], mybir.dt.uint32)
    prefx = csb("prefx", [128, TE])
    zrow = csb("zrow", [128, TE])
    theta = csb("theta", [16, 4])
    cnt128 = csb("cnt128", [128, 1])
    rkm = csb("rkm", [128, TE])
    rkt = csb("rkt", [128, RBLK * 128])
    idc = csb("idc", [128, 2])
    rhs2 = csb("rhs2", [128, 2])
    ps_th = cps("ps_th", [128, 1])
    ps_ct = cps("ps_ct", [16, 1])
    ps_slb = [cps("ps_sl%d" % _b, [128, 2 * E]) for _b in range(NBLK)]
    ps_tr2 = cps("ps_tr2", [128, 128])

    @blk.sync
    def _(sync):
        wt(sync, "cemst_c%d" % (NT - 1))
        sv = scem_dram[:, :].rearrange("e (c t) -> (e c) t", c=8)
        inc(sync.dma_start(out=seb[:], in_=sv), "s_ld", 16)
        mark("sebld", "s_ld")

    @blk.vector
    def _(vector):
        wt(vector, "sebld")
        vector.memset(theta[:, 0:1], 0.5)
        inc(vector.memset(zrow[:], 0.0), "s_v")
        mark("bis0", "s_v")

    NITER = 45
    BISV0 = cnt["s_v"]

    @blk.tensor
    def _(tensor):
        wt(tensor, "bis0")
        for it in range(NITER):
            if it > 0:
                tensor.wait_ge(sems["s_v"], BISV0 + 4 * it)
            inc(tensor.matmul(ps_th[:], cs["repmask16"][:], theta[:, 0:1],
                              start=True, stop=True), "s_p")
            tensor.wait_ge(sems["s_v"], BISV0 + 4 * it + 1)
            inc(tensor.matmul(ps_ct[:], cs["smask16"][:], cnt128[:],
                              start=True, stop=True), "s_p")
        mark("bismm", "s_p")

    pbb = ev["bismm"][1] - 2 * NITER

    @blk.vector
    def _(vector):
        delta = 0.25
        for it in range(NITER):
            vector.wait_ge(sems["s_p"], pbb + 2 * it + 1)
            inc(vector.tensor_scalar(mwork[:], seb[:], ps_th[:, 0:1], 0.0,
                                     ALU.is_gt, ALU.add, accum_out=cnt128[:]), "s_v")
            vector.wait_ge(sems["s_p"], pbb + 2 * it + 2)
            inc(vector.tensor_scalar(theta[:, 3:4], ps_ct[:, 0:1], float(T / 8.0),
                                     None, ALU.is_gt), "s_v")
            vector.wait_ge(sems["s_v"], cnt["s_v"])
            inc(vector.scalar_tensor_tensor(theta[:, 1:2], theta[:, 3:4], 2.0 * delta,
                                            theta[:, 0:1], ALU.mult, ALU.add), "s_v")
            vector.wait_ge(sems["s_v"], cnt["s_v"])
            inc(vector.tensor_scalar(theta[:, 0:1], theta[:, 1:2], -delta, None,
                                     ALU.add), "s_v")
            delta *= 0.5
        vector.wait_ge(sems["s_v"], cnt["s_v"])
        inc(vector.tensor_scalar(theta[:, 0:1], theta[:, 0:1], 8.0 * delta, None,
                                 ALU.add), "s_v")
        mark("bisend", "s_v")

    @blk.tensor
    def _(tensor):
        wt(tensor, "bisend")
        inc(tensor.matmul(ps_th[:], cs["repmask16"][:], theta[:, 0:1],
                          start=True, stop=True), "s_p")
        mark("thbc", "s_p")

    @blk.vector
    def _(vector):
        wt(vector, "thbc")
        inc(vector.tensor_scalar(mwork[:], seb[:], ps_th[:, 0:1], None, ALU.is_gt),
            "s_v")
        vector.tensor_scalar(msku[:], seb[:], ps_th[:, 0:1], None, ALU.is_gt)
        vector.wait_ge(sems["s_v"], cnt["s_v"])
        inc(vector.tensor_tensor_scan(prefx[:], mwork[:], zrow[:], 0.0,
                                      ALU.add, ALU.add), "s_v")
        vector.wait_ge(sems["s_v"], cnt["s_v"])
        inc(vector.tensor_copy(cnt128[:], prefx[:, TE - 1:TE]), "s_v")
        mark("pfx", "s_v")

    @blk.tensor
    def _(tensor):
        wt(tensor, "pfx")
        inc(tensor.matmul(ps_th[:], cs["trig8"][:], cnt128[:], start=True, stop=True),
            "s_p")
        mark("offs", "s_p")

    @blk.vector
    def _(vector):
        wt(vector, "offs")
        vector.memset(zrow[:], DUMP)
        inc(vector.tensor_scalar(prefx[:], prefx[:], ps_th[:, 0:1], -1.0,
                                 ALU.add, ALU.add), "s_v")
        vector.wait_ge(sems["s_v"], cnt["s_v"])
        inc(vector.select(rkm[:], msku[:], prefx[:], zrow[:]), "s_v")
        mark("rkm", "s_v")

    RKTV0 = cnt["s_v"]

    @blk.tensor
    def _(tensor):
        wt(tensor, "rkm")
        for b2 in range(RBLK):
            w2 = min(128, TE - b2 * 128)
            if b2 >= 1:
                tensor.wait_ge(sems["s_v"], RKTV0 + b2)
            inc(tensor.transpose(ps_tr2[0:w2, :], rkm[:, b2 * 128:b2 * 128 + w2],
                                 cs["ident"][:]), "s_p")
        mark("rktr", "s_p")

    pbt = ev["rktr"][1] - RBLK

    @blk.vector
    def _(vector):
        for b2 in range(RBLK):
            w2 = min(128, TE - b2 * 128)
            vector.wait_ge(sems["s_p"], pbt + b2 + 1)
            inc(vector.tensor_copy(rkt[:, b2 * 128:(b2 + 1) * 128],
                                   ps_tr2[0:w2, :].to_broadcast([128, 128])
                                   if False else ps_tr2[:, 0:128]), "s_v")
        mark("rkt_done", "s_v")

    NG = T // 128
    oh2 = csb("oh2", [128, 2 * 512])
    rhs4 = csb("rhs4", [128, 4])

    VOH0 = cnt["s_v"]
    POH0 = cnt["s_p"]

    @blk.vector
    def _(vector):
        wt(vector, "rkt_done")
        for e in range(E):
            for g in range(NG):
                tc8, b2 = divmod(g, RBLK)
                it = e * NG + g
                par = it % 2
                if it >= 2:
                    vector.wait_ge(sems["s_p"], POH0 + (it - 1) * NBLK)
                rcol = rkt[:, b2 * 128 + e * 8 + tc8:b2 * 128 + e * 8 + tc8 + 1]
                vector.tensor_scalar(rhs4[:, 2 * par:2 * par + 1], cs["iotacol"][:],
                                     float(tc8 * TE + b2 * 128), None, ALU.add)
                vector.wait_ge(sems["s_v"], cnt["s_v"])
                inc(vector.tensor_copy(rhs4[:, 2 * par + 1:2 * par + 2],
                                       sco_tm[:, g * E + e:g * E + e + 1]), "s_v")
                inc(vector.tensor_scalar(oh2[:, par * 512:par * 512 + 512],
                                         cs["iota512"][:], rcol, None,
                                         ALU.is_equal), "s_v")
        mark("ohv", "s_v")

    @blk.tensor
    def _(tensor):
        for e in range(E):
            for g in range(NG):
                it = e * NG + g
                par = it % 2
                tensor.wait_ge(sems["s_v"], VOH0 + 2 * (it + 1))
                for b3 in range(NBLK):
                    inc(tensor.matmul(ps_slb[b3][:, 2 * e:2 * e + 2],
                                      oh2[:, par * 512 + b3 * 128:par * 512 + (b3 + 1) * 128],
                                      rhs4[:, 2 * par:2 * par + 2],
                                      start=(g == 0), stop=(g == NG - 1)), "s_p")
        mark("slotmm", "s_p")

    @blk.scalar
    def _(scalar):
        wt(scalar, "slotmm")
        for b3 in range(NBLK):
            inc(scalar.activation(slots[:, b3 * 2 * E:(b3 + 1) * 2 * E],
                                  ps_slb[b3][:], AF.Identity), "s_a")
        mark("slots_sb", "s_a")

    @blk.vector
    def _(vector):
        wt(vector, "slots_sb")
        iv = slots[:, 0:2 * E * NBLK].rearrange("p (q two) -> p q two", two=2)[:, :, 0:1]
        inc(vector.tensor_copy(idint[:, 0:E * NBLK].unsqueeze(2), iv), "s_v")
        mark("idint", "s_v")

    mark("routing_done", "s_v")
    pc.close()

    pc = ExitStack()
    CAPP = NBLK * 128
    xtok_tm = csb("xtok_tm", [128, NBLK * D], BF16)
    xtok_fm = csb("xtok_fm", [128, 8 * CAPP], BF16)
    wgu = csb("wgu", [128, 2 * 2 * 8 * 128], BF16)
    wdd = csb("wdd", [128, 16 * D], BF16)
    hex_ = csb("hex", [128, 16 * CAPP], BF16)
    erfsc = csb("erfsc", [128, CAPP])
    eo = csb("eo", [128, D])
    ps_e1 = cps("ps_e1", [128, CAPP])
    ps_e2 = cps("ps_e2", [128, CAPP])
    ps_eo = cps("ps_eo", [128, D])
    ps_tr3 = cps("ps_tr3", [128, 128])

    def expert(e):
        tg = "_e%d" % e

        @blk.gpsimd
        def _(gpsimd):
            if e == 0:
                wt(gpsimd, "idint")
                wt(gpsimd, "baseout")
            else:
                wt(gpsimd, "escat_%d" % (e - 1))
            for b3 in range(NBLK):
                inc(gpsimd.indirect_dma_start(
                    xtok_tm[:, b3 * D:(b3 + 1) * D], None, x2n_dram,
                    IndirectOffsetOnAxis(idint[:, b3 * E + e:b3 * E + e + 1], 0)),
                    "s_out", 16)
            mark("egath" + tg, "s_out")

        vb = cnt["s_v"]

        @blk.tensor
        def _(tensor):
            wt(tensor, "egath" + tg)
            if e > 0:
                tensor.wait_ge(sems["s_v"], ev["exv_%d" % (e - 1)][1])
            for i in range(NBLK * 8):
                b3, k = divmod(i, 8)
                if i >= 1:
                    tensor.wait_ge(sems["s_v"], vb + i)
                inc(tensor.transpose(ps_tr3[:].bitcast(BF16)[:, 0:128],
                                     xtok_tm[:, b3 * D + k * 128:b3 * D + (k + 1) * 128],
                                     ident_bf[:]), "s_p")
            mark("etr" + tg, "s_p")

        pbe = ev["etr" + tg][1] - NBLK * 8

        @blk.vector
        def _(vector):
            for i in range(NBLK * 8):
                b3, k = divmod(i, 8)
                vector.wait_ge(sems["s_p"], pbe + i + 1)
                inc(vector.tensor_copy(
                    xtok_fm[:, k * CAPP + b3 * 128:k * CAPP + (b3 + 1) * 128],
                    ps_tr3[:].bitcast(BF16)[:, 0:128]), "s_v")
            mark("exv" + tg, "s_v")

        ev["exv_%d" % e] = ("s_v", cnt["s_v"])

        PGU0 = cnt["s_p"]

        @blk.gpsimd
        def _(gpsimd):
            wt(gpsimd, "egath" + tg)
            for fc in range(16):
                par = fc % 2
                if fc >= 2:
                    gpsimd.wait_ge(sems["s_p"], PGU0 + (fc - 1) * 16)
                gv = wdr["wg_t"][e].rearrange("(k p) f -> p k f", p=128)[
                    :, :, fc * 128:(fc + 1) * 128]
                ov = wgu[:, par * 2048:par * 2048 + 1024].rearrange(
                    "p (k f) -> p k f", f=128)
                inc(gpsimd.dma_start(out=ov, in_=gv), "s_out", 16)
                uv = wdr["wu_t"][e].rearrange("(k p) f -> p k f", p=128)[
                    :, :, fc * 128:(fc + 1) * 128]
                ov2 = wgu[:, par * 2048 + 1024:par * 2048 + 2048].rearrange(
                    "p (k f) -> p k f", f=128)
                inc(gpsimd.dma_start(out=ov2, in_=uv), "s_out", 16)
            mark("eguld" + tg, "s_out")

        gb = cnt["s_out"] - 16 * 2 * 16

        EHSH0 = cnt["s_v"]

        @blk.tensor
        def _(tensor):
            wt(tensor, "exv" + tg)
            for fc in range(16):
                par = fc % 2
                tensor.wait_ge(sems["s_out"], gb + (fc + 1) * 32)
                if fc >= 1:
                    tensor.wait_ge(sems["s_v"], EHSH0 + fc)
                for k in range(8):
                    inc(tensor.matmul(ps_e1[:],
                                      wgu[:, par * 2048 + k * 128:par * 2048 + (k + 1) * 128],
                                      xtok_fm[:, k * CAPP:(k + 1) * CAPP],
                                      start=(k == 0), stop=(k == 7)), "s_p")
                for k in range(8):
                    inc(tensor.matmul(ps_e2[:],
                                      wgu[:, par * 2048 + 1024 + k * 128:par * 2048 + 1024 + (k + 1) * 128],
                                      xtok_fm[:, k * CAPP:(k + 1) * CAPP],
                                      start=(k == 0), stop=(k == 7)), "s_p")
            mark("egu" + tg, "s_p")

        pbe2 = ev["egu" + tg][1] - 16 * 16
        abe = cnt["s_a"]

        @blk.scalar
        def _(scalar):
            for fc in range(16):
                scalar.wait_ge(sems["s_p"], pbe2 + fc * 16 + 8)
                if fc >= 1:
                    scalar.wait_ge(sems["s_v"], EHSH0 + fc)
                inc(scalar.activation(erfsc[:], ps_e1[:], AF.Gelu), "s_a")
            mark("eerf" + tg, "s_a")

        @blk.vector
        def _(vector):
            for fc in range(16):
                vector.wait_ge(sems["s_a"], abe + fc + 1)
                vector.wait_ge(sems["s_p"], pbe2 + (fc + 1) * 16)
                inc(vector.tensor_tensor(hex_[:, fc * CAPP:(fc + 1) * CAPP],
                                         erfsc[:], ps_e2[:], ALU.mult), "s_v")
            mark("ehsh" + tg, "s_v")

        @blk.gpsimd
        def _(gpsimd):
            wt(gpsimd, "eguld" + tg)
            if e > 0:
                gpsimd.wait_ge(sems["s_p"], ev["ewd_%d" % (e - 1)][1])
            dv = wdr["wd_t"][e].rearrange("(kk p) d -> p kk d", p=128)
            ov = wdd[:].rearrange("p (kk d) -> p kk d", d=D)
            inc(gpsimd.dma_start(out=ov, in_=dv), "s_out", 16)
            mark("ewdld" + tg, "s_out")

        EOV0 = cnt["s_v"]

        @blk.tensor
        def _(tensor):
            wt(tensor, "ehsh" + tg)
            wt(tensor, "ewdld" + tg)
            for b3 in range(NBLK):
                if b3 >= 1:
                    tensor.wait_ge(sems["s_v"], EOV0 + b3)
                for h2 in range(2):
                    for fidx in range(16):
                        inc(tensor.matmul(ps_eo[:, h2 * 512:(h2 + 1) * 512],
                                          hex_[:, fidx * CAPP + b3 * 128:fidx * CAPP + (b3 + 1) * 128],
                                          wdd[:, fidx * D + h2 * 512:fidx * D + (h2 + 1) * 512],
                                          start=(fidx == 0), stop=(fidx == 15)), "s_p")
            mark("ewd" + tg, "s_p")
        ev["ewd_%d" % e] = ev["ewd" + tg]

        pbe3 = ev["ewd" + tg][1] - NBLK * 32
        ESC0 = cnt["s_out"]

        @blk.vector
        def _(vector):
            for b3 in range(NBLK):
                vector.wait_ge(sems["s_p"], pbe3 + (b3 + 1) * 32)
                if b3 >= 1:
                    vector.wait_ge(sems["s_out"], ESC0 + b3 * 16)
                wcol = slots[:, 2 * (b3 * E + e) + 1:2 * (b3 * E + e) + 2]
                inc(vector.tensor_scalar(eo[:, 0:D] if NBLK == 1 else
                                         eo[:, 0:D], ps_eo[:], wcol, None, ALU.mult),
                    "s_v")
            mark("eos" + tg, "s_v")

        vbe = ev["eos" + tg][1] - NBLK

        @blk.gpsimd
        def _(gpsimd):
            wt(gpsimd, "ewdld" + tg)
            for b3 in range(NBLK):
                gpsimd.wait_ge(sems["s_v"], vbe + b3 + 1)
                inc(gpsimd.indirect_dma_start(
                    out, IndirectOffsetOnAxis(idint[:, b3 * E + e:b3 * E + e + 1], 0),
                    eo[:, 0:D], None, compute_op=ALU.add), "s_out", 16)
                gpsimd.wait_ge(sems["s_out"], cnt["s_out"])
            mark("escat_%d" % e, "s_out")


    for e in range(E):
        expert(e)

    @blk.sync
    def _(sync):
        wt(sync, "escat_%d" % (E - 1))
        wt(sync, "baseout")
        sync.wait_ge(sems["s_out"], cnt["s_out"])
        sync.wait_ge(sems["s_ld"], cnt["s_ld"])
        if dbg:
            for nm, (src, dst) in dbg_mirror.items():
                inc(sync.dma_start(out=dst, in_=src), "s_ld", 16)
            sync.wait_ge(sems["s_ld"], cnt["s_ld"])
    pc.close()

    return nc



def kernel(**inputs):
    T = int(np.asarray(inputs["L0_attn_x"]).shape[1])
    x = np.ascontiguousarray(np.asarray(inputs["L0_attn_x"], dtype=np.float32))
    w = stage_weights(inputs)
    c = host_consts()
    nc = build(T)
    in_map = {}
    for k, v in w.items():
        in_map[k] = np.ascontiguousarray(v.astype(np.float32))
    for k, v in c.items():
        in_map["c_" + k] = np.ascontiguousarray(v.astype(np.float32))
    in_maps = [dict(in_map, x_tm=np.ascontiguousarray(x[b])) for b in range(8)]
    res = run_bass_kernel_spmd(nc, in_maps, core_ids=list(range(8)))
    return np.stack([res.results[b]["out"] for b in range(8)], axis=0)
